# version 02
# baseline (speedup 1.0000x reference)
import math
import numpy as np
import ml_dtypes
from contextlib import ExitStack
import concourse.bass as bass
import concourse.mybir as mybir
from concourse.bass_utils import run_bass_kernel_spmd

F32 = mybir.dt.float32
BF16 = mybir.dt.bfloat16
I32 = mybir.dt.int32
AF = mybir.ActivationFunctionType
ALU = mybir.AluOpType
AX = mybir.AxisListType
BFNP = ml_dtypes.bfloat16

T = 2048
D = 2048
NH = 16
DK, DV = 64, 128
NCH = 16
NTT = 4
SCALE = 128.0 ** (-0.5)
NORMI = 1.0 / 16.0
EPS = 1e-5
LNB = math.log(SCALE)


def build_program(limit=99, debug_taps=False):
    nc = bass.Bass("TRN2", target_bir_lowering=False, detect_race_conditions=False)

    x_fm = nc.dram_tensor("x_fm", [D, T], F32, kind="ExternalInput")
    wqT = nc.dram_tensor("wqT", [D, 1024], F32, kind="ExternalInput")
    wkT = nc.dram_tensor("wkT", [D, 1024], F32, kind="ExternalInput")
    wkg1 = nc.dram_tensor("wkg1", [128, 256], F32, kind="ExternalInput")
    wkg2 = nc.dram_tensor("wkg2", [16, 1024], F32, kind="ExternalInput")
    bkg2_t = nc.dram_tensor("bkg2_t", [128, 8], F32, kind="ExternalInput")
    wvT = nc.dram_tensor("wvT", [D, D], F32, kind="ExternalInput")
    wgT = nc.dram_tensor("wgT", [D, D], F32, kind="ExternalInput")
    woT = nc.dram_tensor("woT", [D, D], F32, kind="ExternalInput")
    bg_row = nc.dram_tensor("bg_row", [1, D], F32, kind="ExternalInput")
    w1_n = nc.dram_tensor("w1_n", [8, D, 1024], F32, kind="ExternalInput")
    w2r = nc.dram_tensor("w2r", [8, 8192, 256], F32, kind="ExternalInput")
    sw1_n = nc.dram_tensor("sw1_n", [D, D], F32, kind="ExternalInput")
    sw2r = nc.dram_tensor("sw2r", [8, 2048, 256], F32, kind="ExternalInput")
    gate_t = nc.dram_tensor("gate_t", [128, 128], F32, kind="ExternalInput")
    b1_t = nc.dram_tensor("b1_t", [128, 64], F32, kind="ExternalInput")
    b2_n = nc.dram_tensor("b2_n", [8, D], F32, kind="ExternalInput")
    sb1_t = nc.dram_tensor("sb1_t", [128, 16], F32, kind="ExternalInput")
    sb2_t = nc.dram_tensor("sb2_t", [128, 16], F32, kind="ExternalInput")
    identb_d = nc.dram_tensor("identb_d", [128, 128], BF16, kind="ExternalInput")
    identf_d = nc.dram_tensor("identf_d", [128, 128], F32, kind="ExternalInput")
    maskut_d = nc.dram_tensor("maskut_d", [128, 128], F32, kind="ExternalInput")
    zeros_d = nc.dram_tensor("zeros_d", [128, 128], F32, kind="ExternalInput")
    ones_r = nc.dram_tensor("ones_r", [1, 128], F32, kind="ExternalInput")
    ones_c = nc.dram_tensor("ones_c", [128, 1], F32, kind="ExternalInput")
    oh_d = nc.dram_tensor("oh_d", [8, 1024], F32, kind="ExternalInput")
    emask_d = nc.dram_tensor("emask_d", [128, 64], F32, kind="ExternalInput")
    idx_d = nc.dram_tensor("idx_d", [128, 1], I32, kind="ExternalInput")
    fconst_d = nc.dram_tensor("fconst_d", [128, 4], F32, kind="ExternalInput")
    maskS_d = nc.dram_tensor("maskS_d", [128, 1], F32, kind="ExternalInput")

    out_fm = nc.dram_tensor("out_fm", [D, T], F32, kind="ExternalOutput")

    xbf = nc.dram_tensor("xbf", [D, T], BF16, kind="Internal")
    w1bf = nc.dram_tensor("w1bf", [8, D, 1024], BF16, kind="Internal")
    w2bf = nc.dram_tensor("w2bf", [8, 8192, 256], BF16, kind="Internal")
    sw1bf = nc.dram_tensor("sw1bf", [D, D], BF16, kind="Internal")
    sw2bf = nc.dram_tensor("sw2bf", [8, 2048, 256], BF16, kind="Internal")
    qt_s = nc.dram_tensor("qt_s", [1024, T], BF16, kind="Internal")
    kt_s = nc.dram_tensor("kt_s", [1024, T], BF16, kind="Internal")
    onorm_s = nc.dram_tensor("onorm_s", [T, D], BF16, kind="Internal")
    x1f_s = nc.dram_tensor("x1f_s", [D, T], F32, kind="Internal")
    st_contrib = nc.dram_tensor("st_contrib", [128, 1024], F32, kind="Internal")
    st_gather = nc.dram_tensor("st_gather", [1024, 1024], F32, kind="Internal")

    DMASEMS = ["c0", "c1", "gx", "gk", "gq", "gkk", "gv", "g1b", "gi", "g2b",
               "gg", "go", "gc8", "gx1", "x0", "x1", "q0", "q1", "o0", "o1",
               "s0", "s1", "so0", "so1", "sc0", "xr0", "xr1", "xs0", "xs1",
               "xf0", "xf1", "w0", "w1", "os0", "os1"]
    CNT = dict(pe=0, dv=0, ac=0, cc=0)
    for _k in DMASEMS:
        CNT[_k] = 0
    MARK = {}
    LASTW = {}
    SEM = {}

    def tk(key, step=1):
        CNT[key] += step
        return CNT[key]

    def wt(eng, ename, key, val):
        if val is None or val <= 0:
            return
        if LASTW.get((ename, key), 0) >= val:
            return
        LASTW[(ename, key)] = val
        eng.wait_ge(SEM[key], val)

    def dq(eng, name, dst, src):
        eng.dma_start(dst, src).then_inc(SEM[name], 16)
        CNT[name] += 16
        return CNT[name]

    ctx = ExitStack()
    with ctx:
        E = ctx.enter_context
        for k in CNT:
            SEM[k] = E(nc.semaphore("sem_" + k))

        identb = E(nc.sbuf_tensor("identb", [128, 128], BF16))
        identf = E(nc.sbuf_tensor("identf", [128, 128], F32))
        onesr = E(nc.sbuf_tensor("onesr", [1, 128], F32))
        onesc = E(nc.sbuf_tensor("onesc", [128, 1], F32))
        idx_sb = E(nc.sbuf_tensor("idx_sb", [128, 1], I32))
        fconst = E(nc.sbuf_tensor("fconst", [128, 4], F32))
        maskS_sb = E(nc.sbuf_tensor("maskS_sb", [128, 1], F32))

        with nc.Block() as blk:
            @blk.sync
            def _(sy):
                for dst, src in [(identb, identb_d), (identf, identf_d),
                                 (onesr, ones_r), (onesc, ones_c),
                                 (idx_sb, idx_d),
                                 (fconst, fconst_d), (maskS_sb, maskS_d)]:
                    dq(sy, "c0", dst[:, :], src[:, :])
                MARK["consts"] = CNT["c0"]

            @blk.gpsimd
            def _(g):
                MARK["xbf"] = dq(g, "gx", xbf[:, :], x_fm[:, :])

        sab = ExitStack()
        khT = sab.enter_context(nc.sbuf_tensor("khT", [128, NCH * 1024], BF16))
        maskut = sab.enter_context(nc.sbuf_tensor("maskut", [128, 128], F32))
        gam = sab.enter_context(nc.sbuf_tensor("gam", [128, 128], F32))
        S_buf = sab.enter_context(nc.sbuf_tensor("S_buf", [128, 1024], F32))
        S_bf = [sab.enter_context(nc.sbuf_tensor(f"S_bf{i}", [128, 1024], BF16)) for i in range(2)]
        S_init = sab.enter_context(nc.sbuf_tensor("S_init", [128, 1024], F32))
        sa2 = ExitStack()
        Gbuf = sa2.enter_context(nc.sbuf_tensor("Gbuf", [128, 8 * T], F32))
        zer = sa2.enter_context(nc.sbuf_tensor("zer", [128, 128], F32))
        wkg1_sb = sa2.enter_context(nc.sbuf_tensor("wkg1_sb", [128, 256], BF16))
        wkg2_sb = sa2.enter_context(nc.sbuf_tensor("wkg2_sb", [16, 1024], BF16))
        bkg2_sb = sa2.enter_context(nc.sbuf_tensor("bkg2_sb", [128, 8], F32))

        with nc.Block() as blk:
            @blk.sync
            def _(sy):
                for dst, src in [(maskut, maskut_d), (zer, zeros_d),
                                 (bkg2_sb, bkg2_t)]:
                    dq(sy, "c1", dst[:, :], src[:, :])
                MARK["consts2"] = CNT["c1"]

            @blk.gpsimd
            def _(g):
                dq(g, "gk", wkg1_sb[:, :], wkg1[:, :])
                MARK["wkg"] = dq(g, "gk", wkg2_sb[:, :], wkg2[:, :])

            @blk.vector
            def _(v):
                v.memset(S_buf[:, :], 0.0).then_inc(SEM["dv"], 1)
                tk("dv")

        p1 = ExitStack()
        xsl = [p1.enter_context(nc.sbuf_tensor(f"xsl{i}", [128, 16 * 512], BF16)) for i in range(2)]
        kg1b = p1.enter_context(nc.sbuf_tensor("kg1b", [16, 512], BF16))
        t_mn = p1.enter_context(nc.sbuf_tensor("t_mn", [128, 512], F32))
        t_ab = p1.enter_context(nc.sbuf_tensor("t_ab", [128, 512], F32))
        t_gk = p1.enter_context(nc.sbuf_tensor("t_gk", [128, 512], F32))
        ps1 = p1.enter_context(nc.psum_tensor("ps1", [128, 512], F32))
        ps2 = [p1.enter_context(nc.psum_tensor(f"ps2_{i}", [128, 512], F32)) for i in range(2)]
        xsl_tick = [0, 0]

        def load_xsl(sy, buf_i, tt):
            sem = "x%d" % buf_i
            for ic in range(16):
                dq(sy, sem, xsl[buf_i][:, ic * 512:(ic + 1) * 512],
                   xbf[ic * 128:(ic + 1) * 128, tt * 512:(tt + 1) * 512])
            xsl_tick[buf_i] = CNT[sem]

        for tt in range(NTT if limit >= 1 else 0):
            with nc.Block() as blk:
                bi = tt % 2
                pe0, dv0, ac0 = CNT["pe"], CNT["dv"], CNT["ac"]

                @blk.sync
                def _(sy, bi=bi, tt=tt):
                    if tt == 0:
                        wt(sy, "sy", "gx", MARK["xbf"])
                        load_xsl(sy, 0, 0)
                    if tt + 1 < NTT:
                        load_xsl(sy, (tt + 1) % 2, tt + 1)

                @blk.tensor
                def _(te, bi=bi, tt=tt, pe0=pe0, dv0=dv0, ac0=ac0):
                    wt(te, "te", "x%d" % bi, xsl_tick[bi])
                    wt(te, "te", "gk", MARK["wkg"])
                    ins = None
                    for ic in range(16):
                        ins = te.matmul(ps1[0:16, :], wkg1_sb[:, ic * 16:(ic + 1) * 16],
                                        xsl[bi][:, ic * 512:(ic + 1) * 512],
                                        start=(ic == 0), stop=(ic == 15))
                    ins.then_inc(SEM["pe"], 1)
                    tk("pe")
                    for ic2 in range(8):
                        wt(te, "te", "ac", ac0 + 1)
                        if ic2 >= 2:
                            wt(te, "te", "dv", dv0 + 6 * (ic2 - 2) + 1)
                            wt(te, "te", "ac", ac0 + 1 + 3 * (ic2 - 2) + 1)
                        te.matmul(ps2[ic2 % 2][:, :], wkg2_sb[0:16, ic2 * 128:(ic2 + 1) * 128],
                                  kg1b[:, :], start=True, stop=True).then_inc(SEM["pe"], 1)
                        tk("pe")

                @blk.scalar
                def _(ac, bi=bi, tt=tt, pe0=pe0, dv0=dv0, ac0=ac0):
                    wt(ac, "ac", "c1", MARK["consts2"])
                    wt(ac, "ac", "c0", MARK["consts"])
                    wt(ac, "ac", "pe", pe0 + 1)
                    ac.copy(kg1b[:, :], ps1[0:16, :]).then_inc(SEM["ac"], 1)
                    tk("ac")
                    for ic2 in range(8):
                        wt(ac, "ac", "pe", pe0 + 2 + ic2)
                        if ic2 >= 1:
                            wt(ac, "ac", "dv", dv0 + 6 * (ic2 - 1) + 2)
                        wt(ac, "ac", "dv", dv0 + 6 * ic2 + 1)
                        ac.activation(t_ab[:, :], ps2[ic2 % 2][:, :], AF.Abs,
                                      bias=bkg2_sb[:, ic2:ic2 + 1], scale=1.0).then_inc(SEM["ac"], 1)
                        tk("ac")
                        ac.activation(t_ab[:, :], t_ab[:, :], AF.Exp, scale=-1.0).then_inc(SEM["ac"], 1)
                        tk("ac")
                        ac.activation(t_ab[:, :], t_ab[:, :], AF.Ln, bias=fconst[:, 1:2]).then_inc(SEM["ac"], 1)
                        tk("ac")
                    for ic2 in range(8):
                        gcol = ic2 * T + tt * 512
                        for cs in range(4):
                            wt(ac, "ac", "dv", dv0 + 6 * ic2 + 2 + cs + 1)
                            gi = ic2 * 16 + tt * 4 + cs
                            ac.activation(gam[:, gi:gi + 1],
                                          Gbuf[:, gcol + cs * 128 + 127: gcol + cs * 128 + 128],
                                          AF.Exp, scale=NORMI).then_inc(SEM["ac"], 1)
                            tk("ac")

                @blk.vector
                def _(v, bi=bi, tt=tt, pe0=pe0, dv0=dv0, ac0=ac0):
                    wt(v, "dv", "c1", MARK["consts2"])
                    for ic2 in range(8):
                        wt(v, "dv", "pe", pe0 + 2 + ic2)
                        v.tensor_scalar(t_mn[:, :], ps2[ic2 % 2][:, :], bkg2_sb[:, ic2:ic2 + 1],
                                        0.0, ALU.add, ALU.min).then_inc(SEM["dv"], 1)
                        tk("dv")
                        wt(v, "dv", "ac", ac0 + 1 + 3 * ic2 + 3)
                        v.tensor_sub(t_gk[:, :], t_mn[:, :], t_ab[:, :]).then_inc(SEM["dv"], 1)
                        tk("dv")
                        gcol = ic2 * T + tt * 512
                        for cs in range(4):
                            v.tensor_tensor_scan(
                                Gbuf[:, gcol + cs * 128: gcol + (cs + 1) * 128],
                                t_gk[:, cs * 128:(cs + 1) * 128], zer[:, :], 0.0,
                                ALU.add, ALU.add).then_inc(SEM["dv"], 1)
                            tk("dv")
        p1.close()

        def proj_qk(which):
            px = ExitStack()
            wsb = px.enter_context(nc.sbuf_tensor(f"w{which}_sb", [128, 16 * 1024], BF16))
            xsl2 = [px.enter_context(nc.sbuf_tensor(f"x{which}{i}", [128, 16 * 512], BF16)) for i in range(2)]
            eg = px.enter_context(nc.sbuf_tensor(f"eg_{which}", [128, 512], F32))
            ot = [px.enter_context(nc.sbuf_tensor(f"ot_{which}{i}", [128, 512], BF16)) for i in range(2)]
            kh = px.enter_context(nc.sbuf_tensor(f"kh_{which}", [128, 512], BF16))
            psp = [px.enter_context(nc.psum_tensor(f"psp_{which}{i}", [128, 512], F32)) for i in range(2)]
            pst = px.enter_context(nc.psum_tensor(f"pst_{which}", [128, 1024], BF16))
            wd = wqT if which == "q" else wkT
            sdram = qt_s if which == "q" else kt_s
            is_k = (which == "k")
            NDV = 6 if is_k else 1
            NPE = 5 if is_k else 1
            x2_tick = [0, 0]
            w_tick = [0]
            wsem = "gq" if which == "q" else "gkk"
            st_base = [CNT["s0"], CNT["s1"]]

            def st_sem(fc):
                return "s%d" % (fc % 2)

            def st_tick(tt, fc):
                i = tt * 4 + fc // 2
                if tt < 0 or fc < 0:
                    return 0
                return st_base[fc % 2] + (i + 1) * 16

            with nc.Block() as blk:
                @blk.gpsimd
                def _(g):
                    for ic in range(16):
                        dq(g, wsem, wsb[:, ic * 1024:(ic + 1) * 1024],
                           wd[ic * 128:(ic + 1) * 128, :])
                    w_tick[0] = CNT[wsem]

                @blk.sync
                def _(sy):
                    for ic in range(16):
                        dq(sy, "x0", xsl2[0][:, ic * 512:(ic + 1) * 512],
                           xbf[ic * 128:(ic + 1) * 128, 0:512])
                    x2_tick[0] = CNT["x0"]

            for tt in range(NTT):
                with nc.Block() as blk:
                    bi = tt % 2
                    pe0, dv0, ac0 = CNT["pe"], CNT["dv"], CNT["ac"]

                    @blk.sync
                    def _(sy, bi=bi, tt=tt):
                        if tt + 1 < NTT:
                            nb = (tt + 1) % 2
                            for ic in range(16):
                                dq(sy, "x%d" % nb, xsl2[nb][:, ic * 512:(ic + 1) * 512],
                                   xbf[ic * 128:(ic + 1) * 128, (tt + 1) * 512:(tt + 2) * 512])
                            x2_tick[nb] = CNT["x%d" % nb]

                    @blk.tensor
                    def _(te, bi=bi, tt=tt, pe0=pe0, dv0=dv0, ac0=ac0):
                        wt(te, "te", "c0", MARK["consts"])
                        wt(te, "te", "x%d" % bi, x2_tick[bi])
                        wt(te, "te", wsem, w_tick[0])
                        for fc in range(8):
                            if fc >= 2:
                                wt(te, "te", "dv", dv0 + NDV * (fc - 2) + 1)
                            ins = None
                            for ic in range(16):
                                ins = te.matmul(psp[fc % 2][:, :],
                                                wsb[:, ic * 1024 + fc * 128: ic * 1024 + (fc + 1) * 128],
                                                xsl2[bi][:, ic * 512:(ic + 1) * 512],
                                                start=(ic == 0), stop=(ic == 15))
                            ins.then_inc(SEM["pe"], 1)
                            tk("pe")
                            if is_k:
                                if fc >= 1:
                                    wt(te, "te", "dv", dv0 + NDV * (fc - 1) + 6)
                                for cs in range(4):
                                    wt(te, "te", "dv", dv0 + NDV * fc + 2 + cs)
                                    te.transpose(pst[:, cs * 128:(cs + 1) * 128],
                                                 kh[:, cs * 128:(cs + 1) * 128],
                                                 identb[:, :]).then_inc(SEM["pe"], 1)
                                    tk("pe")

                    @blk.scalar
                    def _(ac, bi=bi, tt=tt, pe0=pe0, dv0=dv0, ac0=ac0):
                        for fc in range(8):
                            if fc >= 1:
                                wt(ac, "ac", "dv", dv0 + NDV * (fc - 1) + 1)
                            if which == "q":
                                ac.activation(eg[:, :], Gbuf[:, fc * T + tt * 512: fc * T + (tt + 1) * 512],
                                              AF.Exp, bias=fconst[:, 0:1], scale=NORMI).then_inc(SEM["ac"], 1)
                            else:
                                ac.activation(eg[:, :], Gbuf[:, fc * T + tt * 512: fc * T + (tt + 1) * 512],
                                              AF.Exp, scale=-NORMI).then_inc(SEM["ac"], 1)
                            tk("ac")
                            if fc >= 1:
                                dq(ac, st_sem(fc - 1),
                                   sdram[(fc - 1) * 128: fc * 128, tt * 512:(tt + 1) * 512],
                                   ot[(fc - 1) % 2][:, :])
                        wt(ac, "ac", "dv", dv0 + NDV * 7 + 1)
                        dq(ac, st_sem(7), sdram[7 * 128: 8 * 128, tt * 512:(tt + 1) * 512],
                           ot[1][:, :])

                    @blk.vector
                    def _(v, bi=bi, tt=tt, pe0=pe0, dv0=dv0, ac0=ac0):
                        for fc in range(8):
                            wt(v, "dv", "pe", pe0 + NPE * fc + 1)
                            wt(v, "dv", "ac", ac0 + fc + 1)
                            if tt * 8 + fc - 2 >= 0:
                                t2, f2 = divmod(tt * 8 + fc - 2, 8)
                                wt(v, "dv", st_sem(f2), st_tick(t2, f2))
                            ob = ot[fc % 2]
                            v.tensor_mul(ob[:, :], psp[fc % 2][:, :], eg[:, :]).then_inc(SEM["dv"], 1)
                            tk("dv")
                            if is_k:
                                if fc >= 1:
                                    wt(v, "dv", "pe", pe0 + NPE * (fc - 1) + 5)
                                for cs in range(4):
                                    v.tensor_scalar(kh[:, cs * 128:(cs + 1) * 128],
                                                    ob[:, cs * 128:(cs + 1) * 128],
                                                    gam[:, fc * 16 + tt * 4 + cs: fc * 16 + tt * 4 + cs + 1],
                                                    None, ALU.mult).then_inc(SEM["dv"], 1)
                                    tk("dv")
                                wt(v, "dv", "pe", pe0 + NPE * fc + 5)
                                ins = None
                                for cs in range(4):
                                    ch = tt * 4 + cs
                                    ins = v.tensor_copy(
                                        khT[:, ch * 1024 + fc * 128: ch * 1024 + (fc + 1) * 128],
                                        pst[:, cs * 128:(cs + 1) * 128])
                                ins.then_inc(SEM["dv"], 1)
                                tk("dv")
            px.close()

        if limit >= 2:
            proj_qk("q")
        if limit >= 3:
            proj_qk("k")
        sa2.close()

        sb2 = ExitStack()
        v_sb = sb2.enter_context(nc.sbuf_tensor("v_sb", [128, NCH * 2048], BF16))
        p4 = ExitStack()
        wv_sb = p4.enter_context(nc.sbuf_tensor("wv_sb", [128, 16 * 2048], BF16))
        xc4 = p4.enter_context(nc.sbuf_tensor("xc4", [128, 16 * 512], BF16))
        psv = [p4.enter_context(nc.psum_tensor(f"psv{i}", [128, 512], F32)) for i in range(2)]
        pss = [p4.enter_context(nc.psum_tensor(f"pss{i}", [128, 512], F32)) for i in range(2)]
        wv_tick = [0]
        xc4_tick = [0]

        if limit >= 4:
          with nc.Block() as blk:
            @blk.gpsimd
            def _(g):
                for ic in range(16):
                    dq(g, "gv", wv_sb[:, ic * 2048:(ic + 1) * 2048],
                       wvT[ic * 128:(ic + 1) * 128, :])
                wv_tick[0] = CNT["gv"]
                for e in range(8):
                    dq(g, "g1b", w1bf[e, :, :], w1_n[e, :, :])
                MARK["w1bf"] = CNT["g1b"]

        for ch in range(NCH if limit >= 4 else 0):
            with nc.Block() as blk:
                pe0, dv0, ac0 = CNT["pe"], CNT["dv"], CNT["ac"]

                @blk.sync
                def _(sy, ch=ch):
                    if ch % 4 == 0:
                        sem = "x%d" % ((ch // 4) % 2)
                        for ic in range(16):
                            dq(sy, sem, xc4[:, ic * 512:(ic + 1) * 512],
                               xbf[ic * 128:(ic + 1) * 128, ch * 128:(ch + 4) * 128])
                        xc4_tick[0] = CNT[sem]

                @blk.tensor
                def _(te, ch=ch, pe0=pe0, dv0=dv0, ac0=ac0):
                    wt(te, "te", "x%d" % ((ch // 4) % 2), xc4_tick[0])
                    wt(te, "te", "gv", wv_tick[0])
                    co = (ch % 4) * 128
                    for jt in range(4):
                        if jt >= 2:
                            wt(te, "te", "ac", ac0 + jt - 1)
                        ins = None
                        for ic in range(16):
                            ins = te.matmul(psv[jt % 2][:, :],
                                            xc4[:, ic * 512 + co: ic * 512 + co + 128],
                                            wv_sb[:, ic * 2048 + jt * 512: ic * 2048 + (jt + 1) * 512],
                                            start=(ic == 0), stop=(ic == 15))
                        ins.then_inc(SEM["pe"], 1)
                        tk("pe")
                    for h in range(NH):
                        wt(te, "te", "ac", ac0 + h // 4 + 1)
                        if h >= 2:
                            wt(te, "te", "dv", dv0 + 2 * (h - 2) + 2)
                        pb = (h % 2) * 64
                        kcol = ch * 1024 + (h // 2) * 128 + pb
                        te.matmul(pss[h % 2][pb:pb + 64, 0:128],
                                  khT[:, kcol: kcol + 64],
                                  v_sb[:, ch * 2048 + h * 128: ch * 2048 + (h + 1) * 128],
                                  start=True, stop=True).then_inc(SEM["pe"], 1)
                        tk("pe")

                @blk.scalar
                def _(ac, ch=ch, pe0=pe0, dv0=dv0, ac0=ac0):
                    for jt in range(4):
                        wt(ac, "ac", "pe", pe0 + jt + 1)
                        ac.copy(v_sb[:, ch * 2048 + jt * 512: ch * 2048 + (jt + 1) * 512],
                                psv[jt % 2][:, :]).then_inc(SEM["ac"], 1)
                        tk("ac")

                @blk.vector
                def _(v, ch=ch, pe0=pe0, dv0=dv0, ac0=ac0):
                    for h in range(NH):
                        wt(v, "dv", "pe", pe0 + 4 + h + 1)
                        pb = (h % 2) * 64
                        rs = slice(pb, pb + 64)
                        cs_ = slice((h // 2) * 128, (h // 2) * 128 + 128)
                        gcol = (h // 2) * 16 + ch
                        v.tensor_scalar(S_buf[rs, cs_], S_buf[rs, cs_],
                                        gam[rs, gcol:gcol + 1], None, ALU.mult).then_inc(SEM["dv"], 1)
                        tk("dv")
                        v.tensor_add(S_buf[rs, cs_], S_buf[rs, cs_],
                                     pss[h % 2][rs, 0:128]).then_inc(SEM["dv"], 1)
                        tk("dv")

        if limit >= 5:
          with nc.Block() as blk:
            @blk.scalar
            def _(ac):
                MARK["contrib"] = dq(ac, "sc0", st_contrib[:, :], S_buf[:, :])

            @blk.gpsimd
            def _(g):
                wt(g, "gp", "sc0", MARK["contrib"])
                g.collective_compute(
                    "AllGather", ALU.bypass,
                    replica_groups=[[0, 1, 2, 3, 4, 5, 6, 7]],
                    ins=[st_contrib.ap().opt()],
                    outs=[st_gather.ap().opt()],
                ).then_inc(SEM["cc"])
                tk("cc")
                g.wait_ge(SEM["cc"], CNT["cc"])
                g.indirect_dma_start(
                    S_init[:, :], None,
                    st_gather[0:128, :],
                    bass.IndirectOffsetOnAxis(idx_sb[:, 0:1], axis=0),
                ).then_inc(SEM["gi"], 16)
                MARK["sinit"] = tk("gi", 16)
                for e in range(8):
                    dq(g, "g2b", w2bf[e, :, :], w2r[e, :, :])
                dq(g, "g2b", sw1bf[:, :], sw1_n[:, :])
                for e in range(8):
                    dq(g, "g2b", sw2bf[e, :, :], sw2r[e, :, :])
                MARK["wbf2"] = CNT["g2b"]

            @blk.vector
            def _(v):
                wt(v, "dv", "gi", MARK["sinit"])
                v.tensor_scalar(S_buf[:, :], S_init[:, :], maskS_sb[:, 0:1],
                                None, ALU.mult).then_inc(SEM["dv"], 1)
                MARK["smask"] = tk("dv")

            @blk.scalar
            def _(ac2):
                wt(ac2, "ac", "dv", MARK["smask"])
                ac2.copy(S_bf[0][:, :], S_buf[:, :]).then_inc(SEM["ac"], 1)
                tk("ac")
        p4.close()

        p5 = ExitStack()
        qc4 = p5.enter_context(nc.sbuf_tensor("qc4", [128, 8 * 512], BF16))
        kc4 = p5.enter_context(nc.sbuf_tensor("kc4", [128, 8 * 512], BF16))
        Am = [p5.enter_context(nc.sbuf_tensor(f"Am{i}", [128, 128], BF16)) for i in range(2)]
        sqO = [p5.enter_context(nc.sbuf_tensor(f"sqO{i}", [128, 128], F32)) for i in range(2)]
        on_t = [p5.enter_context(nc.sbuf_tensor(f"on_t{i}", [128, 2048], BF16)) for i in range(2)]
        r1c = p5.enter_context(nc.sbuf_tensor("r1c", [128, 1], F32))
        r2c = p5.enter_context(nc.sbuf_tensor("r2c", [128, 1], F32))
        muc = p5.enter_context(nc.sbuf_tensor("muc", [128, 1], F32))
        mu2c = p5.enter_context(nc.sbuf_tensor("mu2c", [128, 1], F32))
        varc = p5.enter_context(nc.sbuf_tensor("varc", [128, 1], F32))
        sdc = p5.enter_context(nc.sbuf_tensor("sdc", [128, 1], F32))
        rsc = p5.enter_context(nc.sbuf_tensor("rsc", [128, 1], F32))
        psA = [p5.enter_context(nc.psum_tensor(f"psA{i}", [128, 512], F32)) for i in range(2)]
        psO = [p5.enter_context(nc.psum_tensor(f"psO{i}", [128, 512], F32)) for i in range(2)]
        psS = [p5.enter_context(nc.psum_tensor(f"psS{i}", [128, 512], F32)) for i in range(2)]
        qk_tick = [0]
        onst_base = [CNT["so0"], CNT["so1"]]
        qkst_done = [CNT["s0"], CNT["s1"]]

        for ch in range(NCH if limit >= 6 else 0):
            with nc.Block() as blk:
                pe0, dv0, ac0 = CNT["pe"], CNT["dv"], CNT["ac"]

                @blk.sync
                def _(sy, ch=ch):
                    if ch % 4 == 0:
                        wt(sy, "sy", "s0", qkst_done[0])
                        wt(sy, "sy", "s1", qkst_done[1])
                        sem = "q%d" % ((ch // 4) % 2)
                        for fc in range(8):
                            dq(sy, sem, qc4[:, fc * 512:(fc + 1) * 512],
                               qt_s[fc * 128:(fc + 1) * 128, ch * 128:(ch + 4) * 128])
                            dq(sy, sem, kc4[:, fc * 512:(fc + 1) * 512],
                               kt_s[fc * 128:(fc + 1) * 128, ch * 128:(ch + 4) * 128])
                        qk_tick[0] = CNT[sem]

                @blk.tensor
                def _(te, ch=ch, pe0=pe0, dv0=dv0, ac0=ac0):
                    wt(te, "te", "q%d" % ((ch // 4) % 2), qk_tick[0])
                    co = (ch % 4) * 128
                    for h in range(NH):
                        pb = (h % 2) * 64
                        qcol = (h // 2) * 512 + co
                        if h >= 2:
                            wt(te, "te", "dv", dv0 + 9 * (h - 2) + 1)
                        te.matmul(psA[h % 2][:, 0:128],
                                  kc4[pb:pb + 64, qcol:qcol + 128],
                                  qc4[pb:pb + 64, qcol:qcol + 128],
                                  start=True, stop=True).then_inc(SEM["pe"], 1)
                        tk("pe")
                        wt(te, "te", "dv", dv0 + 9 * h + 1)
                        if h >= 2:
                            wt(te, "te", "dv", dv0 + 9 * (h - 2) + 9)
                            wt(te, "te", "ac", ac0 + 3 * (h - 2) + 1)
                        te.matmul(psO[h % 2][:, 0:128], Am[h % 2][:, :],
                                  v_sb[:, ch * 2048 + h * 128: ch * 2048 + (h + 1) * 128],
                                  start=True, stop=False)
                        te.matmul(psO[h % 2][:, 0:128],
                                  qc4[pb:pb + 64, qcol:qcol + 128],
                                  S_bf[ch % 2][pb:pb + 64, (h // 2) * 128:(h // 2) * 128 + 128],
                                  start=False, stop=True).then_inc(SEM["pe"], 1)
                        tk("pe")
                        if h >= 2:
                            wt(te, "te", "dv", dv0 + 9 * (h - 2) + 3)
                        kcol = ch * 1024 + (h // 2) * 128 + pb
                        te.matmul(psS[h % 2][pb:pb + 64, 0:128],
                                  khT[:, kcol:kcol + 64],
                                  v_sb[:, ch * 2048 + h * 128: ch * 2048 + (h + 1) * 128],
                                  start=True, stop=True).then_inc(SEM["pe"], 1)
                        tk("pe")

                @blk.vector
                def _(v, ch=ch, pe0=pe0, dv0=dv0, ac0=ac0):
                    for h in range(NH):
                        pb = (h % 2) * 64
                        rs = slice(pb, pb + 64)
                        cs_ = slice((h // 2) * 128, (h // 2) * 128 + 128)
                        wt(v, "dv", "pe", pe0 + 3 * h + 1)
                        v.tensor_mul(Am[h % 2][:, :], psA[h % 2][:, 0:128], maskut[:, :]
                                     ).then_inc(SEM["dv"], 1)
                        tk("dv")
                        wt(v, "dv", "pe", pe0 + 3 * h + 3)
                        gcol = (h // 2) * 16 + ch
                        v.tensor_scalar(S_buf[rs, cs_], S_buf[rs, cs_],
                                        gam[rs, gcol:gcol + 1], None, ALU.mult).then_inc(SEM["dv"], 1)
                        tk("dv")
                        v.tensor_add(S_buf[rs, cs_], S_buf[rs, cs_],
                                     psS[h % 2][rs, 0:128]).then_inc(SEM["dv"], 1)
                        tk("dv")
                        wt(v, "dv", "pe", pe0 + 3 * h + 2)
                        v.reduce_sum(r1c[:, :], psO[h % 2][:, 0:128], axis=AX.X).then_inc(SEM["dv"], 1)
                        tk("dv")
                        wt(v, "dv", "ac", ac0 + 3 * h + 1)
                        v.reduce_sum(r2c[:, :], sqO[h % 2][:, :], axis=AX.X).then_inc(SEM["dv"], 1)
                        tk("dv")
                        v.tensor_scalar(muc[:, :], r1c[:, :], 1.0 / 128.0, None, ALU.mult
                                        ).then_inc(SEM["dv"], 1)
                        tk("dv")
                        wt(v, "dv", "ac", ac0 + 3 * h + 2)
                        v.tensor_scalar(varc[:, :], r2c[:, :], 1.0 / 128.0, mu2c[:, 0:1],
                                        ALU.mult, ALU.subtract).then_inc(SEM["dv"], 1)
                        tk("dv")
                        wt(v, "dv", "ac", ac0 + 3 * h + 3)
                        v.reciprocal(rsc[:, :], sdc[:, :]).then_inc(SEM["dv"], 1)
                        tk("dv")
                        if ch >= 2 and h == 0:
                            wt(v, "dv", "so%d" % (ch % 2),
                               onst_base[ch % 2] + (ch // 2) * 16)
                        v.tensor_scalar(on_t[ch % 2][:, h * 128:(h + 1) * 128],
                                        psO[h % 2][:, 0:128], muc[:, 0:1], rsc[:, 0:1],
                                        ALU.subtract, ALU.mult).then_inc(SEM["dv"], 1)
                        tk("dv")

                @blk.scalar
                def _(ac, ch=ch, pe0=pe0, dv0=dv0, ac0=ac0):
                    for h in range(NH):
                        wt(ac, "ac", "pe", pe0 + 3 * h + 2)
                        if h >= 2:
                            wt(ac, "ac", "dv", dv0 + 9 * (h - 2) + 5)
                        wt(ac, "ac", "dv", dv0 + 9 * h + 4)
                        ac.activation(sqO[h % 2][:, :], psO[h % 2][:, 0:128], AF.Square
                                      ).then_inc(SEM["ac"], 1)
                        tk("ac")
                        wt(ac, "ac", "dv", dv0 + 9 * h + 6)
                        ac.activation(mu2c[:, :], muc[:, :], AF.Square).then_inc(SEM["ac"], 1)
                        tk("ac")
                        wt(ac, "ac", "dv", dv0 + 9 * h + 7)
                        ac.activation(sdc[:, :], varc[:, :], AF.Sqrt, bias=fconst[:, 2:3]).then_inc(SEM["ac"], 1)
                        tk("ac")
                    wt(ac, "ac", "dv", dv0 + 9 * 15 + 3)
                    ac.copy(S_bf[(ch + 1) % 2][:, :], S_buf[:, :]).then_inc(SEM["ac"], 1)
                    tk("ac")
                    wt(ac, "ac", "dv", dv0 + 9 * 15 + 9)
                    dq(ac, "so%d" % (ch % 2), onorm_s[ch * 128:(ch + 1) * 128, :],
                       on_t[ch % 2][:, :])
        p5.close()
        sb2.close()
        sab.close()

        sc2 = ExitStack()
        pre_sb = sc2.enter_context(nc.sbuf_tensor("pre_sb", [128, 16 * 2048], BF16))
        p6 = ExitStack()
        wg_sb = p6.enter_context(nc.sbuf_tensor("wg_sb", [128, 16 * 2048], BF16))
        xsl6 = p6.enter_context(nc.sbuf_tensor("xsl6", [128, 16 * 512], BF16))
        bgb = p6.enter_context(nc.sbuf_tensor("bgb", [128, 2048], F32))
        bg_sb = p6.enter_context(nc.sbuf_tensor("bg_sb", [1, 2048], F32))
        oc2 = [p6.enter_context(nc.sbuf_tensor(f"oc2_{i}", [128, 2048], BF16)) for i in range(2)]
        tmp6 = [p6.enter_context(nc.sbuf_tensor(f"tmp6_{i}", [128, 512], F32)) for i in range(2)]
        sil6 = [p6.enter_context(nc.sbuf_tensor(f"sil6_{i}", [128, 512], F32)) for i in range(2)]
        ptm = [p6.enter_context(nc.sbuf_tensor(f"ptm{i}", [128, 512], BF16)) for i in range(2)]
        pg6 = [p6.enter_context(nc.psum_tensor(f"pg6_{i}", [128, 512], F32)) for i in range(2)]
        ptp = [p6.enter_context(nc.psum_tensor(f"ptp{i}", [128, 1024], BF16)) for i in range(2)]
        pbg = p6.enter_context(nc.psum_tensor("pbg", [128, 512], F32))
        wg_tick = [0]
        x6_tick = [0]
        oc_tick = [0, 0]

        if limit >= 7:
          with nc.Block() as blk:
            @blk.gpsimd
            def _(g):
                for ic in range(16):
                    dq(g, "gg", wg_sb[:, ic * 2048:(ic + 1) * 2048],
                       wgT[ic * 128:(ic + 1) * 128, :])
                wg_tick[0] = CNT["gg"]

            @blk.sync
            def _(sy):
                wt(sy, "sy", "so0", CNT["so0"])
                wt(sy, "sy", "so1", CNT["so1"])
                MARK["bg"] = dq(sy, "c0", bg_sb[:, :], bg_row[:, :])
                oc_tick[0] = dq(sy, "o0", oc2[0][:, :], onorm_s[0:128, :])

            pe_b, dv_b = CNT["pe"], CNT["dv"]

            @blk.tensor
            def _(te, pe_b=pe_b, dv_b=dv_b):
                wt(te, "te", "c0", MARK["bg"])
                for j4 in range(4):
                    if j4 >= 1:
                        wt(te, "te", "dv", dv_b + j4)
                    te.matmul(pbg[:, :], onesr[0:1, :], bg_sb[0:1, j4 * 512:(j4 + 1) * 512],
                              start=True, stop=True).then_inc(SEM["pe"], 1)
                    tk("pe")

            @blk.vector
            def _(v, pe_b=pe_b, dv_b=dv_b):
                for j4 in range(4):
                    wt(v, "dv", "pe", pe_b + j4 + 1)
                    v.tensor_copy(bgb[:, j4 * 512:(j4 + 1) * 512], pbg[:, :]).then_inc(SEM["dv"], 1)
                    tk("dv")

        for tci in range(16 if limit >= 7 else 0):
            with nc.Block() as blk:
                pe0, dv0, ac0 = CNT["pe"], CNT["dv"], CNT["ac"]

                @blk.sync
                def _(sy, tci=tci):
                    if tci % 4 == 0:
                        sem = "x%d" % ((tci // 4) % 2)
                        for ic in range(16):
                            dq(sy, sem, xsl6[:, ic * 512:(ic + 1) * 512],
                               xbf[ic * 128:(ic + 1) * 128, (tci // 4) * 512:(tci // 4 + 1) * 512])
                        x6_tick[0] = CNT[sem]
                    if tci + 1 < 16:
                        oc_tick[(tci + 1) % 2] = dq(
                            sy, "o%d" % ((tci + 1) % 2),
                            oc2[(tci + 1) % 2][:, :],
                            onorm_s[(tci + 1) * 128:(tci + 2) * 128, :])

                @blk.tensor
                def _(te, tci=tci, pe0=pe0, dv0=dv0, ac0=ac0):
                    wt(te, "te", "x%d" % ((tci // 4) % 2), x6_tick[0])
                    wt(te, "te", "gg", wg_tick[0])
                    co = (tci % 4) * 128
                    for ft in range(4):
                        if ft >= 2:
                            wt(te, "te", "dv", dv0 + 6 * (ft - 2) + 1)
                        ins = None
                        for ic in range(16):
                            ins = te.matmul(pg6[ft % 2][:, :],
                                            xsl6[:, ic * 512 + co: ic * 512 + co + 128],
                                            wg_sb[:, ic * 2048 + ft * 512: ic * 2048 + (ft + 1) * 512],
                                            start=(ic == 0), stop=(ic == 15))
                        ins.then_inc(SEM["pe"], 1)
                        tk("pe")
                        wt(te, "te", "dv", dv0 + 6 * ft + 2)
                        if ft >= 2:
                            wt(te, "te", "dv", dv0 + 6 * (ft - 2) + 6)
                        for sub in range(4):
                            te.transpose(ptp[ft % 2][:, sub * 128:(sub + 1) * 128],
                                         ptm[ft % 2][:, sub * 128:(sub + 1) * 128],
                                         identb[:, :]).then_inc(SEM["pe"], 1)
                            tk("pe")

                @blk.vector
                def _(v, tci=tci, pe0=pe0, dv0=dv0, ac0=ac0):
                    for ft in range(4):
                        wt(v, "dv", "pe", pe0 + 5 * ft + 1)
                        if ft >= 2:
                            wt(v, "dv", "ac", ac0 + ft - 1)
                        v.tensor_add(tmp6[ft % 2][:, :], pg6[ft % 2][:, :],
                                     bgb[:, ft * 512:(ft + 1) * 512]).then_inc(SEM["dv"], 1)
                        tk("dv")
                        wt(v, "dv", "ac", ac0 + ft + 1)
                        wt(v, "dv", "o%d" % (tci % 2), oc_tick[tci % 2])
                        if ft >= 2:
                            wt(v, "dv", "pe", pe0 + 5 * (ft - 2) + 5)
                        v.tensor_mul(ptm[ft % 2][:, :], sil6[ft % 2][:, :],
                                     oc2[tci % 2][:, ft * 512:(ft + 1) * 512]).then_inc(SEM["dv"], 1)
                        tk("dv")
                        wt(v, "dv", "pe", pe0 + 5 * ft + 5)
                        for sub in range(4):
                            fcg = ft * 4 + sub
                            v.tensor_copy(
                                pre_sb[:, fcg * 2048 + tci * 128: fcg * 2048 + (tci + 1) * 128],
                                ptp[ft % 2][:, sub * 128:(sub + 1) * 128]).then_inc(SEM["dv"], 1)
                            tk("dv")

                @blk.scalar
                def _(ac, tci=tci, pe0=pe0, dv0=dv0, ac0=ac0):
                    for ft in range(4):
                        wt(ac, "ac", "dv", dv0 + 6 * ft + 1)
                        if ft >= 2:
                            wt(ac, "ac", "dv", dv0 + 6 * (ft - 2) + 2)
                        ac.activation(sil6[ft % 2][:, :], tmp6[ft % 2][:, :], AF.Silu
                                      ).then_inc(SEM["ac"], 1)
                        tk("ac")
        p6.close()

        p7 = ExitStack()
        wo_sb = p7.enter_context(nc.sbuf_tensor("wo_sb", [128, 16 * 2048], BF16))
        r1b = p7.enter_context(nc.sbuf_tensor("r1b", [128, 16 * 512], F32))
        xr = [p7.enter_context(nc.sbuf_tensor(f"xr{i}", [128, 512], F32)) for i in range(2)]
        sq7 = [p7.enter_context(nc.sbuf_tensor(f"sq7_{i}", [128, 512], F32)) for i in range(2)]
        x1t = [p7.enter_context(nc.sbuf_tensor(f"x1t{i}", [128, 512], F32)) for i in range(2)]
        t7 = p7.enter_context(nc.sbuf_tensor("t7", [128, 512], F32))
        mu_row = p7.enter_context(nc.sbuf_tensor("mu_row", [1, 512], F32))
        msq_row = p7.enter_context(nc.sbuf_tensor("msq_row", [1, 512], F32))
        mu2_row = p7.enter_context(nc.sbuf_tensor("mu2_row", [1, 512], F32))
        var_row = p7.enter_context(nc.sbuf_tensor("var_row", [1, 512], F32))
        sd_row = p7.enter_context(nc.sbuf_tensor("sd_row", [1, 512], F32))
        rs_row = p7.enter_context(nc.sbuf_tensor("rs_row", [1, 512], F32))
        statacc1 = p7.enter_context(nc.sbuf_tensor("statacc1", [1, 512], F32))
        statacc2 = p7.enter_context(nc.sbuf_tensor("statacc2", [1, 512], F32))
        pw7 = [p7.enter_context(nc.psum_tensor(f"pw7_{i}", [128, 512], F32)) for i in range(3)]
        pstat = [p7.enter_context(nc.psum_tensor(f"pstat{i}", [128, 512], F32)) for i in range(2)]
        pbc = [p7.enter_context(nc.psum_tensor(f"pbc{i}", [128, 512], F32)) for i in range(2)]
        wo_tick = [0]

        if limit >= 8:
          with nc.Block() as blk:
            @blk.gpsimd
            def _(g):
                for ic in range(16):
                    dq(g, "go", wo_sb[:, ic * 2048:(ic + 1) * 2048],
                       woT[ic * 128:(ic + 1) * 128, :])
                wo_tick[0] = CNT["go"]

        for tt in range(NTT if limit >= 8 else 0):
            with nc.Block() as blk:
                pe0, dv0, ac0 = CNT["pe"], CNT["dv"], CNT["ac"]
                xr_b = [CNT["xr0"], CNT["xr1"]]
                xs_b = [CNT["xs0"], CNT["xs1"]]
                te_seq = []
                for fo in range(16):
                    te_seq.append(("W", fo))
                    if fo >= 2:
                        te_seq.append(("r1", fo - 2))
                        te_seq.append(("sq", fo - 2))
                for j in (14, 15):
                    te_seq.append(("r1", j))
                    te_seq.append(("sq", j))
                te_seq += [("bc", 0), ("bc", 1)]
                pe_t = {ev: pe0 + i + 1 for i, ev in enumerate(te_seq)}
                dv_seq = [("ms", 0), ("ms", 1)]
                for fo in range(16):
                    dv_seq += [("ev", fo), ("a1", fo), ("a2", fo)]
                dv_seq += [("mu",), ("msq",), ("var",), ("rec",)]
                for fo in range(16):
                    dv_seq += [("sub", fo), ("mul", fo)]
                dv_t = {ev: dv0 + i + 1 for i, ev in enumerate(dv_seq)}

                @blk.sync
                def _(sy, tt=tt, dv_t=dv_t):
                    for fo in range(16):
                        if fo >= 2:
                            wt(sy, "sy", "dv", dv_t[("ev", fo - 2)])
                        dq(sy, "xr%d" % (fo % 2), xr[fo % 2][:, :],
                           x_fm[fo * 128:(fo + 1) * 128, tt * 512:(tt + 1) * 512])

                @blk.tensor
                def _(te, tt=tt, pe_t=pe_t, dv_t=dv_t, ac0=ac0):
                    wt(te, "te", "go", wo_tick[0])

                    def stat_one(j):
                        wt(te, "te", "dv", dv_t[("ev", j)])
                        if j >= 1:
                            wt(te, "te", "dv", dv_t[("a1", j - 1)])
                        te.matmul(pstat[0][0:1, :], onesc[:, 0:1],
                                  r1b[:, j * 512:(j + 1) * 512],
                                  start=True, stop=True).then_inc(SEM["pe"], 1)
                        tk("pe")
                        wt(te, "te", "ac", ac0 + j + 1)
                        if j >= 1:
                            wt(te, "te", "dv", dv_t[("a2", j - 1)])
                        te.matmul(pstat[1][0:1, :], onesc[:, 0:1],
                                  sq7[j % 2][:, :],
                                  start=True, stop=True).then_inc(SEM["pe"], 1)
                        tk("pe")

                    for fo in range(16):
                        if fo >= 3:
                            wt(te, "te", "dv", dv_t[("ev", fo - 3)])
                        ins = None
                        for ic in range(16):
                            ins = te.matmul(pw7[fo % 3][:, :],
                                            wo_sb[:, ic * 2048 + fo * 128: ic * 2048 + (fo + 1) * 128],
                                            pre_sb[:, ic * 2048 + tt * 512: ic * 2048 + (tt + 1) * 512],
                                            start=(ic == 0), stop=(ic == 15))
                        ins.then_inc(SEM["pe"], 1)
                        tk("pe")
                        if fo >= 2:
                            stat_one(fo - 2)
                    for j in (14, 15):
                        stat_one(j)
                    wt(te, "te", "dv", dv_t[("mu",)])
                    te.matmul(pbc[0][:, :], onesr[0:1, :], mu_row[0:1, :],
                              start=True, stop=True).then_inc(SEM["pe"], 1)
                    tk("pe")
                    wt(te, "te", "dv", dv_t[("rec",)])
                    te.matmul(pbc[1][:, :], onesr[0:1, :], rs_row[0:1, :],
                              start=True, stop=True).then_inc(SEM["pe"], 1)
                    tk("pe")

                @blk.vector
                def _(v, tt=tt, pe_t=pe_t, dv_t=dv_t, ac0=ac0, xr_b=xr_b, xs_b=xs_b):
                    v.memset(statacc1[:, :], 0.0).then_inc(SEM["dv"], 1)
                    tk("dv")
                    v.memset(statacc2[:, :], 0.0).then_inc(SEM["dv"], 1)
                    tk("dv")
                    for fo in range(16):
                        wt(v, "dv", "pe", pe_t[("W", fo)])
                        wt(v, "dv", "xr%d" % (fo % 2), xr_b[fo % 2] + (fo // 2 + 1) * 16)
                        v.tensor_add(r1b[:, fo * 512:(fo + 1) * 512], pw7[fo % 3][:, :],
                                     xr[fo % 2][:, :]).then_inc(SEM["dv"], 1)
                        tk("dv")
                        wt(v, "dv", "pe", pe_t[("r1", fo)])
                        v.tensor_add(statacc1[:, :], statacc1[:, :], pstat[0][0:1, :]
                                     ).then_inc(SEM["dv"], 1)
                        tk("dv")
                        wt(v, "dv", "pe", pe_t[("sq", fo)])
                        v.tensor_add(statacc2[:, :], statacc2[:, :], pstat[1][0:1, :]
                                     ).then_inc(SEM["dv"], 1)
                        tk("dv")
                    v.tensor_scalar(mu_row[:, :], statacc1[:, :], 1.0 / 2048.0, None,
                                    ALU.mult).then_inc(SEM["dv"], 1)
                    tk("dv")
                    v.tensor_scalar(msq_row[:, :], statacc2[:, :], 1.0 / 2048.0, None,
                                    ALU.mult).then_inc(SEM["dv"], 1)
                    tk("dv")
                    wt(v, "dv", "ac", ac0 + 17)
                    v.tensor_sub(var_row[:, :], msq_row[:, :], mu2_row[:, :]).then_inc(SEM["dv"], 1)
                    tk("dv")
                    wt(v, "dv", "ac", ac0 + 18)
                    v.reciprocal(rs_row[:, :], sd_row[:, :]).then_inc(SEM["dv"], 1)
                    tk("dv")
                    wt(v, "dv", "pe", pe_t[("bc", 1)])
                    for fo in range(16):
                        v.tensor_sub(t7[:, :], r1b[:, fo * 512:(fo + 1) * 512], pbc[0][:, :]
                                     ).then_inc(SEM["dv"], 1)
                        tk("dv")
                        if fo >= 2:
                            wt(v, "dv", "xs%d" % (fo % 2),
                               xs_b[fo % 2] + (fo // 2) * 16)
                        v.tensor_mul(x1t[fo % 2][:, :], t7[:, :], pbc[1][:, :]).then_inc(SEM["dv"], 1)
                        tk("dv")

                @blk.scalar
                def _(ac, tt=tt, pe_t=pe_t, dv_t=dv_t, ac0=ac0):
                    for fo in range(16):
                        wt(ac, "ac", "dv", dv_t[("ev", fo)])
                        if fo >= 2:
                            wt(ac, "ac", "pe", pe_t[("sq", fo - 2)])
                        ac.activation(sq7[fo % 2][:, :], r1b[:, fo * 512:(fo + 1) * 512],
                                      AF.Square).then_inc(SEM["ac"], 1)
                        tk("ac")
                    wt(ac, "ac", "dv", dv_t[("mu",)])
                    ac.activation(mu2_row[:, :], mu_row[:, :], AF.Square).then_inc(SEM["ac"], 1)
                    tk("ac")
                    wt(ac, "ac", "dv", dv_t[("var",)])
                    ac.activation(sd_row[:, :], var_row[:, :], AF.Sqrt, bias=fconst[0:1, 2:3]).then_inc(SEM["ac"], 1)
                    tk("ac")
                    for fo in range(16):
                        wt(ac, "ac", "dv", dv_t[("mul", fo)])
                        dq(ac, "xs%d" % (fo % 2),
                           x1f_s[fo * 128:(fo + 1) * 128, tt * 512:(tt + 1) * 512],
                           x1t[fo % 2][:, :])
        p7.close()
        sc2.close()

        p8 = ExitStack()
        hbuf = p8.enter_context(nc.sbuf_tensor("hbuf", [128, 64 * 512], BF16))
        shb = p8.enter_context(nc.sbuf_tensor("shb", [128, 16 * 512], BF16))
        x1bt = p8.enter_context(nc.sbuf_tensor("x1bt", [128, 16 * 512], BF16))
        r2b = p8.enter_context(nc.sbuf_tensor("r2b", [128, 16 * 512], F32))
        wub = p8.enter_context(nc.sbuf_tensor("wub", [128, 20480], BF16))
        gate_sb = p8.enter_context(nc.sbuf_tensor("gate_sb", [128, 128], BF16))
        b1_sb = p8.enter_context(nc.sbuf_tensor("b1_sb", [128, 64], F32))
        b2_sb = p8.enter_context(nc.sbuf_tensor("b2_sb", [8, 2048], BF16))
        sb1_sb = p8.enter_context(nc.sbuf_tensor("sb1_sb", [128, 16], F32))
        sb2_sb = p8.enter_context(nc.sbuf_tensor("sb2_sb", [128, 16], F32))
        oh_sb = p8.enter_context(nc.sbuf_tensor("oh_sb", [8, 1024], F32))
        lg4 = p8.enter_context(nc.sbuf_tensor("lg4", [128, 32], F32))
        eqa = p8.enter_context(nc.sbuf_tensor("eqa", [128, 32], F32))
        cum4 = p8.enter_context(nc.sbuf_tensor("cum4", [128, 32], F32))
        c4 = p8.enter_context(nc.sbuf_tensor("c4", [128, 32], F32))
        sel14 = p8.enter_context(nc.sbuf_tensor("sel14", [128, 32], F32))
        sel24 = p8.enter_context(nc.sbuf_tensor("sel24", [128, 32], F32))
        lg24 = p8.enter_context(nc.sbuf_tensor("lg24", [128, 32], F32))
        m1c4 = p8.enter_context(nc.sbuf_tensor("m1c4", [128, 4], F32))
        m2c4 = p8.enter_context(nc.sbuf_tensor("m2c4", [128, 4], F32))
        dcc4 = p8.enter_context(nc.sbuf_tensor("dcc4", [128, 4], F32))
        g1c4 = p8.enter_context(nc.sbuf_tensor("g1c4", [128, 4], F32))
        g2c4 = p8.enter_context(nc.sbuf_tensor("g2c4", [128, 4], F32))
        gates4 = p8.enter_context(nc.sbuf_tensor("gates4", [128, 32], F32))
        ppj = p8.enter_context(nc.sbuf_tensor("ppj", [1, 8], F32))
        gfm8 = p8.enter_context(nc.sbuf_tensor("gfm8", [8, 512], F32))
        g8bf = p8.enter_context(nc.sbuf_tensor("g8bf", [8, 512], BF16))
        geb = p8.enter_context(nc.sbuf_tensor("geb", [128, 512], F32))
        glt = [p8.enter_context(nc.sbuf_tensor(f"glt{i}", [128, 512], F32)) for i in range(2)]
        x1ft = [p8.enter_context(nc.sbuf_tensor(f"x1ft{i}", [128, 512], F32)) for i in range(2)]
        sq8 = x1ft
        t8 = geb
        ot8 = glt
        mu_row8 = p8.enter_context(nc.sbuf_tensor("mu_row8", [1, 512], F32))
        statacc81 = p8.enter_context(nc.sbuf_tensor("statacc81", [1, 512], F32))
        statacc82 = p8.enter_context(nc.sbuf_tensor("statacc82", [1, 512], F32))
        msq_row8 = p8.enter_context(nc.sbuf_tensor("msq_row8", [1, 512], F32))
        mu2_row8 = p8.enter_context(nc.sbuf_tensor("mu2_row8", [1, 512], F32))
        sd_row8 = p8.enter_context(nc.sbuf_tensor("sd_row8", [1, 512], F32))
        rs_row8 = p8.enter_context(nc.sbuf_tensor("rs_row8", [1, 512], F32))
        zer8 = p8.enter_context(nc.sbuf_tensor("zer8", [128, 8], F32))
        psh = [p8.enter_context(nc.psum_tensor(f"psh{i}", [128, 512], F32)) for i in range(8)]

        if limit >= 9:
          with nc.Block() as blk:
            @blk.sync
            def _(sy):
                for dst, src in [(b1_sb, b1_t), (sb1_sb, sb1_t), (sb2_sb, sb2_t),
                                 (oh_sb, oh_d)]:
                    dq(sy, "c1", dst[:, :], src[:, :])
                MARK["p8c"] = CNT["c1"]

            @blk.gpsimd
            def _(g):
                dq(g, "gc8", gate_sb[:, :], gate_t[:, :])
                dq(g, "gc8", b2_sb[:, :], b2_n[:, :])
                MARK["p8g"] = dq(g, "gc8", zer8[:, :], zeros_d[:, 0:8])

        NDVG = 54
        for tt in range(NTT if limit >= 9 else 0):
            x1b_tick = [0]
            with nc.Block() as blk:
                pe0, dv0, ac0 = CNT["pe"], CNT["dv"], CNT["ac"]

                @blk.gpsimd
                def _(g, tt=tt):
                    wt(g, "gp", "xs0", CNT["xs0"])
                    wt(g, "gp", "xs1", CNT["xs1"])
                    for ic in range(16):
                        dq(g, "gx1", x1bt[:, ic * 512:(ic + 1) * 512],
                           x1f_s[ic * 128:(ic + 1) * 128, tt * 512:(tt + 1) * 512])
                    x1b_tick[0] = CNT["gx1"]

                @blk.tensor
                def _(te, tt=tt, pe0=pe0, dv0=dv0, ac0=ac0):
                    wt(te, "te", "gx1", x1b_tick[0])
                    wt(te, "te", "gc8", MARK["p8g"])
                    wt(te, "te", "c1", MARK["p8c"])
                    for tcc in range(4):
                        if tcc >= 2:
                            wt(te, "te", "dv", dv0 + (tcc - 2) + 1)
                        ins = None
                        for ic in range(16):
                            ins = te.matmul(psh[tcc % 2][0:128, 0:8],
                                            x1bt[:, ic * 512 + tcc * 128: ic * 512 + (tcc + 1) * 128],
                                            gate_sb[:, ic * 8:(ic + 1) * 8],
                                            start=(ic == 0), stop=(ic == 15))
                        ins.then_inc(SEM["pe"], 1)
                        tk("pe")
                    for tcc in range(4):
                        wt(te, "te", "dv", dv0 + 45)
                        if tcc >= 1:
                            wt(te, "te", "dv", dv0 + 45 + tcc)
                        te.transpose(psh[2][0:8, 0:128], gates4[:, tcc * 8:(tcc + 1) * 8],
                                     identf[:, :]).then_inc(SEM["pe"], 1)
                        tk("pe")

                @blk.vector
                def _(v, tt=tt, pe0=pe0, dv0=dv0, ac0=ac0):
                    nd = [0]

                    def D(ins):
                        ins.then_inc(SEM["dv"], 1)
                        tk("dv")
                        nd[0] += 1

                    def WA(k):
                        wt(v, "dv", "ac", ac0 + k)

                    for tcc in range(4):
                        wt(v, "dv", "pe", pe0 + tcc + 1)
                        D(v.tensor_copy(lg4[:, tcc * 8:(tcc + 1) * 8], psh[tcc % 2][0:128, 0:8]))
                    WA(1)
                    for tcc in range(4):
                        D(v.reduce_max(m1c4[:, tcc:tcc + 1], lg4[:, tcc * 8:(tcc + 1) * 8], axis=AX.X))
                    WA(2)
                    for tcc in range(4):
                        D(v.tensor_scalar(eqa[:, tcc * 8:(tcc + 1) * 8], lg4[:, tcc * 8:(tcc + 1) * 8],
                                          m1c4[:, tcc:tcc + 1], None, ALU.is_equal))
                    WA(3)
                    for tcc in range(4):
                        D(v.tensor_tensor_scan(cum4[:, tcc * 8:(tcc + 1) * 8],
                                               eqa[:, tcc * 8:(tcc + 1) * 8], zer8[:, :], 0.0,
                                               ALU.add, ALU.add))
                    WA(4)
                    D(v.tensor_scalar(c4[:, :], cum4[:, :], 1.0, None, ALU.is_equal))
                    WA(5)
                    D(v.tensor_mul(sel14[:, :], eqa[:, :], c4[:, :]))
                    WA(6)
                    D(v.scalar_tensor_tensor(lg24[:, :], sel14[:, :], -1e30, lg4[:, :],
                                             ALU.mult, ALU.add))
                    WA(7)
                    D(v.tensor_copy(c4[:, :], lg24[:, :]))
                    WA(8)
                    for tcc in range(4):
                        D(v.reduce_max(m2c4[:, tcc:tcc + 1], lg24[:, tcc * 8:(tcc + 1) * 8], axis=AX.X))
                    WA(9)
                    for tcc in range(4):
                        D(v.tensor_scalar(eqa[:, tcc * 8:(tcc + 1) * 8], lg24[:, tcc * 8:(tcc + 1) * 8],
                                          m2c4[:, tcc:tcc + 1], None, ALU.is_equal))
                    WA(10)
                    for tcc in range(4):
                        D(v.tensor_tensor_scan(cum4[:, tcc * 8:(tcc + 1) * 8],
                                               eqa[:, tcc * 8:(tcc + 1) * 8], zer8[:, :], 0.0,
                                               ALU.add, ALU.add))
                    WA(11)
                    D(v.tensor_scalar(c4[:, :], cum4[:, :], 1.0, None, ALU.is_equal))
                    WA(12)
                    D(v.tensor_mul(sel24[:, :], eqa[:, :], c4[:, :]))
                    WA(13)
                    D(v.tensor_sub(dcc4[:, :], m1c4[:, :], m2c4[:, :]))
                    WA(14)
                    D(v.tensor_scalar(g2c4[:, :], g1c4[:, :], -1.0, 1.0, ALU.mult, ALU.add))
                    WA(15)
                    for tcc in range(4):
                        D(v.tensor_scalar(sel14[:, tcc * 8:(tcc + 1) * 8], sel14[:, tcc * 8:(tcc + 1) * 8],
                                          g1c4[:, tcc:tcc + 1], None, ALU.mult))
                    for tcc in range(4):
                        D(v.tensor_scalar(sel24[:, tcc * 8:(tcc + 1) * 8], sel24[:, tcc * 8:(tcc + 1) * 8],
                                          g2c4[:, tcc:tcc + 1], None, ALU.mult))
                    WA(16)
                    D(v.tensor_add(gates4[:, :], sel14[:, :], sel24[:, :]))
                    for tcc in range(4):
                        wt(v, "dv", "pe", pe0 + 4 + tcc + 1)
                        D(v.tensor_copy(gfm8[0:8, tcc * 128:(tcc + 1) * 128], psh[2][0:8, 0:128]))
                    WA(17)
                    D(v.tensor_copy(g8bf[0:8, :], gfm8[0:8, :]))
                    MARK[("g8bf", tt)] = CNT["dv"]

                @blk.scalar
                def _(ac, tt=tt, pe0=pe0, dv0=dv0, ac0=ac0):
                    for k, dvtick in enumerate([4, 8, 12, 16, 17, 18, 19, 20,
                                                24, 28, 32, 33, 34], start=1):
                        wt(ac, "ac", "dv", dv0 + dvtick)
                        ac.copy(ppj[:, :], ppj[:, :]).then_inc(SEM["ac"], 1)
                        tk("ac")
                    wt(ac, "ac", "dv", dv0 + 35)
                    ac.activation(g1c4[:, :], dcc4[:, :], AF.Sigmoid).then_inc(SEM["ac"], 1)
                    tk("ac")
                    wt(ac, "ac", "dv", dv0 + 36)
                    ac.copy(ppj[:, :], ppj[:, :]).then_inc(SEM["ac"], 1)
                    tk("ac")
                    wt(ac, "ac", "dv", dv0 + 44)
                    ac.copy(ppj[:, :], ppj[:, :]).then_inc(SEM["ac"], 1)
                    tk("ac")
                    wt(ac, "ac", "dv", dv0 + 49)
                    ac.copy(ppj[:, :], ppj[:, :]).then_inc(SEM["ac"], 1)
                    tk("ac")

            if debug_taps and tt == 0:
                MARK["dbg_g8"] = nc.dram_tensor("dbg_g8", [8, 512], F32, kind="ExternalOutput")
                with nc.Block() as blk:
                    @blk.gpsimd
                    def _(g):
                        dq(g, "gi", MARK["dbg_g8"][:, :], gfm8[0:8, :])
            with nc.Block() as blk:
                pe0, dv0, ac0 = CNT["pe"], CNT["dv"], CNT["ac"]
                NPEU = [9] * 8 + [8, 8]
                NDVU = [9] * 8 + [0, 0]
                peb = [pe0 + sum(NPEU[:u]) for u in range(10)]
                dvb = [dv0 + sum(NDVU[:u]) for u in range(10)]
                acb = [ac0 + 8 * u for u in range(10)]
                w_b = [CNT["w0"], CNT["w1"]]

                def half_tick(u, half):
                    i = 2 * u + half
                    return w_b[i % 2] + (i // 2 + 1) * 256

                def grp_tick(u, hc):
                    return peb[u] + (1 if u < 8 else 0) + hc + 1

                @blk.sync
                def _(sy, tt=tt):
                    for u in range(10):
                        for half in range(2):
                            sem = "w%d" % ((2 * u + half) % 2)
                            if u >= 1:
                                wt(sy, "sy", "pe", grp_tick(u - 1, half * 4 + 3))
                            for ic in range(16):
                                dst = wub[:, half * 8192 + ic * 512: half * 8192 + (ic + 1) * 512]
                                if u < 8:
                                    dq(sy, sem, dst,
                                       w1bf[u, ic * 128:(ic + 1) * 128, half * 512:(half + 1) * 512])
                                else:
                                    sh2 = u - 8
                                    dq(sy, sem, dst,
                                       sw1bf[ic * 128:(ic + 1) * 128,
                                             sh2 * 1024 + half * 512: sh2 * 1024 + (half + 1) * 512])

                @blk.tensor
                def _(te, tt=tt):
                    wt(te, "te", "g1b", MARK["w1bf"])
                    wt(te, "te", "g2b", MARK["wbf2"])
                    wt(te, "te", "gx1", x1b_tick[0])
                    for u in range(10):
                        if u < 8:
                            if u >= 1:
                                wt(te, "te", "ac", acb[u - 1] + 8)
                            te.matmul(psh[7][:, :], oh_sb[0:8, u * 128:(u + 1) * 128],
                                      gfm8[0:8, :], start=True, stop=True).then_inc(SEM["pe"], 1)
                            tk("pe")
                        for hc in range(8):
                            half = hc // 4
                            wt(te, "te", "w%d" % ((2 * u + half) % 2), half_tick(u, half))
                            if u >= 1:
                                wt(te, "te", "ac", acb[u - 1] + hc + 1)
                            if hc == 7 and u < 8:
                                wt(te, "te", "dv", dvb[u] + 1)
                            ins = None
                            for ic in range(16):
                                ins = te.matmul(
                                    psh[hc][:, :],
                                    wub[:, half * 8192 + ic * 512 + (hc % 4) * 128:
                                        half * 8192 + ic * 512 + (hc % 4) * 128 + 128],
                                    x1bt[:, ic * 512:(ic + 1) * 512],
                                    start=(ic == 0), stop=(ic == 15))
                            ins.then_inc(SEM["pe"], 1)
                            tk("pe")

                @blk.vector
                def _(v, tt=tt):
                    for u in range(8):
                        wt(v, "dv", "pe", peb[u] + 1)
                        v.tensor_copy(geb[:, :], psh[7][:, :]).then_inc(SEM["dv"], 1)
                        tk("dv")
                        for hc in range(8):
                            wt(v, "dv", "ac", acb[u] + hc + 1)
                            v.tensor_mul(hbuf[:, (u * 8 + hc) * 512:(u * 8 + hc + 1) * 512],
                                         glt[hc % 2][:, :], geb[:, :]).then_inc(SEM["dv"], 1)
                            tk("dv")

                @blk.scalar
                def _(ac, tt=tt):
                    for u in range(10):
                        for hc in range(8):
                            wt(ac, "ac", "pe", grp_tick(u, hc))
                            if u < 8:
                                if u * 8 + hc >= 2:
                                    pu, ph = divmod(u * 8 + hc - 2, 8)
                                    wt(ac, "ac", "dv", dvb[pu] + 2 + ph)
                                ac.activation(glt[hc % 2][:, :], psh[hc][:, :], AF.Gelu,
                                              bias=b1_sb[:, u * 8 + hc: u * 8 + hc + 1]
                                              ).then_inc(SEM["ac"], 1)
                                tk("ac")
                            else:
                                sh2 = u - 8
                                ac.activation(shb[:, (sh2 * 8 + hc) * 512:(sh2 * 8 + hc + 1) * 512],
                                              psh[hc][:, :], AF.Gelu,
                                              bias=sb1_sb[:, sh2 * 8 + hc: sh2 * 8 + hc + 1]
                                              ).then_inc(SEM["ac"], 1)
                                tk("ac")

            if debug_taps and tt == 0:
                MARK["dbg_h"] = nc.dram_tensor("dbg_h", [128, 64 * 512], BF16, kind="ExternalOutput")
                MARK["dbg_sh"] = nc.dram_tensor("dbg_sh", [128, 16 * 512], BF16, kind="ExternalOutput")
                with nc.Block() as blk:
                    @blk.gpsimd
                    def _(g):
                        dq(g, "gi", MARK["dbg_h"][:, :], hbuf[:, :])
                        dq(g, "gi", MARK["dbg_sh"][:, :], shb[:, :])
            with nc.Block() as blk:
                pe0, dv0, ac0 = CNT["pe"], CNT["dv"], CNT["ac"]
                w_b2 = [CNT["w0"], CNT["w1"]]
                xf_b = [CNT["xf0"], CNT["xf1"]]

                @blk.sync
                def _(sy, tt=tt, pe0=pe0):
                    for fp in range(8):
                        sem = "w%d" % (fp % 2)
                        if fp >= 1:
                            wt(sy, "sy", "pe", pe0 + 2 * (fp - 1) + 2)
                        for j in range(10):
                            if j < 8:
                                dq(sy, sem,
                                   wub[:, (j * 8) * 256:(j * 8 + 8) * 256]
                                   .rearrange("p (c f) -> p c f", c=8),
                                   w2bf[fp, j * 1024:(j + 1) * 1024, :]
                                   .rearrange("(c p) f -> p c f", p=128))
                            else:
                                dq(sy, sem,
                                   wub[:, (j * 8) * 256:(j * 8 + 8) * 256]
                                   .rearrange("p (c f) -> p c f", c=8),
                                   sw2bf[fp, (j - 8) * 1024:(j - 7) * 1024, :]
                                   .rearrange("(c p) f -> p c f", p=128))

                @blk.scalar
                def _(ac, tt=tt, dv0=dv0):
                    wt(ac, "ac", "xs0", CNT["xs0"])
                    wt(ac, "ac", "xs1", CNT["xs1"])
                    for fp in range(8):
                        for half in range(2):
                            if fp >= 1:
                                wt(ac, "ac", "dv", dv0 + 2 * (fp - 1) + half + 1)
                            fo = fp * 2 + half
                            dq(ac, "xf%d" % half, x1ft[half][:, :],
                               x1f_s[fo * 128:(fo + 1) * 128, tt * 512:(tt + 1) * 512])

                @blk.tensor
                def _(te, tt=tt, pe0=pe0, dv0=dv0, w_b2=w_b2):
                    for fp in range(8):
                        wt(te, "te", "w%d" % (fp % 2),
                           w_b2[fp % 2] + (fp // 2 + 1) * 160)
                        for half in range(2):
                            if fp >= 1:
                                wt(te, "te", "dv", dv0 + 2 * (fp - 1) + half + 1)
                            te.matmul(psh[half][:, :],
                                      b2_sb[0:8, fp * 256 + half * 128: fp * 256 + half * 128 + 128],
                                      g8bf[0:8, :], start=True, stop=False)
                            ins = None
                            for ci in range(80):
                                rhs = hbuf[:, ci * 512:(ci + 1) * 512] if ci < 64 else \
                                    shb[:, (ci - 64) * 512:(ci - 63) * 512]
                                ins = te.matmul(
                                    psh[half][:, :],
                                    wub[:, ci * 256 + half * 128: ci * 256 + half * 128 + 128],
                                    rhs, start=False, stop=(ci == 79))
                            ins.then_inc(SEM["pe"], 1)
                            tk("pe")

                @blk.vector
                def _(v, tt=tt, pe0=pe0, dv0=dv0, xf_b=xf_b):
                    for fp in range(8):
                        for half in range(2):
                            fo = fp * 2 + half
                            wt(v, "dv", "pe", pe0 + 2 * fp + half + 1)
                            wt(v, "dv", "xf%d" % half, xf_b[half] + (fp + 1) * 16)
                            v.scalar_tensor_tensor(r2b[:, fo * 512:(fo + 1) * 512],
                                                   psh[half][:, :], sb2_sb[:, fo:fo + 1],
                                                   x1ft[half][:, :],
                                                   ALU.add, ALU.add).then_inc(SEM["dv"], 1)
                            tk("dv")

            if debug_taps and tt == 0:
                MARK["dbg_r2"] = nc.dram_tensor("dbg_r2", [128, 16 * 512], F32, kind="ExternalOutput")
                with nc.Block() as blk:
                    @blk.gpsimd
                    def _(g):
                        dq(g, "gi", MARK["dbg_r2"][:, :], r2b[:, :])
            with nc.Block() as blk:
                pe0, dv0, ac0 = CNT["pe"], CNT["dv"], CNT["ac"]
                os_b = [CNT["os0"], CNT["os1"]]

                @blk.tensor
                def _(te, tt=tt, pe0=pe0, dv0=dv0, ac0=ac0):
                    for fo in range(16):
                        if fo >= 1:
                            wt(te, "te", "dv", dv0 + 2 + 2 * (fo - 1) + 1)
                        te.matmul(psh[2][0:1, :], onesc[:, 0:1],
                                  r2b[:, fo * 512:(fo + 1) * 512],
                                  start=True, stop=True).then_inc(SEM["pe"], 1)
                        tk("pe")
                        wt(te, "te", "ac", ac0 + fo + 1)
                        if fo >= 1:
                            wt(te, "te", "dv", dv0 + 2 + 2 * (fo - 1) + 2)
                        te.matmul(psh[3][0:1, :], onesc[:, 0:1], sq8[fo % 2][:, :],
                                  start=True, stop=True).then_inc(SEM["pe"], 1)
                        tk("pe")
                    wt(te, "te", "dv", dv0 + 35)
                    te.matmul(psh[4][:, :], onesr[0:1, :], mu_row8[0:1, :],
                              start=True, stop=True).then_inc(SEM["pe"], 1)
                    tk("pe")
                    wt(te, "te", "dv", dv0 + 38)
                    te.matmul(psh[5][:, :], onesr[0:1, :], rs_row8[0:1, :],
                              start=True, stop=True).then_inc(SEM["pe"], 1)
                    tk("pe")

                @blk.scalar
                def _(ac, tt=tt, pe0=pe0, dv0=dv0, ac0=ac0):
                    for fo in range(16):
                        if fo >= 2:
                            wt(ac, "ac", "pe", pe0 + 2 * (fo - 2) + 2)
                        ac.activation(sq8[fo % 2][:, :], r2b[:, fo * 512:(fo + 1) * 512],
                                      AF.Square).then_inc(SEM["ac"], 1)
                        tk("ac")
                    wt(ac, "ac", "dv", dv0 + 35)
                    ac.activation(mu2_row8[:, :], mu_row8[:, :], AF.Square).then_inc(SEM["ac"], 1)
                    tk("ac")
                    wt(ac, "ac", "dv", dv0 + 37)
                    ac.activation(sd_row8[:, :], msq_row8[:, :], AF.Sqrt, bias=fconst[0:1, 2:3]
                                  ).then_inc(SEM["ac"], 1)
                    tk("ac")
                    for fo in range(16):
                        wt(ac, "ac", "dv", dv0 + 38 + 2 * fo + 2)
                        dq(ac, "os%d" % (fo % 2),
                           out_fm[fo * 128:(fo + 1) * 128, tt * 512:(tt + 1) * 512],
                           ot8[fo % 2][:, :])

                @blk.vector
                def _(v, tt=tt, pe0=pe0, dv0=dv0, ac0=ac0, os_b=os_b):
                    v.memset(statacc81[:, :], 0.0).then_inc(SEM["dv"], 1)
                    tk("dv")
                    v.memset(statacc82[:, :], 0.0).then_inc(SEM["dv"], 1)
                    tk("dv")
                    for fo in range(16):
                        wt(v, "dv", "pe", pe0 + 2 * fo + 1)
                        v.tensor_add(statacc81[:, :], statacc81[:, :], psh[2][0:1, :]
                                     ).then_inc(SEM["dv"], 1)
                        tk("dv")
                        wt(v, "dv", "pe", pe0 + 2 * fo + 2)
                        v.tensor_add(statacc82[:, :], statacc82[:, :], psh[3][0:1, :]
                                     ).then_inc(SEM["dv"], 1)
                        tk("dv")
                    v.tensor_scalar(mu_row8[:, :], statacc81[:, :], 1.0 / 2048.0, None,
                                    ALU.mult).then_inc(SEM["dv"], 1)
                    tk("dv")
                    v.tensor_scalar(msq_row8[:, :], statacc82[:, :], 1.0 / 2048.0, None,
                                    ALU.mult).then_inc(SEM["dv"], 1)
                    tk("dv")
                    wt(v, "dv", "ac", ac0 + 17)
                    v.tensor_sub(msq_row8[:, :], msq_row8[:, :], mu2_row8[:, :]).then_inc(SEM["dv"], 1)
                    tk("dv")
                    wt(v, "dv", "ac", ac0 + 18)
                    v.reciprocal(rs_row8[:, :], sd_row8[:, :]).then_inc(SEM["dv"], 1)
                    tk("dv")
                    wt(v, "dv", "pe", pe0 + 34)
                    for fo in range(16):
                        v.tensor_sub(t8[:, :], r2b[:, fo * 512:(fo + 1) * 512], psh[4][:, :]
                                     ).then_inc(SEM["dv"], 1)
                        tk("dv")
                        if fo >= 2:
                            wt(v, "dv", "os%d" % (fo % 2),
                               os_b[fo % 2] + (fo // 2) * 16)
                        v.tensor_mul(ot8[fo % 2][:, :], t8[:, :], psh[5][:, :]).then_inc(SEM["dv"], 1)
                        tk("dv")

        if debug_taps:
            dbg_qt = nc.dram_tensor("dbg_qt", [1024, T], BF16, kind="ExternalOutput")
            dbg_kt = nc.dram_tensor("dbg_kt", [1024, T], BF16, kind="ExternalOutput")
            dbg_on = nc.dram_tensor("dbg_on", [T, D], BF16, kind="ExternalOutput")
            dbg_x1 = nc.dram_tensor("dbg_x1", [D, T], F32, kind="ExternalOutput")
            with nc.Block() as blk:
                @blk.gpsimd
                def _(g):
                    for k in DMASEMS:
                        if CNT[k] > 0:
                            wt(g, "gp", k, CNT[k])
                    dq(g, "gi", dbg_qt[:, :], qt_s[:, :])
                    dq(g, "gi", dbg_kt[:, :], kt_s[:, :])
                    dq(g, "gi", dbg_on[:, :], onorm_s[:, :])
                    dq(g, "gi", dbg_x1[:, :], x1f_s[:, :])
                    wt(g, "gp", "gi", CNT["gi"])

        with nc.Block() as blk:
            @blk.sync
            def _(sy):
                for k in DMASEMS:
                    if CNT[k] > 0:
                        sy.wait_ge(SEM[k], CNT[k])
        p8.close()

    return nc


_CACHE = {}


def _prep_shared(inp):
    asc = np.ascontiguousarray
    f32 = lambda a: np.asarray(a, dtype=np.float32)
    Wq, Wk = f32(inp["L0_attn_Wq"]), f32(inp["L0_attn_Wk"])
    Wkg1, Wkg2 = f32(inp["L0_attn_Wkg1"]), f32(inp["L0_attn_Wkg2"])
    bkg2 = f32(inp["L0_attn_bkg2"])
    Wv, Wg, Wo = f32(inp["L0_attn_Wv"]), f32(inp["L0_attn_Wg"]), f32(inp["L0_attn_Wo"])
    bg = f32(inp["L0_attn_bg"])
    gate_w = f32(inp["L0_ffn_gate_w"])
    w1, b1 = f32(inp["L0_ffn_w1"]), f32(inp["L0_ffn_b1"])
    w2, b2 = f32(inp["L0_ffn_w2"]), f32(inp["L0_ffn_b2"])
    sw1, sb1 = f32(inp["L0_ffn_shared_w1"]), f32(inp["L0_ffn_shared_b1"])
    sw2, sb2 = f32(inp["L0_ffn_shared_w2"]), f32(inp["L0_ffn_shared_b2"])

    d = {}
    d["wqT"] = asc(Wq.T)
    d["wkT"] = asc(Wk.T)
    d["wkg1"] = asc(Wkg1.T.reshape(16, 128, 16).transpose(1, 0, 2).reshape(128, 256))
    d["wkg2"] = asc(Wkg2.T)
    d["bkg2_t"] = asc(bkg2.reshape(8, 128).T)
    d["wvT"] = asc(Wv.T)
    d["wgT"] = asc(Wg.T)
    d["woT"] = asc(Wo.T)
    d["bg_row"] = asc(bg.reshape(1, D))
    d["w1_n"] = asc(w1)
    d["w2r"] = asc(w2.reshape(8, 1024, 8, 256).transpose(2, 0, 1, 3).reshape(8, 8192, 256))
    d["sw1_n"] = asc(sw1)
    d["sw2r"] = asc(sw2.reshape(2048, 8, 256).transpose(1, 0, 2))
    d["gate_t"] = asc(gate_w.reshape(16, 128, 8).transpose(1, 0, 2).reshape(128, 128))
    d["b1_t"] = asc(b1.reshape(8, 8, 128).transpose(2, 0, 1).reshape(128, 64))
    d["b2_n"] = asc(b2)
    d["sb1_t"] = asc(sb1.reshape(16, 128).T)
    d["sb2_t"] = asc(sb2.reshape(16, 128).T)
    d["identb_d"] = np.eye(128, dtype=BFNP)
    d["identf_d"] = np.eye(128, dtype=np.float32)
    s = np.arange(128)[:, None]
    t = np.arange(128)[None, :]
    d["maskut_d"] = (s <= t).astype(np.float32)
    d["zeros_d"] = np.zeros((128, 128), np.float32)
    d["ones_r"] = np.ones((1, 128), np.float32)
    d["ones_c"] = np.ones((128, 1), np.float32)
    oh = np.zeros((8, 1024), np.float32)
    for e in range(8):
        oh[e, e * 128:(e + 1) * 128] = 1.0
    d["oh_d"] = oh
    em = np.zeros((128, 64), np.float32)
    for j in range(8):
        for e in range(8):
            if e > j:
                em[:, j * 8 + e] = 1.0
    d["emask_d"] = em
    fc = np.zeros((128, 4), np.float32)
    fc[:, 0] = LNB
    fc[:, 1] = 1.0
    fc[:, 2] = EPS
    d["fconst_d"] = fc
    return d


def kernel(**inputs):
    x = np.asarray(inputs["L0_attn_x"], dtype=np.float32)
    B, L, _ = x.shape
    shared = _prep_shared(inputs)

    if "nc" not in _CACHE:
        _CACHE["nc"] = build_program()
    nc = _CACHE["nc"]

    in_maps = []
    for c in range(8):
        b, hf = c // 2, c % 2
        m = dict(shared)
        m["x_fm"] = np.ascontiguousarray(x[b, hf * T:(hf + 1) * T, :].T)
        if c % 2 == 1:
            ix = (c - 1) * 128 + np.arange(128, dtype=np.int32)
            ms = np.ones((128, 1), np.float32)
        else:
            ix = c * 128 + np.arange(128, dtype=np.int32)
            ms = np.zeros((128, 1), np.float32)
        m["idx_d"] = ix[:, None]
        m["maskS_d"] = ms
        in_maps.append(m)

    res = run_bass_kernel_spmd(nc, in_maps, core_ids=list(range(8)))
    out = np.empty((B, L, D), np.float32)
    for c in range(8):
        b, hf = c // 2, c % 2
        out[b, hf * T:(hf + 1) * T, :] = res.results[c]["out_fm"].T
    return out


if __name__ == "__main__":
    nc = build_program()
    n = sum(1 for _ in nc.all_instructions())
    print("instructions:", n)


# version 03
# speedup vs baseline: 1.0004x; 1.0004x over previous
import math
import numpy as np
import ml_dtypes
from contextlib import ExitStack
import concourse.bass as bass
import concourse.mybir as mybir
from concourse.bass_utils import run_bass_kernel_spmd

F32 = mybir.dt.float32
BF16 = mybir.dt.bfloat16
I32 = mybir.dt.int32
AF = mybir.ActivationFunctionType
ALU = mybir.AluOpType
AX = mybir.AxisListType
BFNP = ml_dtypes.bfloat16

T = 2048
D = 2048
NH = 16
DK, DV = 64, 128
NCH = 16
NTT = 4
SCALE = 128.0 ** (-0.5)
NORMI = 1.0 / 16.0
EPS = 1e-5
LNB = math.log(SCALE)


def build_program(limit=99, debug_taps=False):
    nc = bass.Bass("TRN2", target_bir_lowering=False, detect_race_conditions=False)

    x_fm = nc.dram_tensor("x_fm", [D, T], F32, kind="ExternalInput")
    wqT = nc.dram_tensor("wqT", [D, 1024], F32, kind="ExternalInput")
    wkT = nc.dram_tensor("wkT", [D, 1024], F32, kind="ExternalInput")
    wkg1 = nc.dram_tensor("wkg1", [128, 256], F32, kind="ExternalInput")
    wkg2 = nc.dram_tensor("wkg2", [16, 1024], F32, kind="ExternalInput")
    bkg2_t = nc.dram_tensor("bkg2_t", [128, 8], F32, kind="ExternalInput")
    wvT = nc.dram_tensor("wvT", [D, D], F32, kind="ExternalInput")
    wgT = nc.dram_tensor("wgT", [D, D], F32, kind="ExternalInput")
    woT = nc.dram_tensor("woT", [D, D], F32, kind="ExternalInput")
    bg_row = nc.dram_tensor("bg_row", [1, D], F32, kind="ExternalInput")
    w1_n = nc.dram_tensor("w1_n", [8, D, 1024], F32, kind="ExternalInput")
    w2r = nc.dram_tensor("w2r", [8, 8192, 256], F32, kind="ExternalInput")
    sw1_n = nc.dram_tensor("sw1_n", [D, D], F32, kind="ExternalInput")
    sw2r = nc.dram_tensor("sw2r", [8, 2048, 256], F32, kind="ExternalInput")
    gate_t = nc.dram_tensor("gate_t", [128, 128], F32, kind="ExternalInput")
    b1_t = nc.dram_tensor("b1_t", [128, 64], F32, kind="ExternalInput")
    b2_n = nc.dram_tensor("b2_n", [8, D], F32, kind="ExternalInput")
    sb1_t = nc.dram_tensor("sb1_t", [128, 16], F32, kind="ExternalInput")
    sb2_t = nc.dram_tensor("sb2_t", [128, 16], F32, kind="ExternalInput")
    identb_d = nc.dram_tensor("identb_d", [128, 128], BF16, kind="ExternalInput")
    identf_d = nc.dram_tensor("identf_d", [128, 128], F32, kind="ExternalInput")
    maskut_d = nc.dram_tensor("maskut_d", [128, 128], F32, kind="ExternalInput")
    zeros_d = nc.dram_tensor("zeros_d", [128, 128], F32, kind="ExternalInput")
    ones_r = nc.dram_tensor("ones_r", [1, 128], F32, kind="ExternalInput")
    ones_c = nc.dram_tensor("ones_c", [128, 1], F32, kind="ExternalInput")
    oh_d = nc.dram_tensor("oh_d", [8, 1024], F32, kind="ExternalInput")
    emask_d = nc.dram_tensor("emask_d", [128, 64], F32, kind="ExternalInput")
    idx_d = nc.dram_tensor("idx_d", [128, 1], I32, kind="ExternalInput")
    fconst_d = nc.dram_tensor("fconst_d", [128, 4], F32, kind="ExternalInput")
    maskS_d = nc.dram_tensor("maskS_d", [128, 1], F32, kind="ExternalInput")

    out_fm = nc.dram_tensor("out_fm", [D, T], F32, kind="ExternalOutput")

    xbf = nc.dram_tensor("xbf", [D, T], BF16, kind="Internal")
    w1bf = nc.dram_tensor("w1bf", [8, D, 1024], BF16, kind="Internal")
    w2bf = nc.dram_tensor("w2bf", [8, 8192, 256], BF16, kind="Internal")
    sw1bf = nc.dram_tensor("sw1bf", [D, D], BF16, kind="Internal")
    sw2bf = nc.dram_tensor("sw2bf", [8, 2048, 256], BF16, kind="Internal")
    qt_s = nc.dram_tensor("qt_s", [1024, T], BF16, kind="Internal")
    kt_s = nc.dram_tensor("kt_s", [1024, T], BF16, kind="Internal")
    onorm_s = nc.dram_tensor("onorm_s", [T, D], BF16, kind="Internal")
    x1f_s = nc.dram_tensor("x1f_s", [D, T], F32, kind="Internal")
    st_contrib = nc.dram_tensor("st_contrib", [128, 1024], F32, kind="Internal")
    st_gather = nc.dram_tensor("st_gather", [1024, 1024], F32, kind="Internal")

    DMASEMS = ["c0", "c1", "gx", "gk", "gq", "gkk", "gv", "g1b", "gi", "g2b",
               "gg", "go", "gc8", "gx1", "x0", "x1", "q0", "q1", "o0", "o1",
               "s0", "s1", "so0", "so1", "sc0", "xr0", "xr1", "xs0", "xs1",
               "xf0", "xf1", "w0", "w1", "os0", "os1"]
    CNT = dict(pe=0, dv=0, ac=0, cc=0)
    for _k in DMASEMS:
        CNT[_k] = 0
    MARK = {}
    LASTW = {}
    SEM = {}

    def tk(key, step=1):
        CNT[key] += step
        return CNT[key]

    def wt(eng, ename, key, val):
        if val is None or val <= 0:
            return
        if LASTW.get((ename, key), 0) >= val:
            return
        LASTW[(ename, key)] = val
        eng.wait_ge(SEM[key], val)

    def dq(eng, name, dst, src):
        eng.dma_start(dst, src).then_inc(SEM[name], 16)
        CNT[name] += 16
        return CNT[name]

    ctx = ExitStack()
    with ctx:
        E = ctx.enter_context
        for k in CNT:
            SEM[k] = E(nc.semaphore("sem_" + k))

        identb = E(nc.sbuf_tensor("identb", [128, 128], BF16))
        identf = E(nc.sbuf_tensor("identf", [128, 128], F32))
        onesr = E(nc.sbuf_tensor("onesr", [1, 128], F32))
        onesc = E(nc.sbuf_tensor("onesc", [128, 1], F32))
        idx_sb = E(nc.sbuf_tensor("idx_sb", [128, 1], I32))
        fconst = E(nc.sbuf_tensor("fconst", [128, 4], F32))
        maskS_sb = E(nc.sbuf_tensor("maskS_sb", [128, 1], F32))

        MARK["sc_p0"] = nc.enter_named_scope("zz_p0", True)[0]
        with nc.Block() as blk:
            @blk.sync
            def _(sy):
                for dst, src in [(identb, identb_d), (identf, identf_d),
                                 (onesr, ones_r), (onesc, ones_c),
                                 (idx_sb, idx_d),
                                 (fconst, fconst_d), (maskS_sb, maskS_d)]:
                    dq(sy, "c0", dst[:, :], src[:, :])
                MARK["consts"] = CNT["c0"]

            @blk.gpsimd
            def _(g):
                MARK["xbf"] = dq(g, "gx", xbf[:, :], x_fm[:, :])

        sab = ExitStack()
        khT = sab.enter_context(nc.sbuf_tensor("khT", [128, NCH * 1024], BF16))
        maskut = sab.enter_context(nc.sbuf_tensor("maskut", [128, 128], F32))
        gam = sab.enter_context(nc.sbuf_tensor("gam", [128, 128], F32))
        S_buf = sab.enter_context(nc.sbuf_tensor("S_buf", [128, 1024], F32))
        S_bf = [sab.enter_context(nc.sbuf_tensor(f"S_bf{i}", [128, 1024], BF16)) for i in range(2)]
        S_init = sab.enter_context(nc.sbuf_tensor("S_init", [128, 1024], F32))
        sa2 = ExitStack()
        Gbuf = sa2.enter_context(nc.sbuf_tensor("Gbuf", [128, 8 * T], F32))
        zer = sa2.enter_context(nc.sbuf_tensor("zer", [128, 128], F32))
        wkg1_sb = sa2.enter_context(nc.sbuf_tensor("wkg1_sb", [128, 256], BF16))
        wkg2_sb = sa2.enter_context(nc.sbuf_tensor("wkg2_sb", [16, 1024], BF16))
        bkg2_sb = sa2.enter_context(nc.sbuf_tensor("bkg2_sb", [128, 8], F32))

        with nc.Block() as blk:
            @blk.sync
            def _(sy):
                for dst, src in [(maskut, maskut_d), (zer, zeros_d),
                                 (bkg2_sb, bkg2_t)]:
                    dq(sy, "c1", dst[:, :], src[:, :])
                MARK["consts2"] = CNT["c1"]

            @blk.gpsimd
            def _(g):
                dq(g, "gk", wkg1_sb[:, :], wkg1[:, :])
                MARK["wkg"] = dq(g, "gk", wkg2_sb[:, :], wkg2[:, :])

            @blk.vector
            def _(v):
                v.memset(S_buf[:, :], 0.0).then_inc(SEM["dv"], 1)
                tk("dv")

        nc.leave_named_scope("zz_p0", MARK["sc_p0"], True)
        MARK["sc_p1"] = nc.enter_named_scope("zz_p1", True)[0]
        p1 = ExitStack()
        xsl = [p1.enter_context(nc.sbuf_tensor(f"xsl{i}", [128, 16 * 512], BF16)) for i in range(2)]
        kg1b = p1.enter_context(nc.sbuf_tensor("kg1b", [16, 512], BF16))
        t_mn = p1.enter_context(nc.sbuf_tensor("t_mn", [128, 512], F32))
        t_ab = p1.enter_context(nc.sbuf_tensor("t_ab", [128, 512], F32))
        t_gk = p1.enter_context(nc.sbuf_tensor("t_gk", [128, 512], F32))
        ps1 = p1.enter_context(nc.psum_tensor("ps1", [128, 512], F32))
        ps2 = [p1.enter_context(nc.psum_tensor(f"ps2_{i}", [128, 512], F32)) for i in range(2)]
        xsl_tick = [0, 0]

        def load_xsl(sy, buf_i, tt):
            sem = "x%d" % buf_i
            for ic in range(16):
                dq(sy, sem, xsl[buf_i][:, ic * 512:(ic + 1) * 512],
                   xbf[ic * 128:(ic + 1) * 128, tt * 512:(tt + 1) * 512])
            xsl_tick[buf_i] = CNT[sem]

        for tt in range(NTT if limit >= 1 else 0):
            with nc.Block() as blk:
                bi = tt % 2
                pe0, dv0, ac0 = CNT["pe"], CNT["dv"], CNT["ac"]

                @blk.sync
                def _(sy, bi=bi, tt=tt):
                    if tt == 0:
                        wt(sy, "sy", "gx", MARK["xbf"])
                        load_xsl(sy, 0, 0)
                    if tt + 1 < NTT:
                        load_xsl(sy, (tt + 1) % 2, tt + 1)

                @blk.tensor
                def _(te, bi=bi, tt=tt, pe0=pe0, dv0=dv0, ac0=ac0):
                    wt(te, "te", "x%d" % bi, xsl_tick[bi])
                    wt(te, "te", "gk", MARK["wkg"])
                    ins = None
                    for ic in range(16):
                        ins = te.matmul(ps1[0:16, :], wkg1_sb[:, ic * 16:(ic + 1) * 16],
                                        xsl[bi][:, ic * 512:(ic + 1) * 512],
                                        start=(ic == 0), stop=(ic == 15))
                    ins.then_inc(SEM["pe"], 1)
                    tk("pe")
                    for ic2 in range(8):
                        wt(te, "te", "ac", ac0 + 1)
                        if ic2 >= 2:
                            wt(te, "te", "dv", dv0 + 6 * (ic2 - 2) + 1)
                            wt(te, "te", "ac", ac0 + 1 + 3 * (ic2 - 2) + 1)
                        te.matmul(ps2[ic2 % 2][:, :], wkg2_sb[0:16, ic2 * 128:(ic2 + 1) * 128],
                                  kg1b[:, :], start=True, stop=True).then_inc(SEM["pe"], 1)
                        tk("pe")

                @blk.scalar
                def _(ac, bi=bi, tt=tt, pe0=pe0, dv0=dv0, ac0=ac0):
                    wt(ac, "ac", "c1", MARK["consts2"])
                    wt(ac, "ac", "c0", MARK["consts"])
                    wt(ac, "ac", "pe", pe0 + 1)
                    ac.copy(kg1b[:, :], ps1[0:16, :]).then_inc(SEM["ac"], 1)
                    tk("ac")
                    for ic2 in range(8):
                        wt(ac, "ac", "pe", pe0 + 2 + ic2)
                        if ic2 >= 1:
                            wt(ac, "ac", "dv", dv0 + 6 * (ic2 - 1) + 2)
                        wt(ac, "ac", "dv", dv0 + 6 * ic2 + 1)
                        ac.activation(t_ab[:, :], ps2[ic2 % 2][:, :], AF.Abs,
                                      bias=bkg2_sb[:, ic2:ic2 + 1], scale=1.0).then_inc(SEM["ac"], 1)
                        tk("ac")
                        ac.activation(t_ab[:, :], t_ab[:, :], AF.Exp, scale=-1.0).then_inc(SEM["ac"], 1)
                        tk("ac")
                        ac.activation(t_ab[:, :], t_ab[:, :], AF.Ln, bias=fconst[:, 1:2]).then_inc(SEM["ac"], 1)
                        tk("ac")
                    for ic2 in range(8):
                        gcol = ic2 * T + tt * 512
                        for cs in range(4):
                            wt(ac, "ac", "dv", dv0 + 6 * ic2 + 2 + cs + 1)
                            gi = ic2 * 16 + tt * 4 + cs
                            ac.activation(gam[:, gi:gi + 1],
                                          Gbuf[:, gcol + cs * 128 + 127: gcol + cs * 128 + 128],
                                          AF.Exp, scale=NORMI).then_inc(SEM["ac"], 1)
                            tk("ac")

                @blk.vector
                def _(v, bi=bi, tt=tt, pe0=pe0, dv0=dv0, ac0=ac0):
                    wt(v, "dv", "c1", MARK["consts2"])
                    for ic2 in range(8):
                        wt(v, "dv", "pe", pe0 + 2 + ic2)
                        v.tensor_scalar(t_mn[:, :], ps2[ic2 % 2][:, :], bkg2_sb[:, ic2:ic2 + 1],
                                        0.0, ALU.add, ALU.min).then_inc(SEM["dv"], 1)
                        tk("dv")
                        wt(v, "dv", "ac", ac0 + 1 + 3 * ic2 + 3)
                        v.tensor_sub(t_gk[:, :], t_mn[:, :], t_ab[:, :]).then_inc(SEM["dv"], 1)
                        tk("dv")
                        gcol = ic2 * T + tt * 512
                        for cs in range(4):
                            v.tensor_tensor_scan(
                                Gbuf[:, gcol + cs * 128: gcol + (cs + 1) * 128],
                                t_gk[:, cs * 128:(cs + 1) * 128], zer[:, :], 0.0,
                                ALU.add, ALU.add).then_inc(SEM["dv"], 1)
                            tk("dv")
        p1.close()

        nc.leave_named_scope("zz_p1", MARK["sc_p1"], True)
        MARK["sc_p23"] = nc.enter_named_scope("zz_p23", True)[0]
        def proj_qk(which):
            px = ExitStack()
            wsb = px.enter_context(nc.sbuf_tensor(f"w{which}_sb", [128, 16 * 1024], BF16))
            xsl2 = [px.enter_context(nc.sbuf_tensor(f"x{which}{i}", [128, 16 * 512], BF16)) for i in range(2)]
            eg = px.enter_context(nc.sbuf_tensor(f"eg_{which}", [128, 512], F32))
            ot = [px.enter_context(nc.sbuf_tensor(f"ot_{which}{i}", [128, 512], BF16)) for i in range(2)]
            kh = px.enter_context(nc.sbuf_tensor(f"kh_{which}", [128, 512], BF16))
            psp = [px.enter_context(nc.psum_tensor(f"psp_{which}{i}", [128, 512], F32)) for i in range(2)]
            pst = px.enter_context(nc.psum_tensor(f"pst_{which}", [128, 1024], BF16))
            wd = wqT if which == "q" else wkT
            sdram = qt_s if which == "q" else kt_s
            is_k = (which == "k")
            NDV = 6 if is_k else 1
            NPE = 5 if is_k else 1
            x2_tick = [0, 0]
            w_tick = [0]
            wsem = "gq" if which == "q" else "gkk"
            st_base = [CNT["s0"], CNT["s1"]]

            def st_sem(fc):
                return "s%d" % (fc % 2)

            def st_tick(tt, fc):
                i = tt * 4 + fc // 2
                if tt < 0 or fc < 0:
                    return 0
                return st_base[fc % 2] + (i + 1) * 16

            with nc.Block() as blk:
                @blk.gpsimd
                def _(g):
                    for ic in range(16):
                        dq(g, wsem, wsb[:, ic * 1024:(ic + 1) * 1024],
                           wd[ic * 128:(ic + 1) * 128, :])
                    w_tick[0] = CNT[wsem]
                    es = range(0, 3) if which == "q" else range(3, 6)
                    for e in es:
                        dq(g, "g1b", w1bf[e, :, :], w1_n[e, :, :])

                @blk.sync
                def _(sy):
                    for ic in range(16):
                        dq(sy, "x0", xsl2[0][:, ic * 512:(ic + 1) * 512],
                           xbf[ic * 128:(ic + 1) * 128, 0:512])
                    x2_tick[0] = CNT["x0"]

            for tt in range(NTT):
                with nc.Block() as blk:
                    bi = tt % 2
                    pe0, dv0, ac0 = CNT["pe"], CNT["dv"], CNT["ac"]

                    @blk.sync
                    def _(sy, bi=bi, tt=tt):
                        if tt + 1 < NTT:
                            nb = (tt + 1) % 2
                            for ic in range(16):
                                dq(sy, "x%d" % nb, xsl2[nb][:, ic * 512:(ic + 1) * 512],
                                   xbf[ic * 128:(ic + 1) * 128, (tt + 1) * 512:(tt + 2) * 512])
                            x2_tick[nb] = CNT["x%d" % nb]

                    @blk.tensor
                    def _(te, bi=bi, tt=tt, pe0=pe0, dv0=dv0, ac0=ac0):
                        wt(te, "te", "c0", MARK["consts"])
                        wt(te, "te", "x%d" % bi, x2_tick[bi])
                        wt(te, "te", wsem, w_tick[0])
                        for fc in range(8):
                            if fc >= 2:
                                wt(te, "te", "dv", dv0 + NDV * (fc - 2) + 1)
                            ins = None
                            for ic in range(16):
                                ins = te.matmul(psp[fc % 2][:, :],
                                                wsb[:, ic * 1024 + fc * 128: ic * 1024 + (fc + 1) * 128],
                                                xsl2[bi][:, ic * 512:(ic + 1) * 512],
                                                start=(ic == 0), stop=(ic == 15))
                            ins.then_inc(SEM["pe"], 1)
                            tk("pe")
                            if is_k:
                                if fc >= 1:
                                    wt(te, "te", "dv", dv0 + NDV * (fc - 1) + 6)
                                for cs in range(4):
                                    wt(te, "te", "dv", dv0 + NDV * fc + 2 + cs)
                                    te.transpose(pst[:, cs * 128:(cs + 1) * 128],
                                                 kh[:, cs * 128:(cs + 1) * 128],
                                                 identb[:, :]).then_inc(SEM["pe"], 1)
                                    tk("pe")

                    @blk.scalar
                    def _(ac, bi=bi, tt=tt, pe0=pe0, dv0=dv0, ac0=ac0):
                        for fc in range(8):
                            if fc >= 1:
                                wt(ac, "ac", "dv", dv0 + NDV * (fc - 1) + 1)
                            if which == "q":
                                ac.activation(eg[:, :], Gbuf[:, fc * T + tt * 512: fc * T + (tt + 1) * 512],
                                              AF.Exp, bias=fconst[:, 0:1], scale=NORMI).then_inc(SEM["ac"], 1)
                            else:
                                ac.activation(eg[:, :], Gbuf[:, fc * T + tt * 512: fc * T + (tt + 1) * 512],
                                              AF.Exp, scale=-NORMI).then_inc(SEM["ac"], 1)
                            tk("ac")
                            if fc >= 1:
                                dq(ac, st_sem(fc - 1),
                                   sdram[(fc - 1) * 128: fc * 128, tt * 512:(tt + 1) * 512],
                                   ot[(fc - 1) % 2][:, :])
                        wt(ac, "ac", "dv", dv0 + NDV * 7 + 1)
                        dq(ac, st_sem(7), sdram[7 * 128: 8 * 128, tt * 512:(tt + 1) * 512],
                           ot[1][:, :])

                    @blk.vector
                    def _(v, bi=bi, tt=tt, pe0=pe0, dv0=dv0, ac0=ac0):
                        for fc in range(8):
                            wt(v, "dv", "pe", pe0 + NPE * fc + 1)
                            wt(v, "dv", "ac", ac0 + fc + 1)
                            if tt * 8 + fc - 2 >= 0:
                                t2, f2 = divmod(tt * 8 + fc - 2, 8)
                                wt(v, "dv", st_sem(f2), st_tick(t2, f2))
                            ob = ot[fc % 2]
                            v.tensor_mul(ob[:, :], psp[fc % 2][:, :], eg[:, :]).then_inc(SEM["dv"], 1)
                            tk("dv")
                            if is_k:
                                if fc >= 1:
                                    wt(v, "dv", "pe", pe0 + NPE * (fc - 1) + 5)
                                for cs in range(4):
                                    v.tensor_scalar(kh[:, cs * 128:(cs + 1) * 128],
                                                    ob[:, cs * 128:(cs + 1) * 128],
                                                    gam[:, fc * 16 + tt * 4 + cs: fc * 16 + tt * 4 + cs + 1],
                                                    None, ALU.mult).then_inc(SEM["dv"], 1)
                                    tk("dv")
                                wt(v, "dv", "pe", pe0 + NPE * fc + 5)
                                ins = None
                                for cs in range(4):
                                    ch = tt * 4 + cs
                                    ins = v.tensor_copy(
                                        khT[:, ch * 1024 + fc * 128: ch * 1024 + (fc + 1) * 128],
                                        pst[:, cs * 128:(cs + 1) * 128])
                                ins.then_inc(SEM["dv"], 1)
                                tk("dv")
            px.close()

        if limit >= 2:
            proj_qk("q")
        if limit >= 3:
            proj_qk("k")
        sa2.close()

        nc.leave_named_scope("zz_p23", MARK["sc_p23"], True)
        MARK["sc_p4"] = nc.enter_named_scope("zz_p4", True)[0]
        sb2 = ExitStack()
        v_sb = sb2.enter_context(nc.sbuf_tensor("v_sb", [128, NCH * 2048], BF16))
        p4 = ExitStack()
        wv_sb = p4.enter_context(nc.sbuf_tensor("wv_sb", [128, 16 * 2048], BF16))
        xc4 = p4.enter_context(nc.sbuf_tensor("xc4", [128, 16 * 512], BF16))
        psv = [p4.enter_context(nc.psum_tensor(f"psv{i}", [128, 512], F32)) for i in range(2)]
        pss = [p4.enter_context(nc.psum_tensor(f"pss{i}", [128, 512], F32)) for i in range(2)]
        wv_tick = [0]
        xc4_tick = [0]

        if limit >= 4:
          with nc.Block() as blk:
            @blk.gpsimd
            def _(g):
                for ic in range(16):
                    dq(g, "gv", wv_sb[:, ic * 2048:(ic + 1) * 2048],
                       wvT[ic * 128:(ic + 1) * 128, :])
                wv_tick[0] = CNT["gv"]
                for e in range(6, 8):
                    dq(g, "g1b", w1bf[e, :, :], w1_n[e, :, :])
                MARK["w1bf"] = CNT["g1b"]

        for ch in range(NCH if limit >= 4 else 0):
            with nc.Block() as blk:
                pe0, dv0, ac0 = CNT["pe"], CNT["dv"], CNT["ac"]

                @blk.sync
                def _(sy, ch=ch):
                    if ch % 4 == 0:
                        sem = "x%d" % ((ch // 4) % 2)
                        for ic in range(16):
                            dq(sy, sem, xc4[:, ic * 512:(ic + 1) * 512],
                               xbf[ic * 128:(ic + 1) * 128, ch * 128:(ch + 4) * 128])
                        xc4_tick[0] = CNT[sem]

                @blk.tensor
                def _(te, ch=ch, pe0=pe0, dv0=dv0, ac0=ac0):
                    wt(te, "te", "x%d" % ((ch // 4) % 2), xc4_tick[0])
                    wt(te, "te", "gv", wv_tick[0])
                    co = (ch % 4) * 128
                    for jt in range(4):
                        if jt >= 2:
                            wt(te, "te", "ac", ac0 + jt - 1)
                        ins = None
                        for ic in range(16):
                            ins = te.matmul(psv[jt % 2][:, :],
                                            xc4[:, ic * 512 + co: ic * 512 + co + 128],
                                            wv_sb[:, ic * 2048 + jt * 512: ic * 2048 + (jt + 1) * 512],
                                            start=(ic == 0), stop=(ic == 15))
                        ins.then_inc(SEM["pe"], 1)
                        tk("pe")
                    for h in range(NH):
                        wt(te, "te", "ac", ac0 + h // 4 + 1)
                        if h >= 2:
                            wt(te, "te", "dv", dv0 + 2 * (h - 2) + 2)
                        pb = (h % 2) * 64
                        kcol = ch * 1024 + (h // 2) * 128 + pb
                        te.matmul(pss[h % 2][pb:pb + 64, 0:128],
                                  khT[:, kcol: kcol + 64],
                                  v_sb[:, ch * 2048 + h * 128: ch * 2048 + (h + 1) * 128],
                                  start=True, stop=True).then_inc(SEM["pe"], 1)
                        tk("pe")

                @blk.scalar
                def _(ac, ch=ch, pe0=pe0, dv0=dv0, ac0=ac0):
                    for jt in range(4):
                        wt(ac, "ac", "pe", pe0 + jt + 1)
                        ac.copy(v_sb[:, ch * 2048 + jt * 512: ch * 2048 + (jt + 1) * 512],
                                psv[jt % 2][:, :]).then_inc(SEM["ac"], 1)
                        tk("ac")

                @blk.vector
                def _(v, ch=ch, pe0=pe0, dv0=dv0, ac0=ac0):
                    for h in range(NH):
                        wt(v, "dv", "pe", pe0 + 4 + h + 1)
                        pb = (h % 2) * 64
                        rs = slice(pb, pb + 64)
                        cs_ = slice((h // 2) * 128, (h // 2) * 128 + 128)
                        gcol = (h // 2) * 16 + ch
                        v.tensor_scalar(S_buf[rs, cs_], S_buf[rs, cs_],
                                        gam[rs, gcol:gcol + 1], None, ALU.mult).then_inc(SEM["dv"], 1)
                        tk("dv")
                        v.tensor_add(S_buf[rs, cs_], S_buf[rs, cs_],
                                     pss[h % 2][rs, 0:128]).then_inc(SEM["dv"], 1)
                        tk("dv")

        nc.leave_named_scope("zz_p4", MARK["sc_p4"], True)
        MARK["sc_hand"] = nc.enter_named_scope("zz_hand", True)[0]
        if limit >= 5:
          with nc.Block() as blk:
            @blk.scalar
            def _(ac):
                MARK["contrib"] = dq(ac, "sc0", st_contrib[:, :], S_buf[:, :])

            @blk.gpsimd
            def _(g):
                wt(g, "gp", "sc0", MARK["contrib"])
                g.collective_compute(
                    "AllGather", ALU.bypass,
                    replica_groups=[[0, 1, 2, 3, 4, 5, 6, 7]],
                    ins=[st_contrib.ap().opt()],
                    outs=[st_gather.ap().opt()],
                ).then_inc(SEM["cc"])
                tk("cc")
                g.wait_ge(SEM["cc"], CNT["cc"])
                g.indirect_dma_start(
                    S_init[:, :], None,
                    st_gather[0:128, :],
                    bass.IndirectOffsetOnAxis(idx_sb[:, 0:1], axis=0),
                ).then_inc(SEM["gi"], 16)
                MARK["sinit"] = tk("gi", 16)

            @blk.vector
            def _(v):
                wt(v, "dv", "gi", MARK["sinit"])
                v.tensor_scalar(S_buf[:, :], S_init[:, :], maskS_sb[:, 0:1],
                                None, ALU.mult).then_inc(SEM["dv"], 1)
                MARK["smask"] = tk("dv")

            @blk.scalar
            def _(ac2):
                wt(ac2, "ac", "dv", MARK["smask"])
                ac2.copy(S_bf[0][:, :], S_buf[:, :]).then_inc(SEM["ac"], 1)
                tk("ac")
        p4.close()

        nc.leave_named_scope("zz_hand", MARK["sc_hand"], True)
        MARK["sc_p5"] = nc.enter_named_scope("zz_p5", True)[0]
        p5 = ExitStack()
        qc4 = p5.enter_context(nc.sbuf_tensor("qc4", [128, 8 * 512], BF16))
        kc4 = p5.enter_context(nc.sbuf_tensor("kc4", [128, 8 * 512], BF16))
        Am = [p5.enter_context(nc.sbuf_tensor(f"Am{i}", [128, 128], BF16)) for i in range(2)]
        sqO = [p5.enter_context(nc.sbuf_tensor(f"sqO{i}", [128, 128], F32)) for i in range(2)]
        on_t = [p5.enter_context(nc.sbuf_tensor(f"on_t{i}", [128, 2048], BF16)) for i in range(2)]
        r1c = p5.enter_context(nc.sbuf_tensor("r1c", [128, 1], F32))
        r2c = p5.enter_context(nc.sbuf_tensor("r2c", [128, 1], F32))
        muc = p5.enter_context(nc.sbuf_tensor("muc", [128, 1], F32))
        mu2c = p5.enter_context(nc.sbuf_tensor("mu2c", [128, 1], F32))
        varc = p5.enter_context(nc.sbuf_tensor("varc", [128, 1], F32))
        sdc = p5.enter_context(nc.sbuf_tensor("sdc", [128, 1], F32))
        rsc = p5.enter_context(nc.sbuf_tensor("rsc", [128, 1], F32))
        psA = [p5.enter_context(nc.psum_tensor(f"psA{i}", [128, 512], F32)) for i in range(2)]
        psO = [p5.enter_context(nc.psum_tensor(f"psO{i}", [128, 512], F32)) for i in range(2)]
        psS = [p5.enter_context(nc.psum_tensor(f"psS{i}", [128, 512], F32)) for i in range(2)]
        qk_tick = [0]
        onst_base = [CNT["so0"], CNT["so1"]]
        qkst_done = [CNT["s0"], CNT["s1"]]

        for ch in range(NCH if limit >= 6 else 0):
            with nc.Block() as blk:
                pe0, dv0, ac0 = CNT["pe"], CNT["dv"], CNT["ac"]

                if ch == 0:
                    @blk.gpsimd
                    def _(g):
                        for e in range(8):
                            dq(g, "g2b", w2bf[e, :, :], w2r[e, :, :])
                        dq(g, "g2b", sw1bf[:, :], sw1_n[:, :])
                        for e in range(8):
                            dq(g, "g2b", sw2bf[e, :, :], sw2r[e, :, :])
                        MARK["wbf2"] = CNT["g2b"]

                @blk.sync
                def _(sy, ch=ch):
                    if ch % 4 == 0:
                        wt(sy, "sy", "s0", qkst_done[0])
                        wt(sy, "sy", "s1", qkst_done[1])
                        sem = "q%d" % ((ch // 4) % 2)
                        for fc in range(8):
                            dq(sy, sem, qc4[:, fc * 512:(fc + 1) * 512],
                               qt_s[fc * 128:(fc + 1) * 128, ch * 128:(ch + 4) * 128])
                            dq(sy, sem, kc4[:, fc * 512:(fc + 1) * 512],
                               kt_s[fc * 128:(fc + 1) * 128, ch * 128:(ch + 4) * 128])
                        qk_tick[0] = CNT[sem]

                @blk.tensor
                def _(te, ch=ch, pe0=pe0, dv0=dv0, ac0=ac0):
                    wt(te, "te", "q%d" % ((ch // 4) % 2), qk_tick[0])
                    co = (ch % 4) * 128
                    for h in range(NH):
                        pb = (h % 2) * 64
                        qcol = (h // 2) * 512 + co
                        if h >= 2:
                            wt(te, "te", "dv", dv0 + 9 * (h - 2) + 1)
                        te.matmul(psA[h % 2][:, 0:128],
                                  kc4[pb:pb + 64, qcol:qcol + 128],
                                  qc4[pb:pb + 64, qcol:qcol + 128],
                                  start=True, stop=True).then_inc(SEM["pe"], 1)
                        tk("pe")
                        wt(te, "te", "dv", dv0 + 9 * h + 1)
                        if h >= 2:
                            wt(te, "te", "dv", dv0 + 9 * (h - 2) + 9)
                            wt(te, "te", "ac", ac0 + 3 * (h - 2) + 1)
                        te.matmul(psO[h % 2][:, 0:128], Am[h % 2][:, :],
                                  v_sb[:, ch * 2048 + h * 128: ch * 2048 + (h + 1) * 128],
                                  start=True, stop=False)
                        te.matmul(psO[h % 2][:, 0:128],
                                  qc4[pb:pb + 64, qcol:qcol + 128],
                                  S_bf[ch % 2][pb:pb + 64, (h // 2) * 128:(h // 2) * 128 + 128],
                                  start=False, stop=True).then_inc(SEM["pe"], 1)
                        tk("pe")
                        if h >= 2:
                            wt(te, "te", "dv", dv0 + 9 * (h - 2) + 3)
                        kcol = ch * 1024 + (h // 2) * 128 + pb
                        te.matmul(psS[h % 2][pb:pb + 64, 0:128],
                                  khT[:, kcol:kcol + 64],
                                  v_sb[:, ch * 2048 + h * 128: ch * 2048 + (h + 1) * 128],
                                  start=True, stop=True).then_inc(SEM["pe"], 1)
                        tk("pe")

                @blk.vector
                def _(v, ch=ch, pe0=pe0, dv0=dv0, ac0=ac0):
                    for h in range(NH):
                        pb = (h % 2) * 64
                        rs = slice(pb, pb + 64)
                        cs_ = slice((h // 2) * 128, (h // 2) * 128 + 128)
                        wt(v, "dv", "pe", pe0 + 3 * h + 1)
                        v.tensor_mul(Am[h % 2][:, :], psA[h % 2][:, 0:128], maskut[:, :]
                                     ).then_inc(SEM["dv"], 1)
                        tk("dv")
                        wt(v, "dv", "pe", pe0 + 3 * h + 3)
                        gcol = (h // 2) * 16 + ch
                        v.tensor_scalar(S_buf[rs, cs_], S_buf[rs, cs_],
                                        gam[rs, gcol:gcol + 1], None, ALU.mult).then_inc(SEM["dv"], 1)
                        tk("dv")
                        v.tensor_add(S_buf[rs, cs_], S_buf[rs, cs_],
                                     psS[h % 2][rs, 0:128]).then_inc(SEM["dv"], 1)
                        tk("dv")
                        wt(v, "dv", "pe", pe0 + 3 * h + 2)
                        v.reduce_sum(r1c[:, :], psO[h % 2][:, 0:128], axis=AX.X).then_inc(SEM["dv"], 1)
                        tk("dv")
                        wt(v, "dv", "ac", ac0 + 3 * h + 1)
                        v.reduce_sum(r2c[:, :], sqO[h % 2][:, :], axis=AX.X).then_inc(SEM["dv"], 1)
                        tk("dv")
                        v.tensor_scalar(muc[:, :], r1c[:, :], 1.0 / 128.0, None, ALU.mult
                                        ).then_inc(SEM["dv"], 1)
                        tk("dv")
                        wt(v, "dv", "ac", ac0 + 3 * h + 2)
                        v.tensor_scalar(varc[:, :], r2c[:, :], 1.0 / 128.0, mu2c[:, 0:1],
                                        ALU.mult, ALU.subtract).then_inc(SEM["dv"], 1)
                        tk("dv")
                        wt(v, "dv", "ac", ac0 + 3 * h + 3)
                        v.reciprocal(rsc[:, :], sdc[:, :]).then_inc(SEM["dv"], 1)
                        tk("dv")
                        if ch >= 2 and h == 0:
                            wt(v, "dv", "so%d" % (ch % 2),
                               onst_base[ch % 2] + (ch // 2) * 16)
                        v.tensor_scalar(on_t[ch % 2][:, h * 128:(h + 1) * 128],
                                        psO[h % 2][:, 0:128], muc[:, 0:1], rsc[:, 0:1],
                                        ALU.subtract, ALU.mult).then_inc(SEM["dv"], 1)
                        tk("dv")

                @blk.scalar
                def _(ac, ch=ch, pe0=pe0, dv0=dv0, ac0=ac0):
                    for h in range(NH):
                        wt(ac, "ac", "pe", pe0 + 3 * h + 2)
                        if h >= 2:
                            wt(ac, "ac", "dv", dv0 + 9 * (h - 2) + 5)
                        wt(ac, "ac", "dv", dv0 + 9 * h + 4)
                        ac.activation(sqO[h % 2][:, :], psO[h % 2][:, 0:128], AF.Square
                                      ).then_inc(SEM["ac"], 1)
                        tk("ac")
                        wt(ac, "ac", "dv", dv0 + 9 * h + 6)
                        ac.activation(mu2c[:, :], muc[:, :], AF.Square).then_inc(SEM["ac"], 1)
                        tk("ac")
                        wt(ac, "ac", "dv", dv0 + 9 * h + 7)
                        ac.activation(sdc[:, :], varc[:, :], AF.Sqrt, bias=fconst[:, 2:3]).then_inc(SEM["ac"], 1)
                        tk("ac")
                    wt(ac, "ac", "dv", dv0 + 9 * 15 + 3)
                    ac.copy(S_bf[(ch + 1) % 2][:, :], S_buf[:, :]).then_inc(SEM["ac"], 1)
                    tk("ac")
                    wt(ac, "ac", "dv", dv0 + 9 * 15 + 9)
                    dq(ac, "so%d" % (ch % 2), onorm_s[ch * 128:(ch + 1) * 128, :],
                       on_t[ch % 2][:, :])
        p5.close()
        sb2.close()
        sab.close()

        nc.leave_named_scope("zz_p5", MARK["sc_p5"], True)
        MARK["sc_p6"] = nc.enter_named_scope("zz_p6", True)[0]
        sc2 = ExitStack()
        pre_sb = sc2.enter_context(nc.sbuf_tensor("pre_sb", [128, 16 * 2048], BF16))
        p6 = ExitStack()
        wg_sb = p6.enter_context(nc.sbuf_tensor("wg_sb", [128, 16 * 2048], BF16))
        xsl6 = p6.enter_context(nc.sbuf_tensor("xsl6", [128, 16 * 512], BF16))
        bgb = p6.enter_context(nc.sbuf_tensor("bgb", [128, 2048], F32))
        bg_sb = p6.enter_context(nc.sbuf_tensor("bg_sb", [1, 2048], F32))
        oc2 = [p6.enter_context(nc.sbuf_tensor(f"oc2_{i}", [128, 2048], BF16)) for i in range(2)]
        tmp6 = [p6.enter_context(nc.sbuf_tensor(f"tmp6_{i}", [128, 512], F32)) for i in range(2)]
        sil6 = [p6.enter_context(nc.sbuf_tensor(f"sil6_{i}", [128, 512], F32)) for i in range(2)]
        ptm = [p6.enter_context(nc.sbuf_tensor(f"ptm{i}", [128, 512], BF16)) for i in range(2)]
        pg6 = [p6.enter_context(nc.psum_tensor(f"pg6_{i}", [128, 512], F32)) for i in range(2)]
        ptp = [p6.enter_context(nc.psum_tensor(f"ptp{i}", [128, 1024], BF16)) for i in range(2)]
        pbg = p6.enter_context(nc.psum_tensor("pbg", [128, 512], F32))
        wg_tick = [0]
        x6_tick = [0]
        oc_tick = [0, 0]

        if limit >= 7:
          with nc.Block() as blk:
            @blk.gpsimd
            def _(g):
                for ic in range(16):
                    dq(g, "gg", wg_sb[:, ic * 2048:(ic + 1) * 2048],
                       wgT[ic * 128:(ic + 1) * 128, :])
                wg_tick[0] = CNT["gg"]

            @blk.sync
            def _(sy):
                wt(sy, "sy", "so0", CNT["so0"])
                wt(sy, "sy", "so1", CNT["so1"])
                MARK["bg"] = dq(sy, "c0", bg_sb[:, :], bg_row[:, :])
                oc_tick[0] = dq(sy, "o0", oc2[0][:, :], onorm_s[0:128, :])

            pe_b, dv_b = CNT["pe"], CNT["dv"]

            @blk.tensor
            def _(te, pe_b=pe_b, dv_b=dv_b):
                wt(te, "te", "c0", MARK["bg"])
                for j4 in range(4):
                    if j4 >= 1:
                        wt(te, "te", "dv", dv_b + j4)
                    te.matmul(pbg[:, :], onesr[0:1, :], bg_sb[0:1, j4 * 512:(j4 + 1) * 512],
                              start=True, stop=True).then_inc(SEM["pe"], 1)
                    tk("pe")

            @blk.vector
            def _(v, pe_b=pe_b, dv_b=dv_b):
                for j4 in range(4):
                    wt(v, "dv", "pe", pe_b + j4 + 1)
                    v.tensor_copy(bgb[:, j4 * 512:(j4 + 1) * 512], pbg[:, :]).then_inc(SEM["dv"], 1)
                    tk("dv")

        for tci in range(16 if limit >= 7 else 0):
            with nc.Block() as blk:
                pe0, dv0, ac0 = CNT["pe"], CNT["dv"], CNT["ac"]

                @blk.sync
                def _(sy, tci=tci):
                    if tci % 4 == 0:
                        sem = "x%d" % ((tci // 4) % 2)
                        for ic in range(16):
                            dq(sy, sem, xsl6[:, ic * 512:(ic + 1) * 512],
                               xbf[ic * 128:(ic + 1) * 128, (tci // 4) * 512:(tci // 4 + 1) * 512])
                        x6_tick[0] = CNT[sem]
                    if tci + 1 < 16:
                        oc_tick[(tci + 1) % 2] = dq(
                            sy, "o%d" % ((tci + 1) % 2),
                            oc2[(tci + 1) % 2][:, :],
                            onorm_s[(tci + 1) * 128:(tci + 2) * 128, :])

                @blk.tensor
                def _(te, tci=tci, pe0=pe0, dv0=dv0, ac0=ac0):
                    wt(te, "te", "x%d" % ((tci // 4) % 2), x6_tick[0])
                    wt(te, "te", "gg", wg_tick[0])
                    co = (tci % 4) * 128
                    for ft in range(4):
                        if ft >= 2:
                            wt(te, "te", "dv", dv0 + 6 * (ft - 2) + 1)
                        ins = None
                        for ic in range(16):
                            ins = te.matmul(pg6[ft % 2][:, :],
                                            xsl6[:, ic * 512 + co: ic * 512 + co + 128],
                                            wg_sb[:, ic * 2048 + ft * 512: ic * 2048 + (ft + 1) * 512],
                                            start=(ic == 0), stop=(ic == 15))
                        ins.then_inc(SEM["pe"], 1)
                        tk("pe")
                        wt(te, "te", "dv", dv0 + 6 * ft + 2)
                        if ft >= 2:
                            wt(te, "te", "dv", dv0 + 6 * (ft - 2) + 6)
                        for sub in range(4):
                            te.transpose(ptp[ft % 2][:, sub * 128:(sub + 1) * 128],
                                         ptm[ft % 2][:, sub * 128:(sub + 1) * 128],
                                         identb[:, :]).then_inc(SEM["pe"], 1)
                            tk("pe")

                @blk.vector
                def _(v, tci=tci, pe0=pe0, dv0=dv0, ac0=ac0):
                    for ft in range(4):
                        wt(v, "dv", "pe", pe0 + 5 * ft + 1)
                        if ft >= 2:
                            wt(v, "dv", "ac", ac0 + ft - 1)
                        v.tensor_add(tmp6[ft % 2][:, :], pg6[ft % 2][:, :],
                                     bgb[:, ft * 512:(ft + 1) * 512]).then_inc(SEM["dv"], 1)
                        tk("dv")
                        wt(v, "dv", "ac", ac0 + ft + 1)
                        wt(v, "dv", "o%d" % (tci % 2), oc_tick[tci % 2])
                        if ft >= 2:
                            wt(v, "dv", "pe", pe0 + 5 * (ft - 2) + 5)
                        v.tensor_mul(ptm[ft % 2][:, :], sil6[ft % 2][:, :],
                                     oc2[tci % 2][:, ft * 512:(ft + 1) * 512]).then_inc(SEM["dv"], 1)
                        tk("dv")
                        wt(v, "dv", "pe", pe0 + 5 * ft + 5)
                        for sub in range(4):
                            fcg = ft * 4 + sub
                            v.tensor_copy(
                                pre_sb[:, fcg * 2048 + tci * 128: fcg * 2048 + (tci + 1) * 128],
                                ptp[ft % 2][:, sub * 128:(sub + 1) * 128]).then_inc(SEM["dv"], 1)
                            tk("dv")

                @blk.scalar
                def _(ac, tci=tci, pe0=pe0, dv0=dv0, ac0=ac0):
                    for ft in range(4):
                        wt(ac, "ac", "dv", dv0 + 6 * ft + 1)
                        if ft >= 2:
                            wt(ac, "ac", "dv", dv0 + 6 * (ft - 2) + 2)
                        ac.activation(sil6[ft % 2][:, :], tmp6[ft % 2][:, :], AF.Silu
                                      ).then_inc(SEM["ac"], 1)
                        tk("ac")
        p6.close()

        nc.leave_named_scope("zz_p6", MARK["sc_p6"], True)
        MARK["sc_p7"] = nc.enter_named_scope("zz_p7", True)[0]
        p7 = ExitStack()
        wo_sb = p7.enter_context(nc.sbuf_tensor("wo_sb", [128, 16 * 2048], BF16))
        r1b = p7.enter_context(nc.sbuf_tensor("r1b", [128, 16 * 512], F32))
        xr = [p7.enter_context(nc.sbuf_tensor(f"xr{i}", [128, 512], F32)) for i in range(2)]
        sq7 = [p7.enter_context(nc.sbuf_tensor(f"sq7_{i}", [128, 512], F32)) for i in range(2)]
        x1t = [p7.enter_context(nc.sbuf_tensor(f"x1t{i}", [128, 512], F32)) for i in range(2)]
        t7 = p7.enter_context(nc.sbuf_tensor("t7", [128, 512], F32))
        mu_row = p7.enter_context(nc.sbuf_tensor("mu_row", [1, 512], F32))
        msq_row = p7.enter_context(nc.sbuf_tensor("msq_row", [1, 512], F32))
        mu2_row = p7.enter_context(nc.sbuf_tensor("mu2_row", [1, 512], F32))
        var_row = p7.enter_context(nc.sbuf_tensor("var_row", [1, 512], F32))
        sd_row = p7.enter_context(nc.sbuf_tensor("sd_row", [1, 512], F32))
        rs_row = p7.enter_context(nc.sbuf_tensor("rs_row", [1, 512], F32))
        statacc1 = p7.enter_context(nc.sbuf_tensor("statacc1", [1, 512], F32))
        statacc2 = p7.enter_context(nc.sbuf_tensor("statacc2", [1, 512], F32))
        pw7 = [p7.enter_context(nc.psum_tensor(f"pw7_{i}", [128, 512], F32)) for i in range(3)]
        pstat = [p7.enter_context(nc.psum_tensor(f"pstat{i}", [128, 512], F32)) for i in range(2)]
        pbc = [p7.enter_context(nc.psum_tensor(f"pbc{i}", [128, 512], F32)) for i in range(2)]
        wo_tick = [0]

        if limit >= 8:
          with nc.Block() as blk:
            @blk.gpsimd
            def _(g):
                for ic in range(16):
                    dq(g, "go", wo_sb[:, ic * 2048:(ic + 1) * 2048],
                       woT[ic * 128:(ic + 1) * 128, :])
                wo_tick[0] = CNT["go"]

        for tt in range(NTT if limit >= 8 else 0):
            with nc.Block() as blk:
                pe0, dv0, ac0 = CNT["pe"], CNT["dv"], CNT["ac"]
                xr_b = [CNT["xr0"], CNT["xr1"]]
                xs_b = [CNT["xs0"], CNT["xs1"]]
                te_seq = []
                for fo in range(16):
                    te_seq.append(("W", fo))
                    if fo >= 2:
                        te_seq.append(("r1", fo - 2))
                        te_seq.append(("sq", fo - 2))
                for j in (14, 15):
                    te_seq.append(("r1", j))
                    te_seq.append(("sq", j))
                te_seq += [("bc", 0), ("bc", 1)]
                pe_t = {ev: pe0 + i + 1 for i, ev in enumerate(te_seq)}
                dv_seq = [("ms", 0), ("ms", 1)]
                for fo in range(16):
                    dv_seq += [("ev", fo), ("a1", fo), ("a2", fo)]
                dv_seq += [("mu",), ("msq",), ("var",), ("rec",)]
                for fo in range(16):
                    dv_seq += [("sub", fo), ("mul", fo)]
                dv_t = {ev: dv0 + i + 1 for i, ev in enumerate(dv_seq)}

                @blk.sync
                def _(sy, tt=tt, dv_t=dv_t):
                    for fo in range(16):
                        if fo >= 2:
                            wt(sy, "sy", "dv", dv_t[("ev", fo - 2)])
                        dq(sy, "xr%d" % (fo % 2), xr[fo % 2][:, :],
                           x_fm[fo * 128:(fo + 1) * 128, tt * 512:(tt + 1) * 512])

                @blk.tensor
                def _(te, tt=tt, pe_t=pe_t, dv_t=dv_t, ac0=ac0):
                    wt(te, "te", "go", wo_tick[0])

                    def stat_one(j):
                        wt(te, "te", "dv", dv_t[("ev", j)])
                        if j >= 1:
                            wt(te, "te", "dv", dv_t[("a1", j - 1)])
                        te.matmul(pstat[0][0:1, :], onesc[:, 0:1],
                                  r1b[:, j * 512:(j + 1) * 512],
                                  start=True, stop=True).then_inc(SEM["pe"], 1)
                        tk("pe")
                        wt(te, "te", "ac", ac0 + j + 1)
                        if j >= 1:
                            wt(te, "te", "dv", dv_t[("a2", j - 1)])
                        te.matmul(pstat[1][0:1, :], onesc[:, 0:1],
                                  sq7[j % 2][:, :],
                                  start=True, stop=True).then_inc(SEM["pe"], 1)
                        tk("pe")

                    for fo in range(16):
                        if fo >= 3:
                            wt(te, "te", "dv", dv_t[("ev", fo - 3)])
                        ins = None
                        for ic in range(16):
                            ins = te.matmul(pw7[fo % 3][:, :],
                                            wo_sb[:, ic * 2048 + fo * 128: ic * 2048 + (fo + 1) * 128],
                                            pre_sb[:, ic * 2048 + tt * 512: ic * 2048 + (tt + 1) * 512],
                                            start=(ic == 0), stop=(ic == 15))
                        ins.then_inc(SEM["pe"], 1)
                        tk("pe")
                        if fo >= 2:
                            stat_one(fo - 2)
                    for j in (14, 15):
                        stat_one(j)
                    wt(te, "te", "dv", dv_t[("mu",)])
                    te.matmul(pbc[0][:, :], onesr[0:1, :], mu_row[0:1, :],
                              start=True, stop=True).then_inc(SEM["pe"], 1)
                    tk("pe")
                    wt(te, "te", "dv", dv_t[("rec",)])
                    te.matmul(pbc[1][:, :], onesr[0:1, :], rs_row[0:1, :],
                              start=True, stop=True).then_inc(SEM["pe"], 1)
                    tk("pe")

                @blk.vector
                def _(v, tt=tt, pe_t=pe_t, dv_t=dv_t, ac0=ac0, xr_b=xr_b, xs_b=xs_b):
                    v.memset(statacc1[:, :], 0.0).then_inc(SEM["dv"], 1)
                    tk("dv")
                    v.memset(statacc2[:, :], 0.0).then_inc(SEM["dv"], 1)
                    tk("dv")
                    for fo in range(16):
                        wt(v, "dv", "pe", pe_t[("W", fo)])
                        wt(v, "dv", "xr%d" % (fo % 2), xr_b[fo % 2] + (fo // 2 + 1) * 16)
                        v.tensor_add(r1b[:, fo * 512:(fo + 1) * 512], pw7[fo % 3][:, :],
                                     xr[fo % 2][:, :]).then_inc(SEM["dv"], 1)
                        tk("dv")
                        wt(v, "dv", "pe", pe_t[("r1", fo)])
                        v.tensor_add(statacc1[:, :], statacc1[:, :], pstat[0][0:1, :]
                                     ).then_inc(SEM["dv"], 1)
                        tk("dv")
                        wt(v, "dv", "pe", pe_t[("sq", fo)])
                        v.tensor_add(statacc2[:, :], statacc2[:, :], pstat[1][0:1, :]
                                     ).then_inc(SEM["dv"], 1)
                        tk("dv")
                    v.tensor_scalar(mu_row[:, :], statacc1[:, :], 1.0 / 2048.0, None,
                                    ALU.mult).then_inc(SEM["dv"], 1)
                    tk("dv")
                    v.tensor_scalar(msq_row[:, :], statacc2[:, :], 1.0 / 2048.0, None,
                                    ALU.mult).then_inc(SEM["dv"], 1)
                    tk("dv")
                    wt(v, "dv", "ac", ac0 + 17)
                    v.tensor_sub(var_row[:, :], msq_row[:, :], mu2_row[:, :]).then_inc(SEM["dv"], 1)
                    tk("dv")
                    wt(v, "dv", "ac", ac0 + 18)
                    v.reciprocal(rs_row[:, :], sd_row[:, :]).then_inc(SEM["dv"], 1)
                    tk("dv")
                    wt(v, "dv", "pe", pe_t[("bc", 1)])
                    for fo in range(16):
                        v.tensor_sub(t7[:, :], r1b[:, fo * 512:(fo + 1) * 512], pbc[0][:, :]
                                     ).then_inc(SEM["dv"], 1)
                        tk("dv")
                        if fo >= 2:
                            wt(v, "dv", "xs%d" % (fo % 2),
                               xs_b[fo % 2] + (fo // 2) * 16)
                        v.tensor_mul(x1t[fo % 2][:, :], t7[:, :], pbc[1][:, :]).then_inc(SEM["dv"], 1)
                        tk("dv")

                @blk.scalar
                def _(ac, tt=tt, pe_t=pe_t, dv_t=dv_t, ac0=ac0):
                    for fo in range(16):
                        wt(ac, "ac", "dv", dv_t[("ev", fo)])
                        if fo >= 2:
                            wt(ac, "ac", "pe", pe_t[("sq", fo - 2)])
                        ac.activation(sq7[fo % 2][:, :], r1b[:, fo * 512:(fo + 1) * 512],
                                      AF.Square).then_inc(SEM["ac"], 1)
                        tk("ac")
                    wt(ac, "ac", "dv", dv_t[("mu",)])
                    ac.activation(mu2_row[:, :], mu_row[:, :], AF.Square).then_inc(SEM["ac"], 1)
                    tk("ac")
                    wt(ac, "ac", "dv", dv_t[("var",)])
                    ac.activation(sd_row[:, :], var_row[:, :], AF.Sqrt, bias=fconst[0:1, 2:3]).then_inc(SEM["ac"], 1)
                    tk("ac")
                    for fo in range(16):
                        wt(ac, "ac", "dv", dv_t[("mul", fo)])
                        dq(ac, "xs%d" % (fo % 2),
                           x1f_s[fo * 128:(fo + 1) * 128, tt * 512:(tt + 1) * 512],
                           x1t[fo % 2][:, :])
        p7.close()
        sc2.close()

        nc.leave_named_scope("zz_p7", MARK["sc_p7"], True)
        MARK["sc_p8"] = nc.enter_named_scope("zz_p8", True)[0]
        p8 = ExitStack()
        hbuf = p8.enter_context(nc.sbuf_tensor("hbuf", [128, 64 * 512], BF16))
        shb = p8.enter_context(nc.sbuf_tensor("shb", [128, 16 * 512], BF16))
        x1bt = p8.enter_context(nc.sbuf_tensor("x1bt", [128, 16 * 512], BF16))
        r2b = p8.enter_context(nc.sbuf_tensor("r2b", [128, 16 * 512], F32))
        wub = p8.enter_context(nc.sbuf_tensor("wub", [128, 20480], BF16))
        gate_sb = p8.enter_context(nc.sbuf_tensor("gate_sb", [128, 128], BF16))
        b1_sb = p8.enter_context(nc.sbuf_tensor("b1_sb", [128, 64], F32))
        b2_sb = p8.enter_context(nc.sbuf_tensor("b2_sb", [8, 2048], BF16))
        sb1_sb = p8.enter_context(nc.sbuf_tensor("sb1_sb", [128, 16], F32))
        sb2_sb = p8.enter_context(nc.sbuf_tensor("sb2_sb", [128, 16], F32))
        oh_sb = p8.enter_context(nc.sbuf_tensor("oh_sb", [8, 1024], F32))
        lg4 = p8.enter_context(nc.sbuf_tensor("lg4", [128, 32], F32))
        eqa = p8.enter_context(nc.sbuf_tensor("eqa", [128, 32], F32))
        cum4 = p8.enter_context(nc.sbuf_tensor("cum4", [128, 32], F32))
        c4 = p8.enter_context(nc.sbuf_tensor("c4", [128, 32], F32))
        sel14 = p8.enter_context(nc.sbuf_tensor("sel14", [128, 32], F32))
        sel24 = p8.enter_context(nc.sbuf_tensor("sel24", [128, 32], F32))
        lg24 = p8.enter_context(nc.sbuf_tensor("lg24", [128, 32], F32))
        m1c4 = p8.enter_context(nc.sbuf_tensor("m1c4", [128, 4], F32))
        m2c4 = p8.enter_context(nc.sbuf_tensor("m2c4", [128, 4], F32))
        dcc4 = p8.enter_context(nc.sbuf_tensor("dcc4", [128, 4], F32))
        g1c4 = p8.enter_context(nc.sbuf_tensor("g1c4", [128, 4], F32))
        g2c4 = p8.enter_context(nc.sbuf_tensor("g2c4", [128, 4], F32))
        gates4 = p8.enter_context(nc.sbuf_tensor("gates4", [128, 32], F32))
        ppj = p8.enter_context(nc.sbuf_tensor("ppj", [1, 8], F32))
        gfm8 = p8.enter_context(nc.sbuf_tensor("gfm8", [8, 512], F32))
        g8bf = p8.enter_context(nc.sbuf_tensor("g8bf", [8, 512], BF16))
        geb = p8.enter_context(nc.sbuf_tensor("geb", [128, 512], F32))
        glt = [p8.enter_context(nc.sbuf_tensor(f"glt{i}", [128, 512], F32)) for i in range(2)]
        x1ft = [p8.enter_context(nc.sbuf_tensor(f"x1ft{i}", [128, 512], F32)) for i in range(2)]
        sq8 = x1ft
        t8 = geb
        ot8 = glt
        mu_row8 = p8.enter_context(nc.sbuf_tensor("mu_row8", [1, 512], F32))
        statacc81 = p8.enter_context(nc.sbuf_tensor("statacc81", [1, 512], F32))
        statacc82 = p8.enter_context(nc.sbuf_tensor("statacc82", [1, 512], F32))
        msq_row8 = p8.enter_context(nc.sbuf_tensor("msq_row8", [1, 512], F32))
        mu2_row8 = p8.enter_context(nc.sbuf_tensor("mu2_row8", [1, 512], F32))
        sd_row8 = p8.enter_context(nc.sbuf_tensor("sd_row8", [1, 512], F32))
        rs_row8 = p8.enter_context(nc.sbuf_tensor("rs_row8", [1, 512], F32))
        zer8 = p8.enter_context(nc.sbuf_tensor("zer8", [128, 8], F32))
        psh = [p8.enter_context(nc.psum_tensor(f"psh{i}", [128, 512], F32)) for i in range(8)]

        if limit >= 9:
          with nc.Block() as blk:
            @blk.sync
            def _(sy):
                for dst, src in [(b1_sb, b1_t), (sb1_sb, sb1_t), (sb2_sb, sb2_t),
                                 (oh_sb, oh_d)]:
                    dq(sy, "c1", dst[:, :], src[:, :])
                MARK["p8c"] = CNT["c1"]

            @blk.gpsimd
            def _(g):
                dq(g, "gc8", gate_sb[:, :], gate_t[:, :])
                dq(g, "gc8", b2_sb[:, :], b2_n[:, :])
                MARK["p8g"] = dq(g, "gc8", zer8[:, :], zeros_d[:, 0:8])

        NDVG = 54
        for tt in range(NTT if limit >= 9 else 0):
            x1b_tick = [0]
            with nc.Block() as blk:
                pe0, dv0, ac0 = CNT["pe"], CNT["dv"], CNT["ac"]

                @blk.gpsimd
                def _(g, tt=tt):
                    wt(g, "gp", "xs0", CNT["xs0"])
                    wt(g, "gp", "xs1", CNT["xs1"])
                    for ic in range(16):
                        dq(g, "gx1", x1bt[:, ic * 512:(ic + 1) * 512],
                           x1f_s[ic * 128:(ic + 1) * 128, tt * 512:(tt + 1) * 512])
                    x1b_tick[0] = CNT["gx1"]

                @blk.tensor
                def _(te, tt=tt, pe0=pe0, dv0=dv0, ac0=ac0):
                    wt(te, "te", "gx1", x1b_tick[0])
                    wt(te, "te", "gc8", MARK["p8g"])
                    wt(te, "te", "c1", MARK["p8c"])
                    for tcc in range(4):
                        if tcc >= 2:
                            wt(te, "te", "dv", dv0 + (tcc - 2) + 1)
                        ins = None
                        for ic in range(16):
                            ins = te.matmul(psh[tcc % 2][0:128, 0:8],
                                            x1bt[:, ic * 512 + tcc * 128: ic * 512 + (tcc + 1) * 128],
                                            gate_sb[:, ic * 8:(ic + 1) * 8],
                                            start=(ic == 0), stop=(ic == 15))
                        ins.then_inc(SEM["pe"], 1)
                        tk("pe")
                    for tcc in range(4):
                        wt(te, "te", "dv", dv0 + 45)
                        if tcc >= 1:
                            wt(te, "te", "dv", dv0 + 45 + tcc)
                        te.transpose(psh[2][0:8, 0:128], gates4[:, tcc * 8:(tcc + 1) * 8],
                                     identf[:, :]).then_inc(SEM["pe"], 1)
                        tk("pe")

                @blk.vector
                def _(v, tt=tt, pe0=pe0, dv0=dv0, ac0=ac0):
                    nd = [0]

                    def D(ins):
                        ins.then_inc(SEM["dv"], 1)
                        tk("dv")
                        nd[0] += 1

                    def WA(k):
                        wt(v, "dv", "ac", ac0 + k)

                    for tcc in range(4):
                        wt(v, "dv", "pe", pe0 + tcc + 1)
                        D(v.tensor_copy(lg4[:, tcc * 8:(tcc + 1) * 8], psh[tcc % 2][0:128, 0:8]))
                    WA(1)
                    for tcc in range(4):
                        D(v.reduce_max(m1c4[:, tcc:tcc + 1], lg4[:, tcc * 8:(tcc + 1) * 8], axis=AX.X))
                    WA(2)
                    for tcc in range(4):
                        D(v.tensor_scalar(eqa[:, tcc * 8:(tcc + 1) * 8], lg4[:, tcc * 8:(tcc + 1) * 8],
                                          m1c4[:, tcc:tcc + 1], None, ALU.is_equal))
                    WA(3)
                    for tcc in range(4):
                        D(v.tensor_tensor_scan(cum4[:, tcc * 8:(tcc + 1) * 8],
                                               eqa[:, tcc * 8:(tcc + 1) * 8], zer8[:, :], 0.0,
                                               ALU.add, ALU.add))
                    WA(4)
                    D(v.tensor_scalar(c4[:, :], cum4[:, :], 1.0, None, ALU.is_equal))
                    WA(5)
                    D(v.tensor_mul(sel14[:, :], eqa[:, :], c4[:, :]))
                    WA(6)
                    D(v.scalar_tensor_tensor(lg24[:, :], sel14[:, :], -1e30, lg4[:, :],
                                             ALU.mult, ALU.add))
                    WA(7)
                    D(v.tensor_copy(c4[:, :], lg24[:, :]))
                    WA(8)
                    for tcc in range(4):
                        D(v.reduce_max(m2c4[:, tcc:tcc + 1], lg24[:, tcc * 8:(tcc + 1) * 8], axis=AX.X))
                    WA(9)
                    for tcc in range(4):
                        D(v.tensor_scalar(eqa[:, tcc * 8:(tcc + 1) * 8], lg24[:, tcc * 8:(tcc + 1) * 8],
                                          m2c4[:, tcc:tcc + 1], None, ALU.is_equal))
                    WA(10)
                    for tcc in range(4):
                        D(v.tensor_tensor_scan(cum4[:, tcc * 8:(tcc + 1) * 8],
                                               eqa[:, tcc * 8:(tcc + 1) * 8], zer8[:, :], 0.0,
                                               ALU.add, ALU.add))
                    WA(11)
                    D(v.tensor_scalar(c4[:, :], cum4[:, :], 1.0, None, ALU.is_equal))
                    WA(12)
                    D(v.tensor_mul(sel24[:, :], eqa[:, :], c4[:, :]))
                    WA(13)
                    D(v.tensor_sub(dcc4[:, :], m1c4[:, :], m2c4[:, :]))
                    WA(14)
                    D(v.tensor_scalar(g2c4[:, :], g1c4[:, :], -1.0, 1.0, ALU.mult, ALU.add))
                    WA(15)
                    for tcc in range(4):
                        D(v.tensor_scalar(sel14[:, tcc * 8:(tcc + 1) * 8], sel14[:, tcc * 8:(tcc + 1) * 8],
                                          g1c4[:, tcc:tcc + 1], None, ALU.mult))
                    for tcc in range(4):
                        D(v.tensor_scalar(sel24[:, tcc * 8:(tcc + 1) * 8], sel24[:, tcc * 8:(tcc + 1) * 8],
                                          g2c4[:, tcc:tcc + 1], None, ALU.mult))
                    WA(16)
                    D(v.tensor_add(gates4[:, :], sel14[:, :], sel24[:, :]))
                    for tcc in range(4):
                        wt(v, "dv", "pe", pe0 + 4 + tcc + 1)
                        D(v.tensor_copy(gfm8[0:8, tcc * 128:(tcc + 1) * 128], psh[2][0:8, 0:128]))
                    WA(17)
                    D(v.tensor_copy(g8bf[0:8, :], gfm8[0:8, :]))
                    MARK[("g8bf", tt)] = CNT["dv"]

                @blk.scalar
                def _(ac, tt=tt, pe0=pe0, dv0=dv0, ac0=ac0):
                    for k, dvtick in enumerate([4, 8, 12, 16, 17, 18, 19, 20,
                                                24, 28, 32, 33, 34], start=1):
                        wt(ac, "ac", "dv", dv0 + dvtick)
                        ac.copy(ppj[:, :], ppj[:, :]).then_inc(SEM["ac"], 1)
                        tk("ac")
                    wt(ac, "ac", "dv", dv0 + 35)
                    ac.activation(g1c4[:, :], dcc4[:, :], AF.Sigmoid).then_inc(SEM["ac"], 1)
                    tk("ac")
                    wt(ac, "ac", "dv", dv0 + 36)
                    ac.copy(ppj[:, :], ppj[:, :]).then_inc(SEM["ac"], 1)
                    tk("ac")
                    wt(ac, "ac", "dv", dv0 + 44)
                    ac.copy(ppj[:, :], ppj[:, :]).then_inc(SEM["ac"], 1)
                    tk("ac")
                    wt(ac, "ac", "dv", dv0 + 49)
                    ac.copy(ppj[:, :], ppj[:, :]).then_inc(SEM["ac"], 1)
                    tk("ac")

            if debug_taps and tt == 0:
                MARK["dbg_g8"] = nc.dram_tensor("dbg_g8", [8, 512], F32, kind="ExternalOutput")
                with nc.Block() as blk:
                    @blk.gpsimd
                    def _(g):
                        dq(g, "gi", MARK["dbg_g8"][:, :], gfm8[0:8, :])
            with nc.Block() as blk:
                pe0, dv0, ac0 = CNT["pe"], CNT["dv"], CNT["ac"]
                NPEU = [9] * 8 + [8, 8]
                NDVU = [9] * 8 + [0, 0]
                peb = [pe0 + sum(NPEU[:u]) for u in range(10)]
                dvb = [dv0 + sum(NDVU[:u]) for u in range(10)]
                acb = [ac0 + 8 * u for u in range(10)]
                w_b = [CNT["w0"], CNT["w1"]]

                def half_tick(u, half):
                    i = 2 * u + half
                    return w_b[i % 2] + (i // 2 + 1) * 256

                def grp_tick(u, hc):
                    return peb[u] + (1 if u < 8 else 0) + hc + 1

                @blk.sync
                def _(sy, tt=tt):
                    for u in range(10):
                        for half in range(2):
                            sem = "w%d" % ((2 * u + half) % 2)
                            if u >= 1:
                                wt(sy, "sy", "pe", grp_tick(u - 1, half * 4 + 3))
                            for ic in range(16):
                                dst = wub[:, half * 8192 + ic * 512: half * 8192 + (ic + 1) * 512]
                                if u < 8:
                                    dq(sy, sem, dst,
                                       w1bf[u, ic * 128:(ic + 1) * 128, half * 512:(half + 1) * 512])
                                else:
                                    sh2 = u - 8
                                    dq(sy, sem, dst,
                                       sw1bf[ic * 128:(ic + 1) * 128,
                                             sh2 * 1024 + half * 512: sh2 * 1024 + (half + 1) * 512])

                @blk.tensor
                def _(te, tt=tt):
                    wt(te, "te", "g1b", MARK["w1bf"])
                    wt(te, "te", "g2b", MARK["wbf2"])
                    wt(te, "te", "gx1", x1b_tick[0])
                    for u in range(10):
                        if u < 8:
                            if u >= 1:
                                wt(te, "te", "ac", acb[u - 1] + 8)
                            te.matmul(psh[7][:, :], oh_sb[0:8, u * 128:(u + 1) * 128],
                                      gfm8[0:8, :], start=True, stop=True).then_inc(SEM["pe"], 1)
                            tk("pe")
                        for hc in range(8):
                            half = hc // 4
                            wt(te, "te", "w%d" % ((2 * u + half) % 2), half_tick(u, half))
                            if u >= 1:
                                wt(te, "te", "ac", acb[u - 1] + hc + 1)
                            if hc == 7 and u < 8:
                                wt(te, "te", "dv", dvb[u] + 1)
                            ins = None
                            for ic in range(16):
                                ins = te.matmul(
                                    psh[hc][:, :],
                                    wub[:, half * 8192 + ic * 512 + (hc % 4) * 128:
                                        half * 8192 + ic * 512 + (hc % 4) * 128 + 128],
                                    x1bt[:, ic * 512:(ic + 1) * 512],
                                    start=(ic == 0), stop=(ic == 15))
                            ins.then_inc(SEM["pe"], 1)
                            tk("pe")

                @blk.vector
                def _(v, tt=tt):
                    for u in range(8):
                        wt(v, "dv", "pe", peb[u] + 1)
                        v.tensor_copy(geb[:, :], psh[7][:, :]).then_inc(SEM["dv"], 1)
                        tk("dv")
                        for hc in range(8):
                            wt(v, "dv", "ac", acb[u] + hc + 1)
                            v.tensor_mul(hbuf[:, (u * 8 + hc) * 512:(u * 8 + hc + 1) * 512],
                                         glt[hc % 2][:, :], geb[:, :]).then_inc(SEM["dv"], 1)
                            tk("dv")

                @blk.scalar
                def _(ac, tt=tt):
                    for u in range(10):
                        for hc in range(8):
                            wt(ac, "ac", "pe", grp_tick(u, hc))
                            if u < 8:
                                if u * 8 + hc >= 2:
                                    pu, ph = divmod(u * 8 + hc - 2, 8)
                                    wt(ac, "ac", "dv", dvb[pu] + 2 + ph)
                                ac.activation(glt[hc % 2][:, :], psh[hc][:, :], AF.Gelu,
                                              bias=b1_sb[:, u * 8 + hc: u * 8 + hc + 1]
                                              ).then_inc(SEM["ac"], 1)
                                tk("ac")
                            else:
                                sh2 = u - 8
                                ac.activation(shb[:, (sh2 * 8 + hc) * 512:(sh2 * 8 + hc + 1) * 512],
                                              psh[hc][:, :], AF.Gelu,
                                              bias=sb1_sb[:, sh2 * 8 + hc: sh2 * 8 + hc + 1]
                                              ).then_inc(SEM["ac"], 1)
                                tk("ac")

            if debug_taps and tt == 0:
                MARK["dbg_h"] = nc.dram_tensor("dbg_h", [128, 64 * 512], BF16, kind="ExternalOutput")
                MARK["dbg_sh"] = nc.dram_tensor("dbg_sh", [128, 16 * 512], BF16, kind="ExternalOutput")
                with nc.Block() as blk:
                    @blk.gpsimd
                    def _(g):
                        dq(g, "gi", MARK["dbg_h"][:, :], hbuf[:, :])
                        dq(g, "gi", MARK["dbg_sh"][:, :], shb[:, :])
            with nc.Block() as blk:
                pe0, dv0, ac0 = CNT["pe"], CNT["dv"], CNT["ac"]
                w_b2 = [CNT["w0"], CNT["w1"]]
                xf_b = [CNT["xf0"], CNT["xf1"]]

                @blk.sync
                def _(sy, tt=tt, pe0=pe0):
                    for fp in range(8):
                        sem = "w%d" % (fp % 2)
                        if fp >= 1:
                            wt(sy, "sy", "pe", pe0 + 2 * (fp - 1) + 2)
                        for j in range(10):
                            if j < 8:
                                dq(sy, sem,
                                   wub[:, (j * 8) * 256:(j * 8 + 8) * 256]
                                   .rearrange("p (c f) -> p c f", c=8),
                                   w2bf[fp, j * 1024:(j + 1) * 1024, :]
                                   .rearrange("(c p) f -> p c f", p=128))
                            else:
                                dq(sy, sem,
                                   wub[:, (j * 8) * 256:(j * 8 + 8) * 256]
                                   .rearrange("p (c f) -> p c f", c=8),
                                   sw2bf[fp, (j - 8) * 1024:(j - 7) * 1024, :]
                                   .rearrange("(c p) f -> p c f", p=128))

                @blk.scalar
                def _(ac, tt=tt, dv0=dv0):
                    wt(ac, "ac", "xs0", CNT["xs0"])
                    wt(ac, "ac", "xs1", CNT["xs1"])
                    for fp in range(8):
                        for half in range(2):
                            if fp >= 1:
                                wt(ac, "ac", "dv", dv0 + 2 * (fp - 1) + half + 1)
                            fo = fp * 2 + half
                            dq(ac, "xf%d" % half, x1ft[half][:, :],
                               x1f_s[fo * 128:(fo + 1) * 128, tt * 512:(tt + 1) * 512])

                @blk.tensor
                def _(te, tt=tt, pe0=pe0, dv0=dv0, w_b2=w_b2):
                    for fp in range(8):
                        wt(te, "te", "w%d" % (fp % 2),
                           w_b2[fp % 2] + (fp // 2 + 1) * 160)
                        for half in range(2):
                            if fp >= 1:
                                wt(te, "te", "dv", dv0 + 2 * (fp - 1) + half + 1)
                            te.matmul(psh[half][:, :],
                                      b2_sb[0:8, fp * 256 + half * 128: fp * 256 + half * 128 + 128],
                                      g8bf[0:8, :], start=True, stop=False)
                            ins = None
                            for ci in range(80):
                                rhs = hbuf[:, ci * 512:(ci + 1) * 512] if ci < 64 else \
                                    shb[:, (ci - 64) * 512:(ci - 63) * 512]
                                ins = te.matmul(
                                    psh[half][:, :],
                                    wub[:, ci * 256 + half * 128: ci * 256 + half * 128 + 128],
                                    rhs, start=False, stop=(ci == 79))
                            ins.then_inc(SEM["pe"], 1)
                            tk("pe")

                @blk.vector
                def _(v, tt=tt, pe0=pe0, dv0=dv0, xf_b=xf_b):
                    for fp in range(8):
                        for half in range(2):
                            fo = fp * 2 + half
                            wt(v, "dv", "pe", pe0 + 2 * fp + half + 1)
                            wt(v, "dv", "xf%d" % half, xf_b[half] + (fp + 1) * 16)
                            v.scalar_tensor_tensor(r2b[:, fo * 512:(fo + 1) * 512],
                                                   psh[half][:, :], sb2_sb[:, fo:fo + 1],
                                                   x1ft[half][:, :],
                                                   ALU.add, ALU.add).then_inc(SEM["dv"], 1)
                            tk("dv")

            if debug_taps and tt == 0:
                MARK["dbg_r2"] = nc.dram_tensor("dbg_r2", [128, 16 * 512], F32, kind="ExternalOutput")
                with nc.Block() as blk:
                    @blk.gpsimd
                    def _(g):
                        dq(g, "gi", MARK["dbg_r2"][:, :], r2b[:, :])
            with nc.Block() as blk:
                pe0, dv0, ac0 = CNT["pe"], CNT["dv"], CNT["ac"]
                os_b = [CNT["os0"], CNT["os1"]]

                @blk.tensor
                def _(te, tt=tt, pe0=pe0, dv0=dv0, ac0=ac0):
                    for fo in range(16):
                        if fo >= 1:
                            wt(te, "te", "dv", dv0 + 2 + 2 * (fo - 1) + 1)
                        te.matmul(psh[2][0:1, :], onesc[:, 0:1],
                                  r2b[:, fo * 512:(fo + 1) * 512],
                                  start=True, stop=True).then_inc(SEM["pe"], 1)
                        tk("pe")
                        wt(te, "te", "ac", ac0 + fo + 1)
                        if fo >= 1:
                            wt(te, "te", "dv", dv0 + 2 + 2 * (fo - 1) + 2)
                        te.matmul(psh[3][0:1, :], onesc[:, 0:1], sq8[fo % 2][:, :],
                                  start=True, stop=True).then_inc(SEM["pe"], 1)
                        tk("pe")
                    wt(te, "te", "dv", dv0 + 35)
                    te.matmul(psh[4][:, :], onesr[0:1, :], mu_row8[0:1, :],
                              start=True, stop=True).then_inc(SEM["pe"], 1)
                    tk("pe")
                    wt(te, "te", "dv", dv0 + 38)
                    te.matmul(psh[5][:, :], onesr[0:1, :], rs_row8[0:1, :],
                              start=True, stop=True).then_inc(SEM["pe"], 1)
                    tk("pe")

                @blk.scalar
                def _(ac, tt=tt, pe0=pe0, dv0=dv0, ac0=ac0):
                    for fo in range(16):
                        if fo >= 2:
                            wt(ac, "ac", "pe", pe0 + 2 * (fo - 2) + 2)
                        ac.activation(sq8[fo % 2][:, :], r2b[:, fo * 512:(fo + 1) * 512],
                                      AF.Square).then_inc(SEM["ac"], 1)
                        tk("ac")
                    wt(ac, "ac", "dv", dv0 + 35)
                    ac.activation(mu2_row8[:, :], mu_row8[:, :], AF.Square).then_inc(SEM["ac"], 1)
                    tk("ac")
                    wt(ac, "ac", "dv", dv0 + 37)
                    ac.activation(sd_row8[:, :], msq_row8[:, :], AF.Sqrt, bias=fconst[0:1, 2:3]
                                  ).then_inc(SEM["ac"], 1)
                    tk("ac")
                    for fo in range(16):
                        wt(ac, "ac", "dv", dv0 + 38 + 2 * fo + 2)
                        dq(ac, "os%d" % (fo % 2),
                           out_fm[fo * 128:(fo + 1) * 128, tt * 512:(tt + 1) * 512],
                           ot8[fo % 2][:, :])

                @blk.vector
                def _(v, tt=tt, pe0=pe0, dv0=dv0, ac0=ac0, os_b=os_b):
                    v.memset(statacc81[:, :], 0.0).then_inc(SEM["dv"], 1)
                    tk("dv")
                    v.memset(statacc82[:, :], 0.0).then_inc(SEM["dv"], 1)
                    tk("dv")
                    for fo in range(16):
                        wt(v, "dv", "pe", pe0 + 2 * fo + 1)
                        v.tensor_add(statacc81[:, :], statacc81[:, :], psh[2][0:1, :]
                                     ).then_inc(SEM["dv"], 1)
                        tk("dv")
                        wt(v, "dv", "pe", pe0 + 2 * fo + 2)
                        v.tensor_add(statacc82[:, :], statacc82[:, :], psh[3][0:1, :]
                                     ).then_inc(SEM["dv"], 1)
                        tk("dv")
                    v.tensor_scalar(mu_row8[:, :], statacc81[:, :], 1.0 / 2048.0, None,
                                    ALU.mult).then_inc(SEM["dv"], 1)
                    tk("dv")
                    v.tensor_scalar(msq_row8[:, :], statacc82[:, :], 1.0 / 2048.0, None,
                                    ALU.mult).then_inc(SEM["dv"], 1)
                    tk("dv")
                    wt(v, "dv", "ac", ac0 + 17)
                    v.tensor_sub(msq_row8[:, :], msq_row8[:, :], mu2_row8[:, :]).then_inc(SEM["dv"], 1)
                    tk("dv")
                    wt(v, "dv", "ac", ac0 + 18)
                    v.reciprocal(rs_row8[:, :], sd_row8[:, :]).then_inc(SEM["dv"], 1)
                    tk("dv")
                    wt(v, "dv", "pe", pe0 + 34)
                    for fo in range(16):
                        v.tensor_sub(t8[:, :], r2b[:, fo * 512:(fo + 1) * 512], psh[4][:, :]
                                     ).then_inc(SEM["dv"], 1)
                        tk("dv")
                        if fo >= 2:
                            wt(v, "dv", "os%d" % (fo % 2),
                               os_b[fo % 2] + (fo // 2) * 16)
                        v.tensor_mul(ot8[fo % 2][:, :], t8[:, :], psh[5][:, :]).then_inc(SEM["dv"], 1)
                        tk("dv")

        if debug_taps:
            dbg_qt = nc.dram_tensor("dbg_qt", [1024, T], BF16, kind="ExternalOutput")
            dbg_kt = nc.dram_tensor("dbg_kt", [1024, T], BF16, kind="ExternalOutput")
            dbg_on = nc.dram_tensor("dbg_on", [T, D], BF16, kind="ExternalOutput")
            dbg_x1 = nc.dram_tensor("dbg_x1", [D, T], F32, kind="ExternalOutput")
            with nc.Block() as blk:
                @blk.gpsimd
                def _(g):
                    for k in DMASEMS:
                        if CNT[k] > 0:
                            wt(g, "gp", k, CNT[k])
                    dq(g, "gi", dbg_qt[:, :], qt_s[:, :])
                    dq(g, "gi", dbg_kt[:, :], kt_s[:, :])
                    dq(g, "gi", dbg_on[:, :], onorm_s[:, :])
                    dq(g, "gi", dbg_x1[:, :], x1f_s[:, :])
                    wt(g, "gp", "gi", CNT["gi"])

        nc.leave_named_scope("zz_p8", MARK["sc_p8"], True)
        with nc.Block() as blk:
            @blk.sync
            def _(sy):
                for k in DMASEMS:
                    if CNT[k] > 0:
                        sy.wait_ge(SEM[k], CNT[k])
        p8.close()

    return nc


_CACHE = {}


def _prep_shared(inp):
    asc = np.ascontiguousarray
    f32 = lambda a: np.asarray(a, dtype=np.float32)
    Wq, Wk = f32(inp["L0_attn_Wq"]), f32(inp["L0_attn_Wk"])
    Wkg1, Wkg2 = f32(inp["L0_attn_Wkg1"]), f32(inp["L0_attn_Wkg2"])
    bkg2 = f32(inp["L0_attn_bkg2"])
    Wv, Wg, Wo = f32(inp["L0_attn_Wv"]), f32(inp["L0_attn_Wg"]), f32(inp["L0_attn_Wo"])
    bg = f32(inp["L0_attn_bg"])
    gate_w = f32(inp["L0_ffn_gate_w"])
    w1, b1 = f32(inp["L0_ffn_w1"]), f32(inp["L0_ffn_b1"])
    w2, b2 = f32(inp["L0_ffn_w2"]), f32(inp["L0_ffn_b2"])
    sw1, sb1 = f32(inp["L0_ffn_shared_w1"]), f32(inp["L0_ffn_shared_b1"])
    sw2, sb2 = f32(inp["L0_ffn_shared_w2"]), f32(inp["L0_ffn_shared_b2"])

    d = {}
    d["wqT"] = asc(Wq.T)
    d["wkT"] = asc(Wk.T)
    d["wkg1"] = asc(Wkg1.T.reshape(16, 128, 16).transpose(1, 0, 2).reshape(128, 256))
    d["wkg2"] = asc(Wkg2.T)
    d["bkg2_t"] = asc(bkg2.reshape(8, 128).T)
    d["wvT"] = asc(Wv.T)
    d["wgT"] = asc(Wg.T)
    d["woT"] = asc(Wo.T)
    d["bg_row"] = asc(bg.reshape(1, D))
    d["w1_n"] = asc(w1)
    d["w2r"] = asc(w2.reshape(8, 1024, 8, 256).transpose(2, 0, 1, 3).reshape(8, 8192, 256))
    d["sw1_n"] = asc(sw1)
    d["sw2r"] = asc(sw2.reshape(2048, 8, 256).transpose(1, 0, 2))
    d["gate_t"] = asc(gate_w.reshape(16, 128, 8).transpose(1, 0, 2).reshape(128, 128))
    d["b1_t"] = asc(b1.reshape(8, 8, 128).transpose(2, 0, 1).reshape(128, 64))
    d["b2_n"] = asc(b2)
    d["sb1_t"] = asc(sb1.reshape(16, 128).T)
    d["sb2_t"] = asc(sb2.reshape(16, 128).T)
    d["identb_d"] = np.eye(128, dtype=BFNP)
    d["identf_d"] = np.eye(128, dtype=np.float32)
    s = np.arange(128)[:, None]
    t = np.arange(128)[None, :]
    d["maskut_d"] = (s <= t).astype(np.float32)
    d["zeros_d"] = np.zeros((128, 128), np.float32)
    d["ones_r"] = np.ones((1, 128), np.float32)
    d["ones_c"] = np.ones((128, 1), np.float32)
    oh = np.zeros((8, 1024), np.float32)
    for e in range(8):
        oh[e, e * 128:(e + 1) * 128] = 1.0
    d["oh_d"] = oh
    em = np.zeros((128, 64), np.float32)
    for j in range(8):
        for e in range(8):
            if e > j:
                em[:, j * 8 + e] = 1.0
    d["emask_d"] = em
    fc = np.zeros((128, 4), np.float32)
    fc[:, 0] = LNB
    fc[:, 1] = 1.0
    fc[:, 2] = EPS
    d["fconst_d"] = fc
    return d


def kernel(**inputs):
    x = np.asarray(inputs["L0_attn_x"], dtype=np.float32)
    B, L, _ = x.shape
    shared = _prep_shared(inputs)

    if "nc" not in _CACHE:
        _CACHE["nc"] = build_program()
    nc = _CACHE["nc"]

    in_maps = []
    for c in range(8):
        b, hf = c // 2, c % 2
        m = dict(shared)
        m["x_fm"] = np.ascontiguousarray(x[b, hf * T:(hf + 1) * T, :].T)
        if c % 2 == 1:
            ix = (c - 1) * 128 + np.arange(128, dtype=np.int32)
            ms = np.ones((128, 1), np.float32)
        else:
            ix = c * 128 + np.arange(128, dtype=np.int32)
            ms = np.zeros((128, 1), np.float32)
        m["idx_d"] = ix[:, None]
        m["maskS_d"] = ms
        in_maps.append(m)

    res = run_bass_kernel_spmd(nc, in_maps, core_ids=list(range(8)))
    out = np.empty((B, L, D), np.float32)
    for c in range(8):
        b, hf = c // 2, c % 2
        out[b, hf * T:(hf + 1) * T, :] = res.results[c]["out_fm"].T
    return out


if __name__ == "__main__":
    nc = build_program()
    n = sum(1 for _ in nc.all_instructions())
    print("instructions:", n)


# version 05
# speedup vs baseline: 1.1410x; 1.1405x over previous
import math
import numpy as np
import ml_dtypes
from contextlib import ExitStack
import concourse.bass as bass
import concourse.mybir as mybir
from concourse.bass_utils import run_bass_kernel_spmd

F32 = mybir.dt.float32
BF16 = mybir.dt.bfloat16
I32 = mybir.dt.int32
AF = mybir.ActivationFunctionType
ALU = mybir.AluOpType
AX = mybir.AxisListType
BFNP = ml_dtypes.bfloat16

T = 2048
D = 2048
NH = 16
DK, DV = 64, 128
NCH = 16
NTT = 4
SCALE = 128.0 ** (-0.5)
NORMI = 1.0 / 16.0
EPS = 1e-5
LNB = math.log(SCALE)


def build_program(limit=99, debug_taps=False):
    nc = bass.Bass("TRN2", target_bir_lowering=False, detect_race_conditions=False)

    x_fm = nc.dram_tensor("x_fm", [D, T], F32, kind="ExternalInput")
    wqT = nc.dram_tensor("wqT", [D, 1024], F32, kind="ExternalInput")
    wkT = nc.dram_tensor("wkT", [D, 1024], F32, kind="ExternalInput")
    wkg1 = nc.dram_tensor("wkg1", [128, 256], F32, kind="ExternalInput")
    wkg2 = nc.dram_tensor("wkg2", [16, 1024], F32, kind="ExternalInput")
    bkg2_t = nc.dram_tensor("bkg2_t", [128, 8], F32, kind="ExternalInput")
    wvT = nc.dram_tensor("wvT", [D, D], F32, kind="ExternalInput")
    wgT = nc.dram_tensor("wgT", [D, D], F32, kind="ExternalInput")
    woT = nc.dram_tensor("woT", [D, D], F32, kind="ExternalInput")
    bg_row = nc.dram_tensor("bg_row", [1, D], F32, kind="ExternalInput")
    w1_n = nc.dram_tensor("w1_n", [8, D, 1024], F32, kind="ExternalInput")
    w2r = nc.dram_tensor("w2r", [8, 2048, 1024], F32, kind="ExternalInput")
    sw1_n = nc.dram_tensor("sw1_n", [D, D], F32, kind="ExternalInput")
    sw2r = nc.dram_tensor("sw2r", [8, 512, 1024], F32, kind="ExternalInput")
    gate_t = nc.dram_tensor("gate_t", [128, 128], F32, kind="ExternalInput")
    b1_t = nc.dram_tensor("b1_t", [128, 64], F32, kind="ExternalInput")
    b2_n = nc.dram_tensor("b2_n", [8, D], F32, kind="ExternalInput")
    sb1_t = nc.dram_tensor("sb1_t", [128, 16], F32, kind="ExternalInput")
    sb2_t = nc.dram_tensor("sb2_t", [128, 16], F32, kind="ExternalInput")
    identb_d = nc.dram_tensor("identb_d", [128, 128], BF16, kind="ExternalInput")
    identf_d = nc.dram_tensor("identf_d", [128, 128], F32, kind="ExternalInput")
    maskut_d = nc.dram_tensor("maskut_d", [128, 128], F32, kind="ExternalInput")
    zeros_d = nc.dram_tensor("zeros_d", [128, 128], F32, kind="ExternalInput")
    ones_r = nc.dram_tensor("ones_r", [1, 128], F32, kind="ExternalInput")
    ones_c = nc.dram_tensor("ones_c", [128, 1], F32, kind="ExternalInput")
    oh_d = nc.dram_tensor("oh_d", [8, 1024], F32, kind="ExternalInput")
    emask_d = nc.dram_tensor("emask_d", [128, 64], F32, kind="ExternalInput")
    idx_d = nc.dram_tensor("idx_d", [128, 1], I32, kind="ExternalInput")
    fconst_d = nc.dram_tensor("fconst_d", [128, 4], F32, kind="ExternalInput")
    maskS_d = nc.dram_tensor("maskS_d", [128, 1], F32, kind="ExternalInput")

    out_fm = nc.dram_tensor("out_fm", [D, T], F32, kind="ExternalOutput")

    xbf = nc.dram_tensor("xbf", [D, T], BF16, kind="Internal")
    w1bf = nc.dram_tensor("w1bf", [8, D, 1024], BF16, kind="Internal")
    w2bf = nc.dram_tensor("w2bf", [8, 2048, 1024], BF16, kind="Internal")
    sw1bf = nc.dram_tensor("sw1bf", [D, D], BF16, kind="Internal")
    sw2bf = nc.dram_tensor("sw2bf", [8, 512, 1024], BF16, kind="Internal")
    qt_s = nc.dram_tensor("qt_s", [1024, T], BF16, kind="Internal")
    kt_s = nc.dram_tensor("kt_s", [1024, T], BF16, kind="Internal")
    onorm_s = nc.dram_tensor("onorm_s", [T, D], BF16, kind="Internal")
    x1f_s = nc.dram_tensor("x1f_s", [D, T], F32, kind="Internal")
    st_contrib = nc.dram_tensor("st_contrib", [128, 1024], F32, kind="Internal")
    st_gather = nc.dram_tensor("st_gather", [1024, 1024], F32, kind="Internal")

    DMASEMS = ["c0", "c1", "gx", "gk", "gq", "gkk", "gv", "g1b", "gi", "g2b",
               "gg", "go", "gc8", "gx1", "x0", "x1", "q0", "q1", "o0", "o1",
               "s0", "s1", "so0", "so1", "sc0", "xr0", "xr1", "xs0", "xs1",
               "xf0", "xf1", "w0", "w1", "os0", "os1"]
    CNT = dict(pe=0, dv=0, ac=0, cc=0)
    for _k in DMASEMS:
        CNT[_k] = 0
    MARK = {}
    LASTW = {}
    SEM = {}

    def tk(key, step=1):
        CNT[key] += step
        return CNT[key]

    def wt(eng, ename, key, val):
        if val is None or val <= 0:
            return
        if LASTW.get((ename, key), 0) >= val:
            return
        LASTW[(ename, key)] = val
        eng.wait_ge(SEM[key], val)

    def dq(eng, name, dst, src):
        eng.dma_start(dst, src).then_inc(SEM[name], 16)
        CNT[name] += 16
        return CNT[name]

    ctx = ExitStack()
    with ctx:
        E = ctx.enter_context
        for k in CNT:
            SEM[k] = E(nc.semaphore("sem_" + k))

        identb = E(nc.sbuf_tensor("identb", [128, 128], BF16))
        identf = E(nc.sbuf_tensor("identf", [128, 128], F32))
        onesr = E(nc.sbuf_tensor("onesr", [1, 128], F32))
        onesc = E(nc.sbuf_tensor("onesc", [128, 1], F32))
        idx_sb = E(nc.sbuf_tensor("idx_sb", [128, 1], I32))
        fconst = E(nc.sbuf_tensor("fconst", [128, 4], F32))
        maskS_sb = E(nc.sbuf_tensor("maskS_sb", [128, 1], F32))

        MARK["sc_p0"] = nc.enter_named_scope("zz_p0", True)[0]
        with nc.Block() as blk:
            @blk.sync
            def _(sy):
                for dst, src in [(identb, identb_d), (identf, identf_d),
                                 (onesr, ones_r), (onesc, ones_c),
                                 (idx_sb, idx_d),
                                 (fconst, fconst_d), (maskS_sb, maskS_d)]:
                    dq(sy, "c0", dst[:, :], src[:, :])
                MARK["consts"] = CNT["c0"]

            @blk.gpsimd
            def _(g):
                MARK["xbf"] = dq(g, "gx", xbf[:, :], x_fm[:, :])

        sab = ExitStack()
        khT = sab.enter_context(nc.sbuf_tensor("khT", [128, NCH * 1024], BF16))
        maskut = sab.enter_context(nc.sbuf_tensor("maskut", [128, 128], F32))
        gam = sab.enter_context(nc.sbuf_tensor("gam", [128, 128], F32))
        S_buf = sab.enter_context(nc.sbuf_tensor("S_buf", [128, 1024], F32))
        S_bf = [sab.enter_context(nc.sbuf_tensor(f"S_bf{i}", [128, 1024], BF16)) for i in range(2)]
        S_init = sab.enter_context(nc.sbuf_tensor("S_init", [128, 1024], F32))
        sa2 = ExitStack()
        Gbuf = sa2.enter_context(nc.sbuf_tensor("Gbuf", [128, 8 * T], F32))
        zer = sa2.enter_context(nc.sbuf_tensor("zer", [128, 128], F32))
        wkg1_sb = sa2.enter_context(nc.sbuf_tensor("wkg1_sb", [128, 256], BF16))
        wkg2_sb = sa2.enter_context(nc.sbuf_tensor("wkg2_sb", [16, 1024], BF16))
        bkg2_sb = sa2.enter_context(nc.sbuf_tensor("bkg2_sb", [128, 8], F32))

        with nc.Block() as blk:
            @blk.sync
            def _(sy):
                for dst, src in [(maskut, maskut_d), (zer, zeros_d),
                                 (bkg2_sb, bkg2_t)]:
                    dq(sy, "c1", dst[:, :], src[:, :])
                MARK["consts2"] = CNT["c1"]

            @blk.gpsimd
            def _(g):
                dq(g, "gk", wkg1_sb[:, :], wkg1[:, :])
                MARK["wkg"] = dq(g, "gk", wkg2_sb[:, :], wkg2[:, :])

            @blk.vector
            def _(v):
                v.memset(S_buf[:, :], 0.0).then_inc(SEM["dv"], 1)
                tk("dv")

        nc.leave_named_scope("zz_p0", MARK["sc_p0"], True)
        MARK["sc_p1"] = nc.enter_named_scope("zz_p1", True)[0]
        p1 = ExitStack()
        xsl = [p1.enter_context(nc.sbuf_tensor(f"xsl{i}", [128, 16 * 512], BF16)) for i in range(2)]
        kg1b = p1.enter_context(nc.sbuf_tensor("kg1b", [16, 512], BF16))
        t_mn = p1.enter_context(nc.sbuf_tensor("t_mn", [128, 512], F32))
        t_ab = p1.enter_context(nc.sbuf_tensor("t_ab", [128, 512], F32))
        t_gk = p1.enter_context(nc.sbuf_tensor("t_gk", [128, 512], F32))
        ps1 = p1.enter_context(nc.psum_tensor("ps1", [128, 512], F32))
        ps2 = [p1.enter_context(nc.psum_tensor(f"ps2_{i}", [128, 512], F32)) for i in range(2)]
        xsl_tick = [0, 0]

        def load_xsl(sy, buf_i, tt):
            sem = "x%d" % buf_i
            for ic in range(16):
                dq(sy, sem, xsl[buf_i][:, ic * 512:(ic + 1) * 512],
                   xbf[ic * 128:(ic + 1) * 128, tt * 512:(tt + 1) * 512])
            xsl_tick[buf_i] = CNT[sem]

        for tt in range(NTT if limit >= 1 else 0):
            with nc.Block() as blk:
                bi = tt % 2
                pe0, dv0, ac0 = CNT["pe"], CNT["dv"], CNT["ac"]

                @blk.sync
                def _(sy, bi=bi, tt=tt):
                    if tt == 0:
                        wt(sy, "sy", "gx", MARK["xbf"])
                        load_xsl(sy, 0, 0)
                    if tt + 1 < NTT:
                        load_xsl(sy, (tt + 1) % 2, tt + 1)

                @blk.tensor
                def _(te, bi=bi, tt=tt, pe0=pe0, dv0=dv0, ac0=ac0):
                    wt(te, "te", "x%d" % bi, xsl_tick[bi])
                    wt(te, "te", "gk", MARK["wkg"])
                    ins = None
                    for ic in range(16):
                        ins = te.matmul(ps1[0:16, :], wkg1_sb[:, ic * 16:(ic + 1) * 16],
                                        xsl[bi][:, ic * 512:(ic + 1) * 512],
                                        start=(ic == 0), stop=(ic == 15))
                    ins.then_inc(SEM["pe"], 1)
                    tk("pe")
                    for ic2 in range(8):
                        wt(te, "te", "ac", ac0 + 1)
                        if ic2 >= 2:
                            wt(te, "te", "dv", dv0 + 6 * (ic2 - 2) + 1)
                            wt(te, "te", "ac", ac0 + 1 + 3 * (ic2 - 2) + 1)
                        te.matmul(ps2[ic2 % 2][:, :], wkg2_sb[0:16, ic2 * 128:(ic2 + 1) * 128],
                                  kg1b[:, :], start=True, stop=True).then_inc(SEM["pe"], 1)
                        tk("pe")

                @blk.scalar
                def _(ac, bi=bi, tt=tt, pe0=pe0, dv0=dv0, ac0=ac0):
                    wt(ac, "ac", "c1", MARK["consts2"])
                    wt(ac, "ac", "c0", MARK["consts"])
                    wt(ac, "ac", "pe", pe0 + 1)
                    ac.copy(kg1b[:, :], ps1[0:16, :]).then_inc(SEM["ac"], 1)
                    tk("ac")
                    for ic2 in range(8):
                        wt(ac, "ac", "pe", pe0 + 2 + ic2)
                        if ic2 >= 1:
                            wt(ac, "ac", "dv", dv0 + 6 * (ic2 - 1) + 2)
                        wt(ac, "ac", "dv", dv0 + 6 * ic2 + 1)
                        ac.activation(t_ab[:, :], ps2[ic2 % 2][:, :], AF.Abs,
                                      bias=bkg2_sb[:, ic2:ic2 + 1], scale=1.0).then_inc(SEM["ac"], 1)
                        tk("ac")
                        ac.activation(t_ab[:, :], t_ab[:, :], AF.Exp, scale=-1.0).then_inc(SEM["ac"], 1)
                        tk("ac")
                        ac.activation(t_ab[:, :], t_ab[:, :], AF.Ln, bias=fconst[:, 1:2]).then_inc(SEM["ac"], 1)
                        tk("ac")
                    for ic2 in range(8):
                        gcol = ic2 * T + tt * 512
                        for cs in range(4):
                            wt(ac, "ac", "dv", dv0 + 6 * ic2 + 2 + cs + 1)
                            gi = ic2 * 16 + tt * 4 + cs
                            ac.activation(gam[:, gi:gi + 1],
                                          Gbuf[:, gcol + cs * 128 + 127: gcol + cs * 128 + 128],
                                          AF.Exp, scale=NORMI).then_inc(SEM["ac"], 1)
                            tk("ac")

                @blk.vector
                def _(v, bi=bi, tt=tt, pe0=pe0, dv0=dv0, ac0=ac0):
                    wt(v, "dv", "c1", MARK["consts2"])
                    for ic2 in range(8):
                        wt(v, "dv", "pe", pe0 + 2 + ic2)
                        v.tensor_scalar(t_mn[:, :], ps2[ic2 % 2][:, :], bkg2_sb[:, ic2:ic2 + 1],
                                        0.0, ALU.add, ALU.min).then_inc(SEM["dv"], 1)
                        tk("dv")
                        wt(v, "dv", "ac", ac0 + 1 + 3 * ic2 + 3)
                        v.tensor_sub(t_gk[:, :], t_mn[:, :], t_ab[:, :]).then_inc(SEM["dv"], 1)
                        tk("dv")
                        gcol = ic2 * T + tt * 512
                        for cs in range(4):
                            v.tensor_tensor_scan(
                                Gbuf[:, gcol + cs * 128: gcol + (cs + 1) * 128],
                                t_gk[:, cs * 128:(cs + 1) * 128], zer[:, :], 0.0,
                                ALU.add, ALU.add).then_inc(SEM["dv"], 1)
                            tk("dv")
        p1.close()

        nc.leave_named_scope("zz_p1", MARK["sc_p1"], True)
        MARK["sc_p23"] = nc.enter_named_scope("zz_p23", True)[0]
        def proj_qk(which):
            px = ExitStack()
            wsb = px.enter_context(nc.sbuf_tensor(f"w{which}_sb", [128, 16 * 1024], BF16))
            xsl2 = [px.enter_context(nc.sbuf_tensor(f"x{which}{i}", [128, 16 * 512], BF16)) for i in range(2)]
            eg = px.enter_context(nc.sbuf_tensor(f"eg_{which}", [128, 512], F32))
            ot = [px.enter_context(nc.sbuf_tensor(f"ot_{which}{i}", [128, 512], BF16)) for i in range(2)]
            kh = px.enter_context(nc.sbuf_tensor(f"kh_{which}", [128, 512], BF16))
            psp = [px.enter_context(nc.psum_tensor(f"psp_{which}{i}", [128, 512], F32)) for i in range(2)]
            pst = px.enter_context(nc.psum_tensor(f"pst_{which}", [128, 1024], BF16))
            wd = wqT if which == "q" else wkT
            sdram = qt_s if which == "q" else kt_s
            is_k = (which == "k")
            NDV = 6 if is_k else 1
            NPE = 5 if is_k else 1
            x2_tick = [0, 0]
            w_tick = [0]
            wsem = "gq" if which == "q" else "gkk"
            st_base = [CNT["s0"], CNT["s1"]]

            def st_sem(fc):
                return "s%d" % (fc % 2)

            def st_tick(tt, fc):
                i = tt * 4 + fc // 2
                if tt < 0 or fc < 0:
                    return 0
                return st_base[fc % 2] + (i + 1) * 16

            with nc.Block() as blk:
                @blk.gpsimd
                def _(g):
                    for ic in range(16):
                        dq(g, wsem, wsb[:, ic * 1024:(ic + 1) * 1024],
                           wd[ic * 128:(ic + 1) * 128, :])
                    w_tick[0] = CNT[wsem]
                    es = range(0, 3) if which == "q" else range(3, 6)
                    for e in es:
                        dq(g, "g1b", w1bf[e, :, :], w1_n[e, :, :])

                @blk.sync
                def _(sy):
                    for ic in range(16):
                        dq(sy, "x0", xsl2[0][:, ic * 512:(ic + 1) * 512],
                           xbf[ic * 128:(ic + 1) * 128, 0:512])
                    x2_tick[0] = CNT["x0"]

            for tt in range(NTT):
                with nc.Block() as blk:
                    bi = tt % 2
                    pe0, dv0, ac0 = CNT["pe"], CNT["dv"], CNT["ac"]

                    @blk.sync
                    def _(sy, bi=bi, tt=tt):
                        if tt + 1 < NTT:
                            nb = (tt + 1) % 2
                            for ic in range(16):
                                dq(sy, "x%d" % nb, xsl2[nb][:, ic * 512:(ic + 1) * 512],
                                   xbf[ic * 128:(ic + 1) * 128, (tt + 1) * 512:(tt + 2) * 512])
                            x2_tick[nb] = CNT["x%d" % nb]

                    @blk.tensor
                    def _(te, bi=bi, tt=tt, pe0=pe0, dv0=dv0, ac0=ac0):
                        wt(te, "te", "c0", MARK["consts"])
                        wt(te, "te", "x%d" % bi, x2_tick[bi])
                        wt(te, "te", wsem, w_tick[0])
                        for fc in range(8):
                            if fc >= 2:
                                wt(te, "te", "dv", dv0 + NDV * (fc - 2) + 1)
                            ins = None
                            for ic in range(16):
                                ins = te.matmul(psp[fc % 2][:, :],
                                                wsb[:, ic * 1024 + fc * 128: ic * 1024 + (fc + 1) * 128],
                                                xsl2[bi][:, ic * 512:(ic + 1) * 512],
                                                start=(ic == 0), stop=(ic == 15))
                            ins.then_inc(SEM["pe"], 1)
                            tk("pe")
                            if is_k:
                                if fc >= 1:
                                    wt(te, "te", "dv", dv0 + NDV * (fc - 1) + 6)
                                for cs in range(4):
                                    wt(te, "te", "dv", dv0 + NDV * fc + 2 + cs)
                                    te.transpose(pst[:, cs * 128:(cs + 1) * 128],
                                                 kh[:, cs * 128:(cs + 1) * 128],
                                                 identb[:, :]).then_inc(SEM["pe"], 1)
                                    tk("pe")

                    @blk.scalar
                    def _(ac, bi=bi, tt=tt, pe0=pe0, dv0=dv0, ac0=ac0):
                        for fc in range(8):
                            if fc >= 1:
                                wt(ac, "ac", "dv", dv0 + NDV * (fc - 1) + 1)
                            if which == "q":
                                ac.activation(eg[:, :], Gbuf[:, fc * T + tt * 512: fc * T + (tt + 1) * 512],
                                              AF.Exp, bias=fconst[:, 0:1], scale=NORMI).then_inc(SEM["ac"], 1)
                            else:
                                ac.activation(eg[:, :], Gbuf[:, fc * T + tt * 512: fc * T + (tt + 1) * 512],
                                              AF.Exp, scale=-NORMI).then_inc(SEM["ac"], 1)
                            tk("ac")
                            if fc >= 1:
                                dq(ac, st_sem(fc - 1),
                                   sdram[(fc - 1) * 128: fc * 128, tt * 512:(tt + 1) * 512],
                                   ot[(fc - 1) % 2][:, :])
                        wt(ac, "ac", "dv", dv0 + NDV * 7 + 1)
                        dq(ac, st_sem(7), sdram[7 * 128: 8 * 128, tt * 512:(tt + 1) * 512],
                           ot[1][:, :])

                    @blk.vector
                    def _(v, bi=bi, tt=tt, pe0=pe0, dv0=dv0, ac0=ac0):
                        for fc in range(8):
                            wt(v, "dv", "pe", pe0 + NPE * fc + 1)
                            wt(v, "dv", "ac", ac0 + fc + 1)
                            if tt * 8 + fc - 2 >= 0:
                                t2, f2 = divmod(tt * 8 + fc - 2, 8)
                                wt(v, "dv", st_sem(f2), st_tick(t2, f2))
                            ob = ot[fc % 2]
                            v.tensor_mul(ob[:, :], psp[fc % 2][:, :], eg[:, :]).then_inc(SEM["dv"], 1)
                            tk("dv")
                            if is_k:
                                if fc >= 1:
                                    wt(v, "dv", "pe", pe0 + NPE * (fc - 1) + 5)
                                for cs in range(4):
                                    v.tensor_scalar(kh[:, cs * 128:(cs + 1) * 128],
                                                    ob[:, cs * 128:(cs + 1) * 128],
                                                    gam[:, fc * 16 + tt * 4 + cs: fc * 16 + tt * 4 + cs + 1],
                                                    None, ALU.mult).then_inc(SEM["dv"], 1)
                                    tk("dv")
                                wt(v, "dv", "pe", pe0 + NPE * fc + 5)
                                ins = None
                                for cs in range(4):
                                    ch = tt * 4 + cs
                                    ins = v.tensor_copy(
                                        khT[:, ch * 1024 + fc * 128: ch * 1024 + (fc + 1) * 128],
                                        pst[:, cs * 128:(cs + 1) * 128])
                                ins.then_inc(SEM["dv"], 1)
                                tk("dv")
            px.close()

        if limit >= 2:
            proj_qk("q")
        if limit >= 3:
            proj_qk("k")
        sa2.close()

        nc.leave_named_scope("zz_p23", MARK["sc_p23"], True)
        MARK["sc_p4"] = nc.enter_named_scope("zz_p4", True)[0]
        sb2 = ExitStack()
        v_sb = sb2.enter_context(nc.sbuf_tensor("v_sb", [128, NCH * 2048], BF16))
        p4 = ExitStack()
        wv_sb = p4.enter_context(nc.sbuf_tensor("wv_sb", [128, 16 * 2048], BF16))
        xc4 = p4.enter_context(nc.sbuf_tensor("xc4", [128, 16 * 512], BF16))
        psv = [p4.enter_context(nc.psum_tensor(f"psv{i}", [128, 512], F32)) for i in range(2)]
        pss = [p4.enter_context(nc.psum_tensor(f"pss{i}", [128, 512], F32)) for i in range(2)]
        wv_tick = [0]
        xc4_tick = [0]

        if limit >= 4:
          with nc.Block() as blk:
            @blk.gpsimd
            def _(g):
                for ic in range(16):
                    dq(g, "gv", wv_sb[:, ic * 2048:(ic + 1) * 2048],
                       wvT[ic * 128:(ic + 1) * 128, :])
                wv_tick[0] = CNT["gv"]
                for e in range(6, 8):
                    dq(g, "g1b", w1bf[e, :, :], w1_n[e, :, :])
                MARK["w1bf"] = CNT["g1b"]

        for ch in range(NCH if limit >= 4 else 0):
            with nc.Block() as blk:
                pe0, dv0, ac0 = CNT["pe"], CNT["dv"], CNT["ac"]

                @blk.sync
                def _(sy, ch=ch):
                    if ch % 4 == 0:
                        sem = "x%d" % ((ch // 4) % 2)
                        for ic in range(16):
                            dq(sy, sem, xc4[:, ic * 512:(ic + 1) * 512],
                               xbf[ic * 128:(ic + 1) * 128, ch * 128:(ch + 4) * 128])
                        xc4_tick[0] = CNT[sem]

                @blk.tensor
                def _(te, ch=ch, pe0=pe0, dv0=dv0, ac0=ac0):
                    wt(te, "te", "x%d" % ((ch // 4) % 2), xc4_tick[0])
                    wt(te, "te", "gv", wv_tick[0])
                    co = (ch % 4) * 128
                    for jt in range(4):
                        if jt >= 2:
                            wt(te, "te", "ac", ac0 + jt - 1)
                        ins = None
                        for ic in range(16):
                            ins = te.matmul(psv[jt % 2][:, :],
                                            xc4[:, ic * 512 + co: ic * 512 + co + 128],
                                            wv_sb[:, ic * 2048 + jt * 512: ic * 2048 + (jt + 1) * 512],
                                            start=(ic == 0), stop=(ic == 15))
                        ins.then_inc(SEM["pe"], 1)
                        tk("pe")
                    for h in range(NH):
                        wt(te, "te", "ac", ac0 + h // 4 + 1)
                        if h >= 2:
                            wt(te, "te", "dv", dv0 + 2 * (h - 2) + 2)
                        pb = (h % 2) * 64
                        kcol = ch * 1024 + (h // 2) * 128 + pb
                        te.matmul(pss[h % 2][pb:pb + 64, 0:128],
                                  khT[:, kcol: kcol + 64],
                                  v_sb[:, ch * 2048 + h * 128: ch * 2048 + (h + 1) * 128],
                                  start=True, stop=True).then_inc(SEM["pe"], 1)
                        tk("pe")

                @blk.scalar
                def _(ac, ch=ch, pe0=pe0, dv0=dv0, ac0=ac0):
                    for jt in range(4):
                        wt(ac, "ac", "pe", pe0 + jt + 1)
                        ac.copy(v_sb[:, ch * 2048 + jt * 512: ch * 2048 + (jt + 1) * 512],
                                psv[jt % 2][:, :]).then_inc(SEM["ac"], 1)
                        tk("ac")

                @blk.vector
                def _(v, ch=ch, pe0=pe0, dv0=dv0, ac0=ac0):
                    for h in range(NH):
                        wt(v, "dv", "pe", pe0 + 4 + h + 1)
                        pb = (h % 2) * 64
                        rs = slice(pb, pb + 64)
                        cs_ = slice((h // 2) * 128, (h // 2) * 128 + 128)
                        gcol = (h // 2) * 16 + ch
                        v.tensor_scalar(S_buf[rs, cs_], S_buf[rs, cs_],
                                        gam[rs, gcol:gcol + 1], None, ALU.mult).then_inc(SEM["dv"], 1)
                        tk("dv")
                        v.tensor_add(S_buf[rs, cs_], S_buf[rs, cs_],
                                     pss[h % 2][rs, 0:128]).then_inc(SEM["dv"], 1)
                        tk("dv")

        nc.leave_named_scope("zz_p4", MARK["sc_p4"], True)
        MARK["sc_hand"] = nc.enter_named_scope("zz_hand", True)[0]
        if limit >= 5:
          with nc.Block() as blk:
            @blk.scalar
            def _(ac):
                MARK["contrib"] = dq(ac, "sc0", st_contrib[:, :], S_buf[:, :])

            @blk.gpsimd
            def _(g):
                wt(g, "gp", "sc0", MARK["contrib"])
                g.collective_compute(
                    "AllGather", ALU.bypass,
                    replica_groups=[[0, 1, 2, 3, 4, 5, 6, 7]],
                    ins=[st_contrib.ap().opt()],
                    outs=[st_gather.ap().opt()],
                ).then_inc(SEM["cc"])
                tk("cc")
                g.wait_ge(SEM["cc"], CNT["cc"])
                g.indirect_dma_start(
                    S_init[:, :], None,
                    st_gather[0:128, :],
                    bass.IndirectOffsetOnAxis(idx_sb[:, 0:1], axis=0),
                ).then_inc(SEM["gi"], 16)
                MARK["sinit"] = tk("gi", 16)

            @blk.vector
            def _(v):
                wt(v, "dv", "gi", MARK["sinit"])
                v.tensor_scalar(S_buf[:, :], S_init[:, :], maskS_sb[:, 0:1],
                                None, ALU.mult).then_inc(SEM["dv"], 1)
                MARK["smask"] = tk("dv")

            @blk.scalar
            def _(ac2):
                wt(ac2, "ac", "dv", MARK["smask"])
                ac2.copy(S_bf[0][:, :], S_buf[:, :]).then_inc(SEM["ac"], 1)
                tk("ac")
        p4.close()

        nc.leave_named_scope("zz_hand", MARK["sc_hand"], True)
        MARK["sc_p5"] = nc.enter_named_scope("zz_p5", True)[0]
        p5 = ExitStack()
        qc4 = p5.enter_context(nc.sbuf_tensor("qc4", [128, 8 * 512], BF16))
        kc4 = p5.enter_context(nc.sbuf_tensor("kc4", [128, 8 * 512], BF16))
        Am = [p5.enter_context(nc.sbuf_tensor(f"Am{i}", [128, 128], BF16)) for i in range(2)]
        on_t = [p5.enter_context(nc.sbuf_tensor(f"on_t{i}", [128, 2048], BF16)) for i in range(2)]
        o_raw = p5.enter_context(nc.sbuf_tensor("o_raw", [128, 2048], F32))
        sq_raw = p5.enter_context(nc.sbuf_tensor("sq_raw", [128, 2048], F32))
        r1w = p5.enter_context(nc.sbuf_tensor("r1w", [128, 16], F32))
        r2w = p5.enter_context(nc.sbuf_tensor("r2w", [128, 16], F32))
        muw = p5.enter_context(nc.sbuf_tensor("muw", [128, 16], F32))
        mu2w = p5.enter_context(nc.sbuf_tensor("mu2w", [128, 16], F32))
        t2w = p5.enter_context(nc.sbuf_tensor("t2w", [128, 16], F32))
        varw = p5.enter_context(nc.sbuf_tensor("varw", [128, 16], F32))
        sdw = p5.enter_context(nc.sbuf_tensor("sdw", [128, 16], F32))
        rsw = p5.enter_context(nc.sbuf_tensor("rsw", [128, 16], F32))
        psA = [p5.enter_context(nc.psum_tensor(f"psA{i}", [128, 512], F32)) for i in range(2)]
        psO = [p5.enter_context(nc.psum_tensor(f"psO{i}", [128, 512], F32)) for i in range(2)]
        psS = [p5.enter_context(nc.psum_tensor(f"psS{i}", [128, 512], F32)) for i in range(2)]
        qk_tick = [0]
        onst_base = [CNT["so0"], CNT["so1"]]
        qkst_done = [CNT["s0"], CNT["s1"]]

        for ch in range(NCH if limit >= 6 else 0):
            with nc.Block() as blk:
                pe0, dv0, ac0 = CNT["pe"], CNT["dv"], CNT["ac"]

                if ch == 0:
                    @blk.gpsimd
                    def _(g):
                        for e in range(8):
                            dq(g, "g2b", w2bf[e, :, :], w2r[e, :, :])
                        dq(g, "g2b", sw1bf[:, :], sw1_n[:, :])
                        for e in range(8):
                            dq(g, "g2b", sw2bf[e, :, :], sw2r[e, :, :])
                        MARK["wbf2"] = CNT["g2b"]

                @blk.sync
                def _(sy, ch=ch):
                    if ch % 4 == 0:
                        wt(sy, "sy", "s0", qkst_done[0])
                        wt(sy, "sy", "s1", qkst_done[1])
                        sem = "q%d" % ((ch // 4) % 2)
                        for fc in range(8):
                            dq(sy, sem, qc4[:, fc * 512:(fc + 1) * 512],
                               qt_s[fc * 128:(fc + 1) * 128, ch * 128:(ch + 4) * 128])
                            dq(sy, sem, kc4[:, fc * 512:(fc + 1) * 512],
                               kt_s[fc * 128:(fc + 1) * 128, ch * 128:(ch + 4) * 128])
                        qk_tick[0] = CNT[sem]

                @blk.tensor
                def _(te, ch=ch, pe0=pe0, dv0=dv0, ac0=ac0):
                    wt(te, "te", "q%d" % ((ch // 4) % 2), qk_tick[0])
                    co = (ch % 4) * 128
                    for h in range(NH):
                        pb = (h % 2) * 64
                        qcol = (h // 2) * 512 + co
                        if h >= 2:
                            wt(te, "te", "dv", dv0 + 4 * (h - 2) + 1)
                        te.matmul(psA[h % 2][:, 0:128],
                                  kc4[pb:pb + 64, qcol:qcol + 128],
                                  qc4[pb:pb + 64, qcol:qcol + 128],
                                  start=True, stop=True).then_inc(SEM["pe"], 1)
                        tk("pe")
                        wt(te, "te", "dv", dv0 + 4 * h + 1)
                        if h >= 2:
                            wt(te, "te", "dv", dv0 + 4 * (h - 2) + 4)
                        if h == 0 and ch >= 1:
                            wt(te, "te", "ac", ac0)
                        te.matmul(psO[h % 2][:, 0:128], Am[h % 2][:, :],
                                  v_sb[:, ch * 2048 + h * 128: ch * 2048 + (h + 1) * 128],
                                  start=True, stop=False)
                        te.matmul(psO[h % 2][:, 0:128],
                                  qc4[pb:pb + 64, qcol:qcol + 128],
                                  S_bf[ch % 2][pb:pb + 64, (h // 2) * 128:(h // 2) * 128 + 128],
                                  start=False, stop=True).then_inc(SEM["pe"], 1)
                        tk("pe")
                        if h >= 2:
                            wt(te, "te", "dv", dv0 + 4 * (h - 2) + 3)
                        kcol = ch * 1024 + (h // 2) * 128 + pb
                        te.matmul(psS[h % 2][pb:pb + 64, 0:128],
                                  khT[:, kcol:kcol + 64],
                                  v_sb[:, ch * 2048 + h * 128: ch * 2048 + (h + 1) * 128],
                                  start=True, stop=True).then_inc(SEM["pe"], 1)
                        tk("pe")

                @blk.vector
                def _(v, ch=ch, pe0=pe0, dv0=dv0, ac0=ac0):
                    for h in range(NH):
                        pb = (h % 2) * 64
                        rs = slice(pb, pb + 64)
                        cs_ = slice((h // 2) * 128, (h // 2) * 128 + 128)
                        wt(v, "dv", "pe", pe0 + 3 * h + 1)
                        v.tensor_mul(Am[h % 2][:, :], psA[h % 2][:, 0:128], maskut[:, :]
                                     ).then_inc(SEM["dv"], 1)
                        tk("dv")
                        wt(v, "dv", "pe", pe0 + 3 * h + 3)
                        if h == 0 and ch >= 1:
                            wt(v, "dv", "ac", ac0)
                        gcol = (h // 2) * 16 + ch
                        v.tensor_scalar(S_buf[rs, cs_], S_buf[rs, cs_],
                                        gam[rs, gcol:gcol + 1], None, ALU.mult).then_inc(SEM["dv"], 1)
                        tk("dv")
                        v.tensor_add(S_buf[rs, cs_], S_buf[rs, cs_],
                                     psS[h % 2][rs, 0:128]).then_inc(SEM["dv"], 1)
                        tk("dv")
                        wt(v, "dv", "pe", pe0 + 3 * h + 2)
                        v.tensor_copy(o_raw[:, h * 128:(h + 1) * 128],
                                      psO[h % 2][:, 0:128]).then_inc(SEM["dv"], 1)
                        tk("dv")
                    v.reduce_sum(r1w[:, :], o_raw.rearrange("p (h f) -> p h f", f=128),
                                 axis=AX.X).then_inc(SEM["dv"], 1)
                    tk("dv")
                    wt(v, "dv", "ac", ac0 + 1)
                    v.reduce_sum(r2w[:, :], sq_raw.rearrange("p (h f) -> p h f", f=128),
                                 axis=AX.X).then_inc(SEM["dv"], 1)
                    tk("dv")
                    v.tensor_scalar(muw[:, :], r1w[:, :], 1.0 / 128.0, None, ALU.mult
                                    ).then_inc(SEM["dv"], 1)
                    tk("dv")
                    v.tensor_scalar(t2w[:, :], r2w[:, :], 1.0 / 128.0, None, ALU.mult
                                    ).then_inc(SEM["dv"], 1)
                    tk("dv")
                    wt(v, "dv", "ac", ac0 + 2)
                    v.tensor_tensor(varw[:, :], t2w[:, :], mu2w[:, :], ALU.subtract
                                    ).then_inc(SEM["dv"], 1)
                    tk("dv")
                    wt(v, "dv", "ac", ac0 + 3)
                    v.reciprocal(rsw[:, :], sdw[:, :]).then_inc(SEM["dv"], 1)
                    tk("dv")
                    for h in range(NH):
                        if ch >= 2 and h == 0:
                            wt(v, "dv", "so%d" % (ch % 2),
                               onst_base[ch % 2] + (ch // 2) * 16)
                        v.tensor_scalar(on_t[ch % 2][:, h * 128:(h + 1) * 128],
                                        o_raw[:, h * 128:(h + 1) * 128],
                                        muw[:, h:h + 1], rsw[:, h:h + 1],
                                        ALU.subtract, ALU.mult).then_inc(SEM["dv"], 1)
                        tk("dv")

                @blk.scalar
                def _(ac, ch=ch, pe0=pe0, dv0=dv0, ac0=ac0):
                    wt(ac, "ac", "dv", dv0 + 64)
                    ac.activation(sq_raw[:, :], o_raw[:, :], AF.Square).then_inc(SEM["ac"], 1)
                    tk("ac")
                    wt(ac, "ac", "dv", dv0 + 67)
                    ac.activation(mu2w[:, :], muw[:, :], AF.Square).then_inc(SEM["ac"], 1)
                    tk("ac")
                    wt(ac, "ac", "dv", dv0 + 69)
                    ac.activation(sdw[:, :], varw[:, :], AF.Sqrt, bias=fconst[:, 2:3]).then_inc(SEM["ac"], 1)
                    tk("ac")
                    wt(ac, "ac", "dv", dv0 + 4 * 15 + 3)
                    ac.copy(S_bf[(ch + 1) % 2][:, :], S_buf[:, :]).then_inc(SEM["ac"], 1)
                    tk("ac")
                    wt(ac, "ac", "dv", dv0 + 86)
                    dq(ac, "so%d" % (ch % 2), onorm_s[ch * 128:(ch + 1) * 128, :],
                       on_t[ch % 2][:, :])
        p5.close()
        sb2.close()
        sab.close()

        nc.leave_named_scope("zz_p5", MARK["sc_p5"], True)
        MARK["sc_p6"] = nc.enter_named_scope("zz_p6", True)[0]
        sc2 = ExitStack()
        pre_sb = sc2.enter_context(nc.sbuf_tensor("pre_sb", [128, 16 * 2048], BF16))
        p6 = ExitStack()
        wg_sb = p6.enter_context(nc.sbuf_tensor("wg_sb", [128, 16 * 2048], BF16))
        xsl6 = p6.enter_context(nc.sbuf_tensor("xsl6", [128, 16 * 512], BF16))
        bgb = p6.enter_context(nc.sbuf_tensor("bgb", [128, 2048], F32))
        bg_sb = p6.enter_context(nc.sbuf_tensor("bg_sb", [1, 2048], F32))
        oc2 = [p6.enter_context(nc.sbuf_tensor(f"oc2_{i}", [128, 2048], BF16)) for i in range(2)]
        tmp6 = [p6.enter_context(nc.sbuf_tensor(f"tmp6_{i}", [128, 512], F32)) for i in range(2)]
        sil6 = [p6.enter_context(nc.sbuf_tensor(f"sil6_{i}", [128, 512], F32)) for i in range(2)]
        ptm = [p6.enter_context(nc.sbuf_tensor(f"ptm{i}", [128, 512], BF16)) for i in range(2)]
        pg6 = [p6.enter_context(nc.psum_tensor(f"pg6_{i}", [128, 512], F32)) for i in range(2)]
        ptp = [p6.enter_context(nc.psum_tensor(f"ptp{i}", [128, 1024], BF16)) for i in range(2)]
        pbg = p6.enter_context(nc.psum_tensor("pbg", [128, 512], F32))
        wg_tick = [0]
        x6_tick = [0]
        oc_tick = [0, 0]

        if limit >= 7:
          with nc.Block() as blk:
            @blk.gpsimd
            def _(g):
                for ic in range(16):
                    dq(g, "gg", wg_sb[:, ic * 2048:(ic + 1) * 2048],
                       wgT[ic * 128:(ic + 1) * 128, :])
                wg_tick[0] = CNT["gg"]

            @blk.sync
            def _(sy):
                wt(sy, "sy", "so0", CNT["so0"])
                wt(sy, "sy", "so1", CNT["so1"])
                MARK["bg"] = dq(sy, "c0", bg_sb[:, :], bg_row[:, :])
                oc_tick[0] = dq(sy, "o0", oc2[0][:, :], onorm_s[0:128, :])

            pe_b, dv_b = CNT["pe"], CNT["dv"]

            @blk.tensor
            def _(te, pe_b=pe_b, dv_b=dv_b):
                wt(te, "te", "c0", MARK["bg"])
                for j4 in range(4):
                    if j4 >= 1:
                        wt(te, "te", "dv", dv_b + j4)
                    te.matmul(pbg[:, :], onesr[0:1, :], bg_sb[0:1, j4 * 512:(j4 + 1) * 512],
                              start=True, stop=True).then_inc(SEM["pe"], 1)
                    tk("pe")

            @blk.vector
            def _(v, pe_b=pe_b, dv_b=dv_b):
                for j4 in range(4):
                    wt(v, "dv", "pe", pe_b + j4 + 1)
                    v.tensor_copy(bgb[:, j4 * 512:(j4 + 1) * 512], pbg[:, :]).then_inc(SEM["dv"], 1)
                    tk("dv")

        for tci in range(16 if limit >= 7 else 0):
            with nc.Block() as blk:
                pe0, dv0, ac0 = CNT["pe"], CNT["dv"], CNT["ac"]

                @blk.sync
                def _(sy, tci=tci):
                    if tci % 4 == 0:
                        sem = "x%d" % ((tci // 4) % 2)
                        for ic in range(16):
                            dq(sy, sem, xsl6[:, ic * 512:(ic + 1) * 512],
                               xbf[ic * 128:(ic + 1) * 128, (tci // 4) * 512:(tci // 4 + 1) * 512])
                        x6_tick[0] = CNT[sem]
                    if tci + 1 < 16:
                        oc_tick[(tci + 1) % 2] = dq(
                            sy, "o%d" % ((tci + 1) % 2),
                            oc2[(tci + 1) % 2][:, :],
                            onorm_s[(tci + 1) * 128:(tci + 2) * 128, :])

                @blk.tensor
                def _(te, tci=tci, pe0=pe0, dv0=dv0, ac0=ac0):
                    wt(te, "te", "x%d" % ((tci // 4) % 2), x6_tick[0])
                    wt(te, "te", "gg", wg_tick[0])
                    co = (tci % 4) * 128
                    for ft in range(4):
                        if ft >= 2:
                            wt(te, "te", "dv", dv0 + 6 * (ft - 2) + 1)
                        ins = None
                        for ic in range(16):
                            ins = te.matmul(pg6[ft % 2][:, :],
                                            xsl6[:, ic * 512 + co: ic * 512 + co + 128],
                                            wg_sb[:, ic * 2048 + ft * 512: ic * 2048 + (ft + 1) * 512],
                                            start=(ic == 0), stop=(ic == 15))
                        ins.then_inc(SEM["pe"], 1)
                        tk("pe")
                        wt(te, "te", "dv", dv0 + 6 * ft + 2)
                        if ft >= 2:
                            wt(te, "te", "dv", dv0 + 6 * (ft - 2) + 6)
                        for sub in range(4):
                            te.transpose(ptp[ft % 2][:, sub * 128:(sub + 1) * 128],
                                         ptm[ft % 2][:, sub * 128:(sub + 1) * 128],
                                         identb[:, :]).then_inc(SEM["pe"], 1)
                            tk("pe")

                @blk.vector
                def _(v, tci=tci, pe0=pe0, dv0=dv0, ac0=ac0):
                    for ft in range(4):
                        wt(v, "dv", "pe", pe0 + 5 * ft + 1)
                        if ft >= 2:
                            wt(v, "dv", "ac", ac0 + ft - 1)
                        v.tensor_add(tmp6[ft % 2][:, :], pg6[ft % 2][:, :],
                                     bgb[:, ft * 512:(ft + 1) * 512]).then_inc(SEM["dv"], 1)
                        tk("dv")
                        wt(v, "dv", "ac", ac0 + ft + 1)
                        wt(v, "dv", "o%d" % (tci % 2), oc_tick[tci % 2])
                        if ft >= 2:
                            wt(v, "dv", "pe", pe0 + 5 * (ft - 2) + 5)
                        v.tensor_mul(ptm[ft % 2][:, :], sil6[ft % 2][:, :],
                                     oc2[tci % 2][:, ft * 512:(ft + 1) * 512]).then_inc(SEM["dv"], 1)
                        tk("dv")
                        wt(v, "dv", "pe", pe0 + 5 * ft + 5)
                        for sub in range(4):
                            fcg = ft * 4 + sub
                            v.tensor_copy(
                                pre_sb[:, fcg * 2048 + tci * 128: fcg * 2048 + (tci + 1) * 128],
                                ptp[ft % 2][:, sub * 128:(sub + 1) * 128]).then_inc(SEM["dv"], 1)
                            tk("dv")

                @blk.scalar
                def _(ac, tci=tci, pe0=pe0, dv0=dv0, ac0=ac0):
                    for ft in range(4):
                        wt(ac, "ac", "dv", dv0 + 6 * ft + 1)
                        if ft >= 2:
                            wt(ac, "ac", "dv", dv0 + 6 * (ft - 2) + 2)
                        ac.activation(sil6[ft % 2][:, :], tmp6[ft % 2][:, :], AF.Silu
                                      ).then_inc(SEM["ac"], 1)
                        tk("ac")
        p6.close()

        nc.leave_named_scope("zz_p6", MARK["sc_p6"], True)
        MARK["sc_p7"] = nc.enter_named_scope("zz_p7", True)[0]
        p7 = ExitStack()
        wo_sb = p7.enter_context(nc.sbuf_tensor("wo_sb", [128, 16 * 2048], BF16))
        r1b = p7.enter_context(nc.sbuf_tensor("r1b", [128, 16 * 512], F32))
        xr = [p7.enter_context(nc.sbuf_tensor(f"xr{i}", [128, 512], F32)) for i in range(2)]
        sq7 = [p7.enter_context(nc.sbuf_tensor(f"sq7_{i}", [128, 512], F32)) for i in range(2)]
        x1t = [p7.enter_context(nc.sbuf_tensor(f"x1t{i}", [128, 512], F32)) for i in range(2)]
        t7 = p7.enter_context(nc.sbuf_tensor("t7", [128, 512], F32))
        mu_row = p7.enter_context(nc.sbuf_tensor("mu_row", [1, 512], F32))
        msq_row = p7.enter_context(nc.sbuf_tensor("msq_row", [1, 512], F32))
        mu2_row = p7.enter_context(nc.sbuf_tensor("mu2_row", [1, 512], F32))
        var_row = p7.enter_context(nc.sbuf_tensor("var_row", [1, 512], F32))
        sd_row = p7.enter_context(nc.sbuf_tensor("sd_row", [1, 512], F32))
        rs_row = p7.enter_context(nc.sbuf_tensor("rs_row", [1, 512], F32))
        statacc1 = p7.enter_context(nc.sbuf_tensor("statacc1", [1, 512], F32))
        statacc2 = p7.enter_context(nc.sbuf_tensor("statacc2", [1, 512], F32))
        pw7 = [p7.enter_context(nc.psum_tensor(f"pw7_{i}", [128, 512], F32)) for i in range(3)]
        pstat = [p7.enter_context(nc.psum_tensor(f"pstat{i}", [128, 512], F32)) for i in range(2)]
        pbc = [p7.enter_context(nc.psum_tensor(f"pbc{i}", [128, 512], F32)) for i in range(2)]
        wo_tick = [0]

        if limit >= 8:
          with nc.Block() as blk:
            @blk.gpsimd
            def _(g):
                for ic in range(16):
                    dq(g, "go", wo_sb[:, ic * 2048:(ic + 1) * 2048],
                       woT[ic * 128:(ic + 1) * 128, :])
                wo_tick[0] = CNT["go"]

        for tt in range(NTT if limit >= 8 else 0):
            with nc.Block() as blk:
                pe0, dv0, ac0 = CNT["pe"], CNT["dv"], CNT["ac"]
                xr_b = [CNT["xr0"], CNT["xr1"]]
                xs_b = [CNT["xs0"], CNT["xs1"]]
                te_seq = []
                for fo in range(16):
                    te_seq.append(("W", fo))
                    if fo >= 2:
                        te_seq.append(("r1", fo - 2))
                        te_seq.append(("sq", fo - 2))
                for j in (14, 15):
                    te_seq.append(("r1", j))
                    te_seq.append(("sq", j))
                te_seq += [("bc", 0), ("bc", 1)]
                pe_t = {ev: pe0 + i + 1 for i, ev in enumerate(te_seq)}
                dv_seq = [("ms", 0), ("ms", 1)]
                for fo in range(16):
                    dv_seq += [("ev", fo), ("a1", fo), ("a2", fo)]
                dv_seq += [("mu",), ("msq",), ("var",), ("rec",)]
                for fo in range(16):
                    dv_seq += [("sub", fo), ("mul", fo)]
                dv_t = {ev: dv0 + i + 1 for i, ev in enumerate(dv_seq)}

                @blk.sync
                def _(sy, tt=tt, dv_t=dv_t):
                    for fo in range(16):
                        if fo >= 2:
                            wt(sy, "sy", "dv", dv_t[("ev", fo - 2)])
                        dq(sy, "xr%d" % (fo % 2), xr[fo % 2][:, :],
                           x_fm[fo * 128:(fo + 1) * 128, tt * 512:(tt + 1) * 512])

                @blk.tensor
                def _(te, tt=tt, pe_t=pe_t, dv_t=dv_t, ac0=ac0):
                    wt(te, "te", "go", wo_tick[0])

                    def stat_one(j):
                        wt(te, "te", "dv", dv_t[("ev", j)])
                        if j >= 1:
                            wt(te, "te", "dv", dv_t[("a1", j - 1)])
                        te.matmul(pstat[0][0:1, :], onesc[:, 0:1],
                                  r1b[:, j * 512:(j + 1) * 512],
                                  start=True, stop=True).then_inc(SEM["pe"], 1)
                        tk("pe")
                        wt(te, "te", "ac", ac0 + j + 1)
                        if j >= 1:
                            wt(te, "te", "dv", dv_t[("a2", j - 1)])
                        te.matmul(pstat[1][0:1, :], onesc[:, 0:1],
                                  sq7[j % 2][:, :],
                                  start=True, stop=True).then_inc(SEM["pe"], 1)
                        tk("pe")

                    for fo in range(16):
                        if fo >= 3:
                            wt(te, "te", "dv", dv_t[("ev", fo - 3)])
                        ins = None
                        for ic in range(16):
                            ins = te.matmul(pw7[fo % 3][:, :],
                                            wo_sb[:, ic * 2048 + fo * 128: ic * 2048 + (fo + 1) * 128],
                                            pre_sb[:, ic * 2048 + tt * 512: ic * 2048 + (tt + 1) * 512],
                                            start=(ic == 0), stop=(ic == 15))
                        ins.then_inc(SEM["pe"], 1)
                        tk("pe")
                        if fo >= 2:
                            stat_one(fo - 2)
                    for j in (14, 15):
                        stat_one(j)
                    wt(te, "te", "dv", dv_t[("mu",)])
                    te.matmul(pbc[0][:, :], onesr[0:1, :], mu_row[0:1, :],
                              start=True, stop=True).then_inc(SEM["pe"], 1)
                    tk("pe")
                    wt(te, "te", "dv", dv_t[("rec",)])
                    te.matmul(pbc[1][:, :], onesr[0:1, :], rs_row[0:1, :],
                              start=True, stop=True).then_inc(SEM["pe"], 1)
                    tk("pe")

                @blk.vector
                def _(v, tt=tt, pe_t=pe_t, dv_t=dv_t, ac0=ac0, xr_b=xr_b, xs_b=xs_b):
                    v.memset(statacc1[:, :], 0.0).then_inc(SEM["dv"], 1)
                    tk("dv")
                    v.memset(statacc2[:, :], 0.0).then_inc(SEM["dv"], 1)
                    tk("dv")
                    for fo in range(16):
                        wt(v, "dv", "pe", pe_t[("W", fo)])
                        wt(v, "dv", "xr%d" % (fo % 2), xr_b[fo % 2] + (fo // 2 + 1) * 16)
                        v.tensor_add(r1b[:, fo * 512:(fo + 1) * 512], pw7[fo % 3][:, :],
                                     xr[fo % 2][:, :]).then_inc(SEM["dv"], 1)
                        tk("dv")
                        wt(v, "dv", "pe", pe_t[("r1", fo)])
                        v.tensor_add(statacc1[:, :], statacc1[:, :], pstat[0][0:1, :]
                                     ).then_inc(SEM["dv"], 1)
                        tk("dv")
                        wt(v, "dv", "pe", pe_t[("sq", fo)])
                        v.tensor_add(statacc2[:, :], statacc2[:, :], pstat[1][0:1, :]
                                     ).then_inc(SEM["dv"], 1)
                        tk("dv")
                    v.tensor_scalar(mu_row[:, :], statacc1[:, :], 1.0 / 2048.0, None,
                                    ALU.mult).then_inc(SEM["dv"], 1)
                    tk("dv")
                    v.tensor_scalar(msq_row[:, :], statacc2[:, :], 1.0 / 2048.0, None,
                                    ALU.mult).then_inc(SEM["dv"], 1)
                    tk("dv")
                    wt(v, "dv", "ac", ac0 + 17)
                    v.tensor_sub(var_row[:, :], msq_row[:, :], mu2_row[:, :]).then_inc(SEM["dv"], 1)
                    tk("dv")
                    wt(v, "dv", "ac", ac0 + 18)
                    v.reciprocal(rs_row[:, :], sd_row[:, :]).then_inc(SEM["dv"], 1)
                    tk("dv")
                    wt(v, "dv", "pe", pe_t[("bc", 1)])
                    for fo in range(16):
                        v.tensor_sub(t7[:, :], r1b[:, fo * 512:(fo + 1) * 512], pbc[0][:, :]
                                     ).then_inc(SEM["dv"], 1)
                        tk("dv")
                        if fo >= 2:
                            wt(v, "dv", "xs%d" % (fo % 2),
                               xs_b[fo % 2] + (fo // 2) * 16)
                        v.tensor_mul(x1t[fo % 2][:, :], t7[:, :], pbc[1][:, :]).then_inc(SEM["dv"], 1)
                        tk("dv")

                @blk.scalar
                def _(ac, tt=tt, pe_t=pe_t, dv_t=dv_t, ac0=ac0):
                    for fo in range(16):
                        wt(ac, "ac", "dv", dv_t[("ev", fo)])
                        if fo >= 2:
                            wt(ac, "ac", "pe", pe_t[("sq", fo - 2)])
                        ac.activation(sq7[fo % 2][:, :], r1b[:, fo * 512:(fo + 1) * 512],
                                      AF.Square).then_inc(SEM["ac"], 1)
                        tk("ac")
                    wt(ac, "ac", "dv", dv_t[("mu",)])
                    ac.activation(mu2_row[:, :], mu_row[:, :], AF.Square).then_inc(SEM["ac"], 1)
                    tk("ac")
                    wt(ac, "ac", "dv", dv_t[("var",)])
                    ac.activation(sd_row[:, :], var_row[:, :], AF.Sqrt, bias=fconst[0:1, 2:3]).then_inc(SEM["ac"], 1)
                    tk("ac")
                    for fo in range(16):
                        wt(ac, "ac", "dv", dv_t[("mul", fo)])
                        dq(ac, "xs%d" % (fo % 2),
                           x1f_s[fo * 128:(fo + 1) * 128, tt * 512:(tt + 1) * 512],
                           x1t[fo % 2][:, :])
        p7.close()
        sc2.close()

        nc.leave_named_scope("zz_p7", MARK["sc_p7"], True)
        MARK["sc_p8"] = nc.enter_named_scope("zz_p8", True)[0]
        p8 = ExitStack()
        hbuf = p8.enter_context(nc.sbuf_tensor("hbuf", [128, 64 * 512], BF16))
        shb = p8.enter_context(nc.sbuf_tensor("shb", [128, 16 * 512], BF16))
        x1bt = p8.enter_context(nc.sbuf_tensor("x1bt", [128, 16 * 512], BF16))
        r2b = p8.enter_context(nc.sbuf_tensor("r2b", [128, 16 * 512], F32))
        wub = p8.enter_context(nc.sbuf_tensor("wub", [128, 20480], BF16))
        gate_sb = p8.enter_context(nc.sbuf_tensor("gate_sb", [128, 128], BF16))
        b1_sb = p8.enter_context(nc.sbuf_tensor("b1_sb", [128, 64], F32))
        b2_sb = p8.enter_context(nc.sbuf_tensor("b2_sb", [8, 2048], BF16))
        sb1_sb = p8.enter_context(nc.sbuf_tensor("sb1_sb", [128, 16], F32))
        sb2_sb = p8.enter_context(nc.sbuf_tensor("sb2_sb", [128, 16], F32))
        oh_sb = p8.enter_context(nc.sbuf_tensor("oh_sb", [8, 1024], F32))
        lg4 = p8.enter_context(nc.sbuf_tensor("lg4", [128, 32], F32))
        eqa = p8.enter_context(nc.sbuf_tensor("eqa", [128, 32], F32))
        cum4 = p8.enter_context(nc.sbuf_tensor("cum4", [128, 32], F32))
        c4 = p8.enter_context(nc.sbuf_tensor("c4", [128, 32], F32))
        sel14 = p8.enter_context(nc.sbuf_tensor("sel14", [128, 32], F32))
        sel24 = p8.enter_context(nc.sbuf_tensor("sel24", [128, 32], F32))
        lg24 = p8.enter_context(nc.sbuf_tensor("lg24", [128, 32], F32))
        m1c4 = p8.enter_context(nc.sbuf_tensor("m1c4", [128, 4], F32))
        m2c4 = p8.enter_context(nc.sbuf_tensor("m2c4", [128, 4], F32))
        dcc4 = p8.enter_context(nc.sbuf_tensor("dcc4", [128, 4], F32))
        g1c4 = p8.enter_context(nc.sbuf_tensor("g1c4", [128, 4], F32))
        g2c4 = p8.enter_context(nc.sbuf_tensor("g2c4", [128, 4], F32))
        gates4 = p8.enter_context(nc.sbuf_tensor("gates4", [128, 32], F32))
        ppj = p8.enter_context(nc.sbuf_tensor("ppj", [1, 8], F32))
        gfm8 = p8.enter_context(nc.sbuf_tensor("gfm8", [8, 512], F32))
        g8bf = p8.enter_context(nc.sbuf_tensor("g8bf", [8, 512], BF16))
        geb = p8.enter_context(nc.sbuf_tensor("geb", [128, 512], F32))
        glt = [p8.enter_context(nc.sbuf_tensor(f"glt{i}", [128, 512], F32)) for i in range(2)]
        x1ft = [p8.enter_context(nc.sbuf_tensor(f"x1ft{i}", [128, 512], F32)) for i in range(2)]
        sq8 = x1ft
        t8 = geb
        ot8 = glt
        mu_row8 = p8.enter_context(nc.sbuf_tensor("mu_row8", [1, 512], F32))
        statacc81 = p8.enter_context(nc.sbuf_tensor("statacc81", [1, 512], F32))
        statacc82 = p8.enter_context(nc.sbuf_tensor("statacc82", [1, 512], F32))
        msq_row8 = p8.enter_context(nc.sbuf_tensor("msq_row8", [1, 512], F32))
        mu2_row8 = p8.enter_context(nc.sbuf_tensor("mu2_row8", [1, 512], F32))
        sd_row8 = p8.enter_context(nc.sbuf_tensor("sd_row8", [1, 512], F32))
        rs_row8 = p8.enter_context(nc.sbuf_tensor("rs_row8", [1, 512], F32))
        zer8 = p8.enter_context(nc.sbuf_tensor("zer8", [128, 8], F32))
        psh = [p8.enter_context(nc.psum_tensor(f"psh{i}", [128, 512], F32)) for i in range(8)]

        if limit >= 9:
          with nc.Block() as blk:
            @blk.sync
            def _(sy):
                for dst, src in [(b1_sb, b1_t), (sb1_sb, sb1_t), (sb2_sb, sb2_t),
                                 (oh_sb, oh_d)]:
                    dq(sy, "c1", dst[:, :], src[:, :])
                MARK["p8c"] = CNT["c1"]

            @blk.gpsimd
            def _(g):
                dq(g, "gc8", gate_sb[:, :], gate_t[:, :])
                dq(g, "gc8", b2_sb[:, :], b2_n[:, :])
                MARK["p8g"] = dq(g, "gc8", zer8[:, :], zeros_d[:, 0:8])

        NDVG = 54
        for tt in range(NTT if limit >= 9 else 0):
            x1b_tick = [0]
            with nc.Block() as blk:
                pe0, dv0, ac0 = CNT["pe"], CNT["dv"], CNT["ac"]

                @blk.gpsimd
                def _(g, tt=tt):
                    wt(g, "gp", "xs0", CNT["xs0"])
                    wt(g, "gp", "xs1", CNT["xs1"])
                    for ic in range(16):
                        dq(g, "gx1", x1bt[:, ic * 512:(ic + 1) * 512],
                           x1f_s[ic * 128:(ic + 1) * 128, tt * 512:(tt + 1) * 512])
                    x1b_tick[0] = CNT["gx1"]

                @blk.tensor
                def _(te, tt=tt, pe0=pe0, dv0=dv0, ac0=ac0):
                    wt(te, "te", "gx1", x1b_tick[0])
                    wt(te, "te", "gc8", MARK["p8g"])
                    wt(te, "te", "c1", MARK["p8c"])
                    for tcc in range(4):
                        if tcc >= 2:
                            wt(te, "te", "dv", dv0 + (tcc - 2) + 1)
                        ins = None
                        for ic in range(16):
                            ins = te.matmul(psh[tcc % 2][0:128, 0:8],
                                            x1bt[:, ic * 512 + tcc * 128: ic * 512 + (tcc + 1) * 128],
                                            gate_sb[:, ic * 8:(ic + 1) * 8],
                                            start=(ic == 0), stop=(ic == 15))
                        ins.then_inc(SEM["pe"], 1)
                        tk("pe")
                    for tcc in range(4):
                        wt(te, "te", "dv", dv0 + 45)
                        if tcc >= 1:
                            wt(te, "te", "dv", dv0 + 45 + tcc)
                        te.transpose(psh[2][0:8, 0:128], gates4[:, tcc * 8:(tcc + 1) * 8],
                                     identf[:, :]).then_inc(SEM["pe"], 1)
                        tk("pe")

                @blk.vector
                def _(v, tt=tt, pe0=pe0, dv0=dv0, ac0=ac0):
                    nd = [0]

                    def D(ins):
                        ins.then_inc(SEM["dv"], 1)
                        tk("dv")
                        nd[0] += 1

                    def WA(k):
                        wt(v, "dv", "ac", ac0 + k)

                    for tcc in range(4):
                        wt(v, "dv", "pe", pe0 + tcc + 1)
                        D(v.tensor_copy(lg4[:, tcc * 8:(tcc + 1) * 8], psh[tcc % 2][0:128, 0:8]))
                    WA(1)
                    for tcc in range(4):
                        D(v.reduce_max(m1c4[:, tcc:tcc + 1], lg4[:, tcc * 8:(tcc + 1) * 8], axis=AX.X))
                    WA(2)
                    for tcc in range(4):
                        D(v.tensor_scalar(eqa[:, tcc * 8:(tcc + 1) * 8], lg4[:, tcc * 8:(tcc + 1) * 8],
                                          m1c4[:, tcc:tcc + 1], None, ALU.is_equal))
                    WA(3)
                    for tcc in range(4):
                        D(v.tensor_tensor_scan(cum4[:, tcc * 8:(tcc + 1) * 8],
                                               eqa[:, tcc * 8:(tcc + 1) * 8], zer8[:, :], 0.0,
                                               ALU.add, ALU.add))
                    WA(4)
                    D(v.tensor_scalar(c4[:, :], cum4[:, :], 1.0, None, ALU.is_equal))
                    WA(5)
                    D(v.tensor_mul(sel14[:, :], eqa[:, :], c4[:, :]))
                    WA(6)
                    D(v.scalar_tensor_tensor(lg24[:, :], sel14[:, :], -1e30, lg4[:, :],
                                             ALU.mult, ALU.add))
                    WA(7)
                    D(v.tensor_copy(c4[:, :], lg24[:, :]))
                    WA(8)
                    for tcc in range(4):
                        D(v.reduce_max(m2c4[:, tcc:tcc + 1], lg24[:, tcc * 8:(tcc + 1) * 8], axis=AX.X))
                    WA(9)
                    for tcc in range(4):
                        D(v.tensor_scalar(eqa[:, tcc * 8:(tcc + 1) * 8], lg24[:, tcc * 8:(tcc + 1) * 8],
                                          m2c4[:, tcc:tcc + 1], None, ALU.is_equal))
                    WA(10)
                    for tcc in range(4):
                        D(v.tensor_tensor_scan(cum4[:, tcc * 8:(tcc + 1) * 8],
                                               eqa[:, tcc * 8:(tcc + 1) * 8], zer8[:, :], 0.0,
                                               ALU.add, ALU.add))
                    WA(11)
                    D(v.tensor_scalar(c4[:, :], cum4[:, :], 1.0, None, ALU.is_equal))
                    WA(12)
                    D(v.tensor_mul(sel24[:, :], eqa[:, :], c4[:, :]))
                    WA(13)
                    D(v.tensor_sub(dcc4[:, :], m1c4[:, :], m2c4[:, :]))
                    WA(14)
                    D(v.tensor_scalar(g2c4[:, :], g1c4[:, :], -1.0, 1.0, ALU.mult, ALU.add))
                    WA(15)
                    for tcc in range(4):
                        D(v.tensor_scalar(sel14[:, tcc * 8:(tcc + 1) * 8], sel14[:, tcc * 8:(tcc + 1) * 8],
                                          g1c4[:, tcc:tcc + 1], None, ALU.mult))
                    for tcc in range(4):
                        D(v.tensor_scalar(sel24[:, tcc * 8:(tcc + 1) * 8], sel24[:, tcc * 8:(tcc + 1) * 8],
                                          g2c4[:, tcc:tcc + 1], None, ALU.mult))
                    WA(16)
                    D(v.tensor_add(gates4[:, :], sel14[:, :], sel24[:, :]))
                    for tcc in range(4):
                        wt(v, "dv", "pe", pe0 + 4 + tcc + 1)
                        D(v.tensor_copy(gfm8[0:8, tcc * 128:(tcc + 1) * 128], psh[2][0:8, 0:128]))
                    WA(17)
                    D(v.tensor_copy(g8bf[0:8, :], gfm8[0:8, :]))
                    MARK[("g8bf", tt)] = CNT["dv"]

                @blk.scalar
                def _(ac, tt=tt, pe0=pe0, dv0=dv0, ac0=ac0):
                    for k, dvtick in enumerate([4, 8, 12, 16, 17, 18, 19, 20,
                                                24, 28, 32, 33, 34], start=1):
                        wt(ac, "ac", "dv", dv0 + dvtick)
                        ac.copy(ppj[:, :], ppj[:, :]).then_inc(SEM["ac"], 1)
                        tk("ac")
                    wt(ac, "ac", "dv", dv0 + 35)
                    ac.activation(g1c4[:, :], dcc4[:, :], AF.Sigmoid).then_inc(SEM["ac"], 1)
                    tk("ac")
                    wt(ac, "ac", "dv", dv0 + 36)
                    ac.copy(ppj[:, :], ppj[:, :]).then_inc(SEM["ac"], 1)
                    tk("ac")
                    wt(ac, "ac", "dv", dv0 + 44)
                    ac.copy(ppj[:, :], ppj[:, :]).then_inc(SEM["ac"], 1)
                    tk("ac")
                    wt(ac, "ac", "dv", dv0 + 49)
                    ac.copy(ppj[:, :], ppj[:, :]).then_inc(SEM["ac"], 1)
                    tk("ac")

            if debug_taps and tt == 0:
                MARK["dbg_g8"] = nc.dram_tensor("dbg_g8", [8, 512], F32, kind="ExternalOutput")
                with nc.Block() as blk:
                    @blk.gpsimd
                    def _(g):
                        dq(g, "gi", MARK["dbg_g8"][:, :], gfm8[0:8, :])
            with nc.Block() as blk:
                pe0, dv0, ac0 = CNT["pe"], CNT["dv"], CNT["ac"]
                NPEU = [9] * 8 + [8, 8]
                NDVU = [9] * 8 + [0, 0]
                peb = [pe0 + sum(NPEU[:u]) for u in range(10)]
                dvb = [dv0 + sum(NDVU[:u]) for u in range(10)]
                acb = [ac0 + 8 * u for u in range(10)]
                w_b = [CNT["w0"], CNT["w1"]]

                def half_tick(u, half):
                    i = 2 * u + half
                    return w_b[i % 2] + (i // 2 + 1) * 256

                def grp_tick(u, hc):
                    return peb[u] + (1 if u < 8 else 0) + hc + 1

                @blk.sync
                def _(sy, tt=tt):
                    for u in range(10):
                        for half in range(2):
                            sem = "w%d" % ((2 * u + half) % 2)
                            if u >= 1:
                                wt(sy, "sy", "pe", grp_tick(u - 1, half * 4 + 3))
                            for ic in range(16):
                                dst = wub[:, half * 8192 + ic * 512: half * 8192 + (ic + 1) * 512]
                                if u < 8:
                                    dq(sy, sem, dst,
                                       w1bf[u, ic * 128:(ic + 1) * 128, half * 512:(half + 1) * 512])
                                else:
                                    sh2 = u - 8
                                    dq(sy, sem, dst,
                                       sw1bf[ic * 128:(ic + 1) * 128,
                                             sh2 * 1024 + half * 512: sh2 * 1024 + (half + 1) * 512])

                @blk.tensor
                def _(te, tt=tt):
                    wt(te, "te", "g1b", MARK["w1bf"])
                    wt(te, "te", "g2b", MARK["wbf2"])
                    wt(te, "te", "gx1", x1b_tick[0])
                    for u in range(10):
                        if u < 8:
                            if u >= 1:
                                wt(te, "te", "ac", acb[u - 1] + 8)
                            te.matmul(psh[7][:, :], oh_sb[0:8, u * 128:(u + 1) * 128],
                                      gfm8[0:8, :], start=True, stop=True).then_inc(SEM["pe"], 1)
                            tk("pe")
                        for hc in range(8):
                            half = hc // 4
                            wt(te, "te", "w%d" % ((2 * u + half) % 2), half_tick(u, half))
                            if u >= 1:
                                wt(te, "te", "ac", acb[u - 1] + hc + 1)
                            if hc == 7 and u < 8:
                                wt(te, "te", "dv", dvb[u] + 1)
                            ins = None
                            for ic in range(16):
                                ins = te.matmul(
                                    psh[hc][:, :],
                                    wub[:, half * 8192 + ic * 512 + (hc % 4) * 128:
                                        half * 8192 + ic * 512 + (hc % 4) * 128 + 128],
                                    x1bt[:, ic * 512:(ic + 1) * 512],
                                    start=(ic == 0), stop=(ic == 15))
                            ins.then_inc(SEM["pe"], 1)
                            tk("pe")

                @blk.vector
                def _(v, tt=tt):
                    for u in range(8):
                        wt(v, "dv", "pe", peb[u] + 1)
                        v.tensor_copy(geb[:, :], psh[7][:, :]).then_inc(SEM["dv"], 1)
                        tk("dv")
                        for hc in range(8):
                            wt(v, "dv", "ac", acb[u] + hc + 1)
                            v.tensor_mul(hbuf[:, (u * 8 + hc) * 512:(u * 8 + hc + 1) * 512],
                                         glt[hc % 2][:, :], geb[:, :]).then_inc(SEM["dv"], 1)
                            tk("dv")

                @blk.scalar
                def _(ac, tt=tt):
                    for u in range(10):
                        for hc in range(8):
                            wt(ac, "ac", "pe", grp_tick(u, hc))
                            if u < 8:
                                if u * 8 + hc >= 2:
                                    pu, ph = divmod(u * 8 + hc - 2, 8)
                                    wt(ac, "ac", "dv", dvb[pu] + 2 + ph)
                                ac.activation(glt[hc % 2][:, :], psh[hc][:, :], AF.Gelu,
                                              bias=b1_sb[:, u * 8 + hc: u * 8 + hc + 1]
                                              ).then_inc(SEM["ac"], 1)
                                tk("ac")
                            else:
                                sh2 = u - 8
                                ac.activation(shb[:, (sh2 * 8 + hc) * 512:(sh2 * 8 + hc + 1) * 512],
                                              psh[hc][:, :], AF.Gelu,
                                              bias=sb1_sb[:, sh2 * 8 + hc: sh2 * 8 + hc + 1]
                                              ).then_inc(SEM["ac"], 1)
                                tk("ac")

            if debug_taps and tt == 0:
                MARK["dbg_h"] = nc.dram_tensor("dbg_h", [128, 64 * 512], BF16, kind="ExternalOutput")
                MARK["dbg_sh"] = nc.dram_tensor("dbg_sh", [128, 16 * 512], BF16, kind="ExternalOutput")
                with nc.Block() as blk:
                    @blk.gpsimd
                    def _(g):
                        dq(g, "gi", MARK["dbg_h"][:, :], hbuf[:, :])
                        dq(g, "gi", MARK["dbg_sh"][:, :], shb[:, :])
            with nc.Block() as blk:
                pe0, dv0, ac0 = CNT["pe"], CNT["dv"], CNT["ac"]
                w_b2 = [CNT["w0"], CNT["w1"]]
                xf_b = [CNT["xf0"], CNT["xf1"]]

                @blk.sync
                def _(sy, tt=tt, pe0=pe0):
                    for fo in range(0, 16, 2):
                        if fo >= 2:
                            wt(sy, "sy", "pe", pe0 + (fo - 2) + 1)
                        for j in range(10):
                            dst = wub[:, j * 1024:(j + 1) * 1024]
                            if j < 8:
                                dq(sy, "w0", dst, w2bf[fo // 2, j * 128:(j + 1) * 128, :])
                            else:
                                dq(sy, "w0", dst,
                                   sw2bf[fo // 2, (j - 8) * 128:(j - 7) * 128, :])

                @blk.scalar
                def _(ac, tt=tt, pe0=pe0, dv0=dv0):
                    wt(ac, "ac", "xs0", CNT["xs0"])
                    wt(ac, "ac", "xs1", CNT["xs1"])
                    for fo in range(16):
                        if fo % 2 == 1:
                            if fo >= 2:
                                wt(ac, "ac", "pe", pe0 + (fo - 2) + 1)
                            for j in range(10):
                                dst = wub[:, 10240 + j * 1024: 10240 + (j + 1) * 1024]
                                if j < 8:
                                    dq(ac, "w1", dst,
                                       w2bf[fo // 2, 1024 + j * 128: 1024 + (j + 1) * 128, :])
                                else:
                                    dq(ac, "w1", dst,
                                       sw2bf[fo // 2, 256 + (j - 8) * 128: 256 + (j - 7) * 128, :])
                        if fo >= 2:
                            wt(ac, "ac", "dv", dv0 + (fo - 2) + 1)
                        dq(ac, "xf%d" % (fo % 2), x1ft[fo % 2][:, :],
                           x1f_s[fo * 128:(fo + 1) * 128, tt * 512:(tt + 1) * 512])

                @blk.tensor
                def _(te, tt=tt, pe0=pe0, dv0=dv0, w_b2=w_b2):
                    for fo in range(16):
                        wt(te, "te", "w%d" % (fo % 2),
                           w_b2[fo % 2] + (fo // 2 + 1) * 160)
                        if fo >= 2:
                            wt(te, "te", "dv", dv0 + (fo - 2) + 1)
                        te.matmul(psh[fo % 2][:, :],
                                  b2_sb[0:8, fo * 128: fo * 128 + 128],
                                  g8bf[0:8, :], start=True, stop=False)
                        ins = None
                        wbase = (fo % 2) * 10240
                        for ci in range(80):
                            rhs = hbuf[:, ci * 512:(ci + 1) * 512] if ci < 64 else \
                                shb[:, (ci - 64) * 512:(ci - 63) * 512]
                            ins = te.matmul(
                                psh[fo % 2][:, :],
                                wub[:, wbase + ci * 128: wbase + ci * 128 + 128],
                                rhs, start=False, stop=(ci == 79))
                        ins.then_inc(SEM["pe"], 1)
                        tk("pe")

                @blk.vector
                def _(v, tt=tt, pe0=pe0, dv0=dv0, xf_b=xf_b):
                    for fo in range(16):
                        wt(v, "dv", "pe", pe0 + fo + 1)
                        wt(v, "dv", "xf%d" % (fo % 2), xf_b[fo % 2] + (fo // 2 + 1) * 16)
                        v.scalar_tensor_tensor(r2b[:, fo * 512:(fo + 1) * 512],
                                               psh[fo % 2][:, :], sb2_sb[:, fo:fo + 1],
                                               x1ft[fo % 2][:, :],
                                               ALU.add, ALU.add).then_inc(SEM["dv"], 1)
                        tk("dv")

            if debug_taps and tt == 0:
                MARK["dbg_r2"] = nc.dram_tensor("dbg_r2", [128, 16 * 512], F32, kind="ExternalOutput")
                with nc.Block() as blk:
                    @blk.gpsimd
                    def _(g):
                        dq(g, "gi", MARK["dbg_r2"][:, :], r2b[:, :])
            with nc.Block() as blk:
                pe0, dv0, ac0 = CNT["pe"], CNT["dv"], CNT["ac"]
                os_b = [CNT["os0"], CNT["os1"]]

                @blk.tensor
                def _(te, tt=tt, pe0=pe0, dv0=dv0, ac0=ac0):
                    for fo in range(16):
                        if fo >= 1:
                            wt(te, "te", "dv", dv0 + 2 + 2 * (fo - 1) + 1)
                        te.matmul(psh[2][0:1, :], onesc[:, 0:1],
                                  r2b[:, fo * 512:(fo + 1) * 512],
                                  start=True, stop=True).then_inc(SEM["pe"], 1)
                        tk("pe")
                        wt(te, "te", "ac", ac0 + fo + 1)
                        if fo >= 1:
                            wt(te, "te", "dv", dv0 + 2 + 2 * (fo - 1) + 2)
                        te.matmul(psh[3][0:1, :], onesc[:, 0:1], sq8[fo % 2][:, :],
                                  start=True, stop=True).then_inc(SEM["pe"], 1)
                        tk("pe")
                    wt(te, "te", "dv", dv0 + 35)
                    te.matmul(psh[4][:, :], onesr[0:1, :], mu_row8[0:1, :],
                              start=True, stop=True).then_inc(SEM["pe"], 1)
                    tk("pe")
                    wt(te, "te", "dv", dv0 + 38)
                    te.matmul(psh[5][:, :], onesr[0:1, :], rs_row8[0:1, :],
                              start=True, stop=True).then_inc(SEM["pe"], 1)
                    tk("pe")

                @blk.scalar
                def _(ac, tt=tt, pe0=pe0, dv0=dv0, ac0=ac0):
                    for fo in range(16):
                        if fo >= 2:
                            wt(ac, "ac", "pe", pe0 + 2 * (fo - 2) + 2)
                        ac.activation(sq8[fo % 2][:, :], r2b[:, fo * 512:(fo + 1) * 512],
                                      AF.Square).then_inc(SEM["ac"], 1)
                        tk("ac")
                    wt(ac, "ac", "dv", dv0 + 35)
                    ac.activation(mu2_row8[:, :], mu_row8[:, :], AF.Square).then_inc(SEM["ac"], 1)
                    tk("ac")
                    wt(ac, "ac", "dv", dv0 + 37)
                    ac.activation(sd_row8[:, :], msq_row8[:, :], AF.Sqrt, bias=fconst[0:1, 2:3]
                                  ).then_inc(SEM["ac"], 1)
                    tk("ac")
                    for fo in range(16):
                        wt(ac, "ac", "dv", dv0 + 38 + 2 * fo + 2)
                        dq(ac, "os%d" % (fo % 2),
                           out_fm[fo * 128:(fo + 1) * 128, tt * 512:(tt + 1) * 512],
                           ot8[fo % 2][:, :])

                @blk.vector
                def _(v, tt=tt, pe0=pe0, dv0=dv0, ac0=ac0, os_b=os_b):
                    v.memset(statacc81[:, :], 0.0).then_inc(SEM["dv"], 1)
                    tk("dv")
                    v.memset(statacc82[:, :], 0.0).then_inc(SEM["dv"], 1)
                    tk("dv")
                    for fo in range(16):
                        wt(v, "dv", "pe", pe0 + 2 * fo + 1)
                        v.tensor_add(statacc81[:, :], statacc81[:, :], psh[2][0:1, :]
                                     ).then_inc(SEM["dv"], 1)
                        tk("dv")
                        wt(v, "dv", "pe", pe0 + 2 * fo + 2)
                        v.tensor_add(statacc82[:, :], statacc82[:, :], psh[3][0:1, :]
                                     ).then_inc(SEM["dv"], 1)
                        tk("dv")
                    v.tensor_scalar(mu_row8[:, :], statacc81[:, :], 1.0 / 2048.0, None,
                                    ALU.mult).then_inc(SEM["dv"], 1)
                    tk("dv")
                    v.tensor_scalar(msq_row8[:, :], statacc82[:, :], 1.0 / 2048.0, None,
                                    ALU.mult).then_inc(SEM["dv"], 1)
                    tk("dv")
                    wt(v, "dv", "ac", ac0 + 17)
                    v.tensor_sub(msq_row8[:, :], msq_row8[:, :], mu2_row8[:, :]).then_inc(SEM["dv"], 1)
                    tk("dv")
                    wt(v, "dv", "ac", ac0 + 18)
                    v.reciprocal(rs_row8[:, :], sd_row8[:, :]).then_inc(SEM["dv"], 1)
                    tk("dv")
                    wt(v, "dv", "pe", pe0 + 34)
                    for fo in range(16):
                        v.tensor_sub(t8[:, :], r2b[:, fo * 512:(fo + 1) * 512], psh[4][:, :]
                                     ).then_inc(SEM["dv"], 1)
                        tk("dv")
                        if fo >= 2:
                            wt(v, "dv", "os%d" % (fo % 2),
                               os_b[fo % 2] + (fo // 2) * 16)
                        v.tensor_mul(ot8[fo % 2][:, :], t8[:, :], psh[5][:, :]).then_inc(SEM["dv"], 1)
                        tk("dv")

        if debug_taps:
            dbg_qt = nc.dram_tensor("dbg_qt", [1024, T], BF16, kind="ExternalOutput")
            dbg_kt = nc.dram_tensor("dbg_kt", [1024, T], BF16, kind="ExternalOutput")
            dbg_on = nc.dram_tensor("dbg_on", [T, D], BF16, kind="ExternalOutput")
            dbg_x1 = nc.dram_tensor("dbg_x1", [D, T], F32, kind="ExternalOutput")
            with nc.Block() as blk:
                @blk.gpsimd
                def _(g):
                    for k in DMASEMS:
                        if CNT[k] > 0:
                            wt(g, "gp", k, CNT[k])
                    dq(g, "gi", dbg_qt[:, :], qt_s[:, :])
                    dq(g, "gi", dbg_kt[:, :], kt_s[:, :])
                    dq(g, "gi", dbg_on[:, :], onorm_s[:, :])
                    dq(g, "gi", dbg_x1[:, :], x1f_s[:, :])
                    wt(g, "gp", "gi", CNT["gi"])

        nc.leave_named_scope("zz_p8", MARK["sc_p8"], True)
        with nc.Block() as blk:
            @blk.sync
            def _(sy):
                for k in DMASEMS:
                    if CNT[k] > 0:
                        sy.wait_ge(SEM[k], CNT[k])
        p8.close()

    return nc


_CACHE = {}


def _prep_shared(inp):
    asc = np.ascontiguousarray
    f32 = lambda a: np.asarray(a, dtype=np.float32)
    Wq, Wk = f32(inp["L0_attn_Wq"]), f32(inp["L0_attn_Wk"])
    Wkg1, Wkg2 = f32(inp["L0_attn_Wkg1"]), f32(inp["L0_attn_Wkg2"])
    bkg2 = f32(inp["L0_attn_bkg2"])
    Wv, Wg, Wo = f32(inp["L0_attn_Wv"]), f32(inp["L0_attn_Wg"]), f32(inp["L0_attn_Wo"])
    bg = f32(inp["L0_attn_bg"])
    gate_w = f32(inp["L0_ffn_gate_w"])
    w1, b1 = f32(inp["L0_ffn_w1"]), f32(inp["L0_ffn_b1"])
    w2, b2 = f32(inp["L0_ffn_w2"]), f32(inp["L0_ffn_b2"])
    sw1, sb1 = f32(inp["L0_ffn_shared_w1"]), f32(inp["L0_ffn_shared_b1"])
    sw2, sb2 = f32(inp["L0_ffn_shared_w2"]), f32(inp["L0_ffn_shared_b2"])

    d = {}
    d["wqT"] = asc(Wq.T)
    d["wkT"] = asc(Wk.T)
    d["wkg1"] = asc(Wkg1.T.reshape(16, 128, 16).transpose(1, 0, 2).reshape(128, 256))
    d["wkg2"] = asc(Wkg2.T)
    d["bkg2_t"] = asc(bkg2.reshape(8, 128).T)
    d["wvT"] = asc(Wv.T)
    d["wgT"] = asc(Wg.T)
    d["woT"] = asc(Wo.T)
    d["bg_row"] = asc(bg.reshape(1, D))
    d["w1_n"] = asc(w1)
    d["w2r"] = asc(w2.reshape(8, 8, 128, 8, 2, 128).transpose(3, 4, 0, 2, 1, 5)
                 .reshape(8, 2048, 1024))
    d["sw1_n"] = asc(sw1)
    d["sw2r"] = asc(sw2.reshape(2, 8, 128, 8, 2, 128).transpose(3, 4, 0, 2, 1, 5)
                  .reshape(8, 512, 1024))
    d["gate_t"] = asc(gate_w.reshape(16, 128, 8).transpose(1, 0, 2).reshape(128, 128))
    d["b1_t"] = asc(b1.reshape(8, 8, 128).transpose(2, 0, 1).reshape(128, 64))
    d["b2_n"] = asc(b2)
    d["sb1_t"] = asc(sb1.reshape(16, 128).T)
    d["sb2_t"] = asc(sb2.reshape(16, 128).T)
    d["identb_d"] = np.eye(128, dtype=BFNP)
    d["identf_d"] = np.eye(128, dtype=np.float32)
    s = np.arange(128)[:, None]
    t = np.arange(128)[None, :]
    d["maskut_d"] = (s <= t).astype(np.float32)
    d["zeros_d"] = np.zeros((128, 128), np.float32)
    d["ones_r"] = np.ones((1, 128), np.float32)
    d["ones_c"] = np.ones((128, 1), np.float32)
    oh = np.zeros((8, 1024), np.float32)
    for e in range(8):
        oh[e, e * 128:(e + 1) * 128] = 1.0
    d["oh_d"] = oh
    em = np.zeros((128, 64), np.float32)
    for j in range(8):
        for e in range(8):
            if e > j:
                em[:, j * 8 + e] = 1.0
    d["emask_d"] = em
    fc = np.zeros((128, 4), np.float32)
    fc[:, 0] = LNB
    fc[:, 1] = 1.0
    fc[:, 2] = EPS
    d["fconst_d"] = fc
    return d


def kernel(**inputs):
    x = np.asarray(inputs["L0_attn_x"], dtype=np.float32)
    B, L, _ = x.shape
    shared = _prep_shared(inputs)

    if "nc" not in _CACHE:
        _CACHE["nc"] = build_program()
    nc = _CACHE["nc"]

    in_maps = []
    for c in range(8):
        b, hf = c // 2, c % 2
        m = dict(shared)
        m["x_fm"] = np.ascontiguousarray(x[b, hf * T:(hf + 1) * T, :].T)
        if c % 2 == 1:
            ix = (c - 1) * 128 + np.arange(128, dtype=np.int32)
            ms = np.ones((128, 1), np.float32)
        else:
            ix = c * 128 + np.arange(128, dtype=np.int32)
            ms = np.zeros((128, 1), np.float32)
        m["idx_d"] = ix[:, None]
        m["maskS_d"] = ms
        in_maps.append(m)

    res = run_bass_kernel_spmd(nc, in_maps, core_ids=list(range(8)))
    out = np.empty((B, L, D), np.float32)
    for c in range(8):
        b, hf = c // 2, c % 2
        out[b, hf * T:(hf + 1) * T, :] = res.results[c]["out_fm"].T
    return out


if __name__ == "__main__":
    nc = build_program()
    n = sum(1 for _ in nc.all_instructions())
    print("instructions:", n)


# version 06
# speedup vs baseline: 1.1428x; 1.0016x over previous
import math
import numpy as np
import ml_dtypes
from contextlib import ExitStack
import concourse.bass as bass
import concourse.mybir as mybir
from concourse.bass_utils import run_bass_kernel_spmd

F32 = mybir.dt.float32
BF16 = mybir.dt.bfloat16
I32 = mybir.dt.int32
AF = mybir.ActivationFunctionType
ALU = mybir.AluOpType
AX = mybir.AxisListType
BFNP = ml_dtypes.bfloat16

T = 2048
D = 2048
NH = 16
DK, DV = 64, 128
NCH = 16
NTT = 4
SCALE = 128.0 ** (-0.5)
NORMI = 1.0 / 16.0
EPS = 1e-5
LNB = math.log(SCALE)


def build_program(limit=99, debug_taps=False):
    nc = bass.Bass("TRN2", target_bir_lowering=False, detect_race_conditions=False)

    x_fm = nc.dram_tensor("x_fm", [D, T], F32, kind="ExternalInput")
    wqT = nc.dram_tensor("wqT", [D, 1024], F32, kind="ExternalInput")
    wkT = nc.dram_tensor("wkT", [D, 1024], F32, kind="ExternalInput")
    wkg1 = nc.dram_tensor("wkg1", [128, 256], F32, kind="ExternalInput")
    wkg2 = nc.dram_tensor("wkg2", [16, 1024], F32, kind="ExternalInput")
    bkg2_t = nc.dram_tensor("bkg2_t", [128, 8], F32, kind="ExternalInput")
    wvT = nc.dram_tensor("wvT", [D, D], F32, kind="ExternalInput")
    wgT = nc.dram_tensor("wgT", [D, D], F32, kind="ExternalInput")
    woT = nc.dram_tensor("woT", [D, D], F32, kind="ExternalInput")
    bg_row = nc.dram_tensor("bg_row", [1, D], F32, kind="ExternalInput")
    w1_n = nc.dram_tensor("w1_n", [8, D, 1024], F32, kind="ExternalInput")
    w2r = nc.dram_tensor("w2r", [8, 2048, 1024], F32, kind="ExternalInput")
    sw1_n = nc.dram_tensor("sw1_n", [D, D], F32, kind="ExternalInput")
    sw2r = nc.dram_tensor("sw2r", [8, 512, 1024], F32, kind="ExternalInput")
    gate_t = nc.dram_tensor("gate_t", [128, 128], F32, kind="ExternalInput")
    b1_t = nc.dram_tensor("b1_t", [128, 64], F32, kind="ExternalInput")
    b2_n = nc.dram_tensor("b2_n", [8, D], F32, kind="ExternalInput")
    sb1_t = nc.dram_tensor("sb1_t", [128, 16], F32, kind="ExternalInput")
    sb2_t = nc.dram_tensor("sb2_t", [128, 16], F32, kind="ExternalInput")
    identb_d = nc.dram_tensor("identb_d", [128, 128], BF16, kind="ExternalInput")
    identf_d = nc.dram_tensor("identf_d", [128, 128], F32, kind="ExternalInput")
    maskut_d = nc.dram_tensor("maskut_d", [128, 128], F32, kind="ExternalInput")
    zeros_d = nc.dram_tensor("zeros_d", [128, 128], F32, kind="ExternalInput")
    ones_r = nc.dram_tensor("ones_r", [1, 128], F32, kind="ExternalInput")
    ones_c = nc.dram_tensor("ones_c", [128, 1], F32, kind="ExternalInput")
    oh_d = nc.dram_tensor("oh_d", [8, 1024], F32, kind="ExternalInput")
    emask_d = nc.dram_tensor("emask_d", [128, 64], F32, kind="ExternalInput")
    idx_d = nc.dram_tensor("idx_d", [128, 1], I32, kind="ExternalInput")
    fconst_d = nc.dram_tensor("fconst_d", [128, 4], F32, kind="ExternalInput")
    maskS_d = nc.dram_tensor("maskS_d", [128, 1], F32, kind="ExternalInput")

    out_fm = nc.dram_tensor("out_fm", [D, T], F32, kind="ExternalOutput")

    xbf = nc.dram_tensor("xbf", [D, T], BF16, kind="Internal")
    w1bf = nc.dram_tensor("w1bf", [8, D, 1024], BF16, kind="Internal")
    w2bf = nc.dram_tensor("w2bf", [8, 2048, 1024], BF16, kind="Internal")
    sw1bf = nc.dram_tensor("sw1bf", [D, D], BF16, kind="Internal")
    sw2bf = nc.dram_tensor("sw2bf", [8, 512, 1024], BF16, kind="Internal")
    qt_s = nc.dram_tensor("qt_s", [1024, T], BF16, kind="Internal")
    kt_s = nc.dram_tensor("kt_s", [1024, T], BF16, kind="Internal")
    onorm_s = nc.dram_tensor("onorm_s", [T, D], BF16, kind="Internal")
    x1f_s = nc.dram_tensor("x1f_s", [D, T], F32, kind="Internal")
    st_contrib = nc.dram_tensor("st_contrib", [128, 1024], F32, kind="Internal")
    st_gather = nc.dram_tensor("st_gather", [1024, 1024], F32, kind="Internal")

    DMASEMS = ["c0", "c1", "gx", "gk", "gq", "gkk", "gv", "g1b", "gi", "g2b",
               "gg", "go", "gc8", "gx1", "x0", "x1", "q0", "q1", "o0", "o1",
               "s0", "s1", "so0", "so1", "sc0", "xr0", "xr1", "xs0", "xs1",
               "xf0", "xf1", "w0", "w1", "os0", "os1"]
    CNT = dict(pe=0, dv=0, ac=0, cc=0)
    for _k in DMASEMS:
        CNT[_k] = 0
    MARK = {}
    LASTW = {}
    SEM = {}

    def tk(key, step=1):
        CNT[key] += step
        return CNT[key]

    def wt(eng, ename, key, val):
        if val is None or val <= 0:
            return
        if LASTW.get((ename, key), 0) >= val:
            return
        LASTW[(ename, key)] = val
        eng.wait_ge(SEM[key], val)

    def dq(eng, name, dst, src):
        eng.dma_start(dst, src).then_inc(SEM[name], 16)
        CNT[name] += 16
        return CNT[name]

    ctx = ExitStack()
    with ctx:
        E = ctx.enter_context
        for k in CNT:
            SEM[k] = E(nc.semaphore("sem_" + k))

        identb = E(nc.sbuf_tensor("identb", [128, 128], BF16))
        identf = E(nc.sbuf_tensor("identf", [128, 128], F32))
        onesr = E(nc.sbuf_tensor("onesr", [1, 128], F32))
        onesc = E(nc.sbuf_tensor("onesc", [128, 1], F32))
        idx_sb = E(nc.sbuf_tensor("idx_sb", [128, 1], I32))
        fconst = E(nc.sbuf_tensor("fconst", [128, 4], F32))
        maskS_sb = E(nc.sbuf_tensor("maskS_sb", [128, 1], F32))

        MARK["sc_p0"] = nc.enter_named_scope("zz_p0", True)[0]
        with nc.Block() as blk:
            @blk.sync
            def _(sy):
                for dst, src in [(identb, identb_d), (identf, identf_d),
                                 (onesr, ones_r), (onesc, ones_c),
                                 (idx_sb, idx_d),
                                 (fconst, fconst_d), (maskS_sb, maskS_d)]:
                    dq(sy, "c0", dst[:, :], src[:, :])
                MARK["consts"] = CNT["c0"]

            @blk.gpsimd
            def _(g):
                MARK["xbf"] = dq(g, "gx", xbf[:, :], x_fm[:, :])

        sab = ExitStack()
        khT = sab.enter_context(nc.sbuf_tensor("khT", [128, NCH * 1024], BF16))
        maskut = sab.enter_context(nc.sbuf_tensor("maskut", [128, 128], F32))
        gam = sab.enter_context(nc.sbuf_tensor("gam", [128, 128], F32))
        S_buf = sab.enter_context(nc.sbuf_tensor("S_buf", [128, 1024], F32))
        S_bf = [sab.enter_context(nc.sbuf_tensor(f"S_bf{i}", [128, 1024], BF16)) for i in range(2)]
        S_init = sab.enter_context(nc.sbuf_tensor("S_init", [128, 1024], F32))
        sa2 = ExitStack()
        Gbuf = sa2.enter_context(nc.sbuf_tensor("Gbuf", [128, 8 * T], F32))
        zer = sa2.enter_context(nc.sbuf_tensor("zer", [128, 128], F32))
        wkg1_sb = sa2.enter_context(nc.sbuf_tensor("wkg1_sb", [128, 256], BF16))
        wkg2_sb = sa2.enter_context(nc.sbuf_tensor("wkg2_sb", [16, 1024], BF16))
        bkg2_sb = sa2.enter_context(nc.sbuf_tensor("bkg2_sb", [128, 8], F32))

        with nc.Block() as blk:
            @blk.sync
            def _(sy):
                for dst, src in [(maskut, maskut_d), (zer, zeros_d),
                                 (bkg2_sb, bkg2_t)]:
                    dq(sy, "c1", dst[:, :], src[:, :])
                MARK["consts2"] = CNT["c1"]

            @blk.gpsimd
            def _(g):
                dq(g, "gk", wkg1_sb[:, :], wkg1[:, :])
                MARK["wkg"] = dq(g, "gk", wkg2_sb[:, :], wkg2[:, :])

            @blk.vector
            def _(v):
                v.memset(S_buf[:, :], 0.0).then_inc(SEM["dv"], 1)
                tk("dv")

        nc.leave_named_scope("zz_p0", MARK["sc_p0"], True)
        MARK["sc_p1"] = nc.enter_named_scope("zz_p1", True)[0]
        p1 = ExitStack()
        xsl = [p1.enter_context(nc.sbuf_tensor(f"xsl{i}", [128, 16 * 512], BF16)) for i in range(2)]
        kg1b = p1.enter_context(nc.sbuf_tensor("kg1b", [16, 512], BF16))
        t_mn = p1.enter_context(nc.sbuf_tensor("t_mn", [128, 512], F32))
        t_ab = p1.enter_context(nc.sbuf_tensor("t_ab", [128, 512], F32))
        t_gk = p1.enter_context(nc.sbuf_tensor("t_gk", [128, 512], F32))
        ps1 = p1.enter_context(nc.psum_tensor("ps1", [128, 512], F32))
        ps2 = [p1.enter_context(nc.psum_tensor(f"ps2_{i}", [128, 512], F32)) for i in range(2)]
        xsl_tick = [0, 0]

        def load_xsl(sy, buf_i, tt):
            sem = "x%d" % buf_i
            for ic in range(16):
                dq(sy, sem, xsl[buf_i][:, ic * 512:(ic + 1) * 512],
                   xbf[ic * 128:(ic + 1) * 128, tt * 512:(tt + 1) * 512])
            xsl_tick[buf_i] = CNT[sem]

        for tt in range(NTT if limit >= 1 else 0):
            with nc.Block() as blk:
                bi = tt % 2
                pe0, dv0, ac0 = CNT["pe"], CNT["dv"], CNT["ac"]

                @blk.sync
                def _(sy, bi=bi, tt=tt):
                    if tt == 0:
                        wt(sy, "sy", "gx", MARK["xbf"])
                        load_xsl(sy, 0, 0)
                    if tt + 1 < NTT:
                        load_xsl(sy, (tt + 1) % 2, tt + 1)

                @blk.tensor
                def _(te, bi=bi, tt=tt, pe0=pe0, dv0=dv0, ac0=ac0):
                    wt(te, "te", "x%d" % bi, xsl_tick[bi])
                    wt(te, "te", "gk", MARK["wkg"])
                    ins = None
                    for ic in range(16):
                        ins = te.matmul(ps1[0:16, :], wkg1_sb[:, ic * 16:(ic + 1) * 16],
                                        xsl[bi][:, ic * 512:(ic + 1) * 512],
                                        start=(ic == 0), stop=(ic == 15))
                    ins.then_inc(SEM["pe"], 1)
                    tk("pe")
                    for ic2 in range(8):
                        wt(te, "te", "ac", ac0 + 1)
                        if ic2 >= 2:
                            wt(te, "te", "dv", dv0 + 6 * (ic2 - 2) + 1)
                            wt(te, "te", "ac", ac0 + 1 + 3 * (ic2 - 2) + 1)
                        te.matmul(ps2[ic2 % 2][:, :], wkg2_sb[0:16, ic2 * 128:(ic2 + 1) * 128],
                                  kg1b[:, :], start=True, stop=True).then_inc(SEM["pe"], 1)
                        tk("pe")

                @blk.scalar
                def _(ac, bi=bi, tt=tt, pe0=pe0, dv0=dv0, ac0=ac0):
                    wt(ac, "ac", "c1", MARK["consts2"])
                    wt(ac, "ac", "c0", MARK["consts"])
                    wt(ac, "ac", "pe", pe0 + 1)
                    ac.copy(kg1b[:, :], ps1[0:16, :]).then_inc(SEM["ac"], 1)
                    tk("ac")
                    for ic2 in range(8):
                        wt(ac, "ac", "pe", pe0 + 2 + ic2)
                        if ic2 >= 1:
                            wt(ac, "ac", "dv", dv0 + 6 * (ic2 - 1) + 2)
                        wt(ac, "ac", "dv", dv0 + 6 * ic2 + 1)
                        ac.activation(t_ab[:, :], ps2[ic2 % 2][:, :], AF.Abs,
                                      bias=bkg2_sb[:, ic2:ic2 + 1], scale=1.0).then_inc(SEM["ac"], 1)
                        tk("ac")
                        ac.activation(t_ab[:, :], t_ab[:, :], AF.Exp, scale=-1.0).then_inc(SEM["ac"], 1)
                        tk("ac")
                        ac.activation(t_ab[:, :], t_ab[:, :], AF.Ln, bias=fconst[:, 1:2]).then_inc(SEM["ac"], 1)
                        tk("ac")
                    for ic2 in range(8):
                        gcol = ic2 * T + tt * 512
                        for cs in range(4):
                            wt(ac, "ac", "dv", dv0 + 6 * ic2 + 2 + cs + 1)
                            gi = ic2 * 16 + tt * 4 + cs
                            ac.activation(gam[:, gi:gi + 1],
                                          Gbuf[:, gcol + cs * 128 + 127: gcol + cs * 128 + 128],
                                          AF.Exp, scale=NORMI).then_inc(SEM["ac"], 1)
                            tk("ac")

                @blk.vector
                def _(v, bi=bi, tt=tt, pe0=pe0, dv0=dv0, ac0=ac0):
                    wt(v, "dv", "c1", MARK["consts2"])
                    for ic2 in range(8):
                        wt(v, "dv", "pe", pe0 + 2 + ic2)
                        v.tensor_scalar(t_mn[:, :], ps2[ic2 % 2][:, :], bkg2_sb[:, ic2:ic2 + 1],
                                        0.0, ALU.add, ALU.min).then_inc(SEM["dv"], 1)
                        tk("dv")
                        wt(v, "dv", "ac", ac0 + 1 + 3 * ic2 + 3)
                        v.tensor_sub(t_gk[:, :], t_mn[:, :], t_ab[:, :]).then_inc(SEM["dv"], 1)
                        tk("dv")
                        gcol = ic2 * T + tt * 512
                        for cs in range(4):
                            v.tensor_tensor_scan(
                                Gbuf[:, gcol + cs * 128: gcol + (cs + 1) * 128],
                                t_gk[:, cs * 128:(cs + 1) * 128], zer[:, :], 0.0,
                                ALU.add, ALU.add).then_inc(SEM["dv"], 1)
                            tk("dv")
        p1.close()

        nc.leave_named_scope("zz_p1", MARK["sc_p1"], True)
        MARK["sc_p23"] = nc.enter_named_scope("zz_p23", True)[0]
        def proj_qk(which):
            px = ExitStack()
            wsb = px.enter_context(nc.sbuf_tensor(f"w{which}_sb", [128, 16 * 1024], BF16))
            xsl2 = [px.enter_context(nc.sbuf_tensor(f"x{which}{i}", [128, 16 * 512], BF16)) for i in range(2)]
            eg = px.enter_context(nc.sbuf_tensor(f"eg_{which}", [128, 512], F32))
            ot = [px.enter_context(nc.sbuf_tensor(f"ot_{which}{i}", [128, 512], BF16)) for i in range(2)]
            kh = px.enter_context(nc.sbuf_tensor(f"kh_{which}", [128, 512], BF16))
            psp = [px.enter_context(nc.psum_tensor(f"psp_{which}{i}", [128, 512], F32)) for i in range(2)]
            pst = px.enter_context(nc.psum_tensor(f"pst_{which}", [128, 1024], BF16))
            wd = wqT if which == "q" else wkT
            sdram = qt_s if which == "q" else kt_s
            is_k = (which == "k")
            NDV = 6 if is_k else 1
            NPE = 5 if is_k else 1
            x2_tick = [0, 0]
            w_tick = [0]
            wsem = "gq" if which == "q" else "gkk"
            st_base = [CNT["s0"], CNT["s1"]]

            def st_sem(fc):
                return "s%d" % (fc % 2)

            def st_tick(tt, fc):
                i = tt * 4 + fc // 2
                if tt < 0 or fc < 0:
                    return 0
                return st_base[fc % 2] + (i + 1) * 16

            with nc.Block() as blk:
                @blk.gpsimd
                def _(g):
                    for ic in range(16):
                        dq(g, wsem, wsb[:, ic * 1024:(ic + 1) * 1024],
                           wd[ic * 128:(ic + 1) * 128, :])
                    w_tick[0] = CNT[wsem]
                    es = range(0, 3) if which == "q" else range(3, 6)
                    for e in es:
                        dq(g, "g1b", w1bf[e, :, :], w1_n[e, :, :])

                @blk.sync
                def _(sy):
                    for ic in range(16):
                        dq(sy, "x0", xsl2[0][:, ic * 512:(ic + 1) * 512],
                           xbf[ic * 128:(ic + 1) * 128, 0:512])
                    x2_tick[0] = CNT["x0"]

            for tt in range(NTT):
                with nc.Block() as blk:
                    bi = tt % 2
                    pe0, dv0, ac0 = CNT["pe"], CNT["dv"], CNT["ac"]

                    @blk.sync
                    def _(sy, bi=bi, tt=tt):
                        if tt + 1 < NTT:
                            nb = (tt + 1) % 2
                            for ic in range(16):
                                dq(sy, "x%d" % nb, xsl2[nb][:, ic * 512:(ic + 1) * 512],
                                   xbf[ic * 128:(ic + 1) * 128, (tt + 1) * 512:(tt + 2) * 512])
                            x2_tick[nb] = CNT["x%d" % nb]

                    @blk.tensor
                    def _(te, bi=bi, tt=tt, pe0=pe0, dv0=dv0, ac0=ac0):
                        wt(te, "te", "c0", MARK["consts"])
                        wt(te, "te", "x%d" % bi, x2_tick[bi])
                        wt(te, "te", wsem, w_tick[0])
                        for fc in range(8):
                            if fc >= 2:
                                wt(te, "te", "dv", dv0 + NDV * (fc - 2) + 1)
                            ins = None
                            for ic in range(16):
                                ins = te.matmul(psp[fc % 2][:, :],
                                                wsb[:, ic * 1024 + fc * 128: ic * 1024 + (fc + 1) * 128],
                                                xsl2[bi][:, ic * 512:(ic + 1) * 512],
                                                start=(ic == 0), stop=(ic == 15))
                            ins.then_inc(SEM["pe"], 1)
                            tk("pe")
                            if is_k:
                                if fc >= 1:
                                    wt(te, "te", "dv", dv0 + NDV * (fc - 1) + 6)
                                for cs in range(4):
                                    wt(te, "te", "dv", dv0 + NDV * fc + 2 + cs)
                                    te.transpose(pst[:, cs * 128:(cs + 1) * 128],
                                                 kh[:, cs * 128:(cs + 1) * 128],
                                                 identb[:, :]).then_inc(SEM["pe"], 1)
                                    tk("pe")

                    @blk.scalar
                    def _(ac, bi=bi, tt=tt, pe0=pe0, dv0=dv0, ac0=ac0):
                        for fc in range(8):
                            if fc >= 1:
                                wt(ac, "ac", "dv", dv0 + NDV * (fc - 1) + 1)
                            if which == "q":
                                ac.activation(eg[:, :], Gbuf[:, fc * T + tt * 512: fc * T + (tt + 1) * 512],
                                              AF.Exp, bias=fconst[:, 0:1], scale=NORMI).then_inc(SEM["ac"], 1)
                            else:
                                ac.activation(eg[:, :], Gbuf[:, fc * T + tt * 512: fc * T + (tt + 1) * 512],
                                              AF.Exp, scale=-NORMI).then_inc(SEM["ac"], 1)
                            tk("ac")
                            if fc >= 1:
                                dq(ac, st_sem(fc - 1),
                                   sdram[(fc - 1) * 128: fc * 128, tt * 512:(tt + 1) * 512],
                                   ot[(fc - 1) % 2][:, :])
                        wt(ac, "ac", "dv", dv0 + NDV * 7 + 1)
                        dq(ac, st_sem(7), sdram[7 * 128: 8 * 128, tt * 512:(tt + 1) * 512],
                           ot[1][:, :])

                    @blk.vector
                    def _(v, bi=bi, tt=tt, pe0=pe0, dv0=dv0, ac0=ac0):
                        for fc in range(8):
                            wt(v, "dv", "pe", pe0 + NPE * fc + 1)
                            wt(v, "dv", "ac", ac0 + fc + 1)
                            if tt * 8 + fc - 2 >= 0:
                                t2, f2 = divmod(tt * 8 + fc - 2, 8)
                                wt(v, "dv", st_sem(f2), st_tick(t2, f2))
                            ob = ot[fc % 2]
                            v.tensor_mul(ob[:, :], psp[fc % 2][:, :], eg[:, :]).then_inc(SEM["dv"], 1)
                            tk("dv")
                            if is_k:
                                if fc >= 1:
                                    wt(v, "dv", "pe", pe0 + NPE * (fc - 1) + 5)
                                for cs in range(4):
                                    v.tensor_scalar(kh[:, cs * 128:(cs + 1) * 128],
                                                    ob[:, cs * 128:(cs + 1) * 128],
                                                    gam[:, fc * 16 + tt * 4 + cs: fc * 16 + tt * 4 + cs + 1],
                                                    None, ALU.mult).then_inc(SEM["dv"], 1)
                                    tk("dv")
                                wt(v, "dv", "pe", pe0 + NPE * fc + 5)
                                ins = None
                                for cs in range(4):
                                    ch = tt * 4 + cs
                                    ins = v.tensor_copy(
                                        khT[:, ch * 1024 + fc * 128: ch * 1024 + (fc + 1) * 128],
                                        pst[:, cs * 128:(cs + 1) * 128])
                                ins.then_inc(SEM["dv"], 1)
                                tk("dv")
            px.close()

        if limit >= 2:
            proj_qk("q")
        if limit >= 3:
            proj_qk("k")
        sa2.close()

        nc.leave_named_scope("zz_p23", MARK["sc_p23"], True)
        MARK["sc_p4"] = nc.enter_named_scope("zz_p4", True)[0]
        sb2 = ExitStack()
        v_sb = sb2.enter_context(nc.sbuf_tensor("v_sb", [128, NCH * 2048], BF16))
        p4 = ExitStack()
        wv_sb = p4.enter_context(nc.sbuf_tensor("wv_sb", [128, 16 * 2048], BF16))
        xc4 = p4.enter_context(nc.sbuf_tensor("xc4", [128, 16 * 512], BF16))
        psv = [p4.enter_context(nc.psum_tensor(f"psv{i}", [128, 512], F32)) for i in range(2)]
        pss = [p4.enter_context(nc.psum_tensor(f"pss{i}", [128, 512], F32)) for i in range(2)]
        wv_tick = [0]
        xc4_tick = [0]

        if limit >= 4:
          with nc.Block() as blk:
            @blk.gpsimd
            def _(g):
                for ic in range(16):
                    dq(g, "gv", wv_sb[:, ic * 2048:(ic + 1) * 2048],
                       wvT[ic * 128:(ic + 1) * 128, :])
                wv_tick[0] = CNT["gv"]
                for e in range(6, 8):
                    dq(g, "g1b", w1bf[e, :, :], w1_n[e, :, :])
                MARK["w1bf"] = CNT["g1b"]

        for ch in range(NCH if limit >= 4 else 0):
            with nc.Block() as blk:
                pe0, dv0, ac0 = CNT["pe"], CNT["dv"], CNT["ac"]

                @blk.sync
                def _(sy, ch=ch):
                    if ch % 4 == 0:
                        sem = "x%d" % ((ch // 4) % 2)
                        for ic in range(16):
                            dq(sy, sem, xc4[:, ic * 512:(ic + 1) * 512],
                               xbf[ic * 128:(ic + 1) * 128, ch * 128:(ch + 4) * 128])
                        xc4_tick[0] = CNT[sem]

                @blk.tensor
                def _(te, ch=ch, pe0=pe0, dv0=dv0, ac0=ac0):
                    wt(te, "te", "x%d" % ((ch // 4) % 2), xc4_tick[0])
                    wt(te, "te", "gv", wv_tick[0])
                    co = (ch % 4) * 128
                    for jt in range(4):
                        if jt >= 2:
                            wt(te, "te", "ac", ac0 + jt - 1)
                        ins = None
                        for ic in range(16):
                            ins = te.matmul(psv[jt % 2][:, :],
                                            xc4[:, ic * 512 + co: ic * 512 + co + 128],
                                            wv_sb[:, ic * 2048 + jt * 512: ic * 2048 + (jt + 1) * 512],
                                            start=(ic == 0), stop=(ic == 15))
                        ins.then_inc(SEM["pe"], 1)
                        tk("pe")
                    for h in range(NH):
                        wt(te, "te", "ac", ac0 + h // 4 + 1)
                        if h >= 2:
                            wt(te, "te", "dv", dv0 + 2 * (h - 2) + 2)
                        pb = (h % 2) * 64
                        kcol = ch * 1024 + (h // 2) * 128 + pb
                        te.matmul(pss[h % 2][pb:pb + 64, 0:128],
                                  khT[:, kcol: kcol + 64],
                                  v_sb[:, ch * 2048 + h * 128: ch * 2048 + (h + 1) * 128],
                                  start=True, stop=True).then_inc(SEM["pe"], 1)
                        tk("pe")

                @blk.scalar
                def _(ac, ch=ch, pe0=pe0, dv0=dv0, ac0=ac0):
                    for jt in range(4):
                        wt(ac, "ac", "pe", pe0 + jt + 1)
                        ac.copy(v_sb[:, ch * 2048 + jt * 512: ch * 2048 + (jt + 1) * 512],
                                psv[jt % 2][:, :]).then_inc(SEM["ac"], 1)
                        tk("ac")

                @blk.vector
                def _(v, ch=ch, pe0=pe0, dv0=dv0, ac0=ac0):
                    for h in range(NH):
                        wt(v, "dv", "pe", pe0 + 4 + h + 1)
                        pb = (h % 2) * 64
                        rs = slice(pb, pb + 64)
                        cs_ = slice((h // 2) * 128, (h // 2) * 128 + 128)
                        gcol = (h // 2) * 16 + ch
                        v.tensor_scalar(S_buf[rs, cs_], S_buf[rs, cs_],
                                        gam[rs, gcol:gcol + 1], None, ALU.mult).then_inc(SEM["dv"], 1)
                        tk("dv")
                        v.tensor_add(S_buf[rs, cs_], S_buf[rs, cs_],
                                     pss[h % 2][rs, 0:128]).then_inc(SEM["dv"], 1)
                        tk("dv")

        nc.leave_named_scope("zz_p4", MARK["sc_p4"], True)
        MARK["sc_hand"] = nc.enter_named_scope("zz_hand", True)[0]
        if limit >= 5:
          with nc.Block() as blk:
            @blk.scalar
            def _(ac):
                MARK["contrib"] = dq(ac, "sc0", st_contrib[:, :], S_buf[:, :])

            @blk.gpsimd
            def _(g):
                wt(g, "gp", "sc0", MARK["contrib"])
                g.collective_compute(
                    "AllGather", ALU.bypass,
                    replica_groups=[[0, 1, 2, 3, 4, 5, 6, 7]],
                    ins=[st_contrib.ap().opt()],
                    outs=[st_gather.ap().opt()],
                ).then_inc(SEM["cc"])
                tk("cc")
                g.wait_ge(SEM["cc"], CNT["cc"])
                g.indirect_dma_start(
                    S_init[:, :], None,
                    st_gather[0:128, :],
                    bass.IndirectOffsetOnAxis(idx_sb[:, 0:1], axis=0),
                ).then_inc(SEM["gi"], 16)
                MARK["sinit"] = tk("gi", 16)

            @blk.vector
            def _(v):
                wt(v, "dv", "gi", MARK["sinit"])
                v.tensor_scalar(S_buf[:, :], S_init[:, :], maskS_sb[:, 0:1],
                                None, ALU.mult).then_inc(SEM["dv"], 1)
                MARK["smask"] = tk("dv")

            @blk.scalar
            def _(ac2):
                wt(ac2, "ac", "dv", MARK["smask"])
                ac2.copy(S_bf[0][:, :], S_buf[:, :]).then_inc(SEM["ac"], 1)
                tk("ac")
        p4.close()

        nc.leave_named_scope("zz_hand", MARK["sc_hand"], True)
        MARK["sc_p5"] = nc.enter_named_scope("zz_p5", True)[0]
        p5 = ExitStack()
        qc4 = p5.enter_context(nc.sbuf_tensor("qc4", [128, 8 * 512], BF16))
        kc4 = p5.enter_context(nc.sbuf_tensor("kc4", [128, 8 * 512], BF16))
        Am = [p5.enter_context(nc.sbuf_tensor(f"Am{i}", [128, 128], BF16)) for i in range(2)]
        on_t = [p5.enter_context(nc.sbuf_tensor(f"on_t{i}", [128, 2048], BF16)) for i in range(2)]
        o_raw = p5.enter_context(nc.sbuf_tensor("o_raw", [128, 2048], F32))
        sq_raw = p5.enter_context(nc.sbuf_tensor("sq_raw", [128, 2048], F32))
        r1w = p5.enter_context(nc.sbuf_tensor("r1w", [128, 16], F32))
        r2w = p5.enter_context(nc.sbuf_tensor("r2w", [128, 16], F32))
        muw = p5.enter_context(nc.sbuf_tensor("muw", [128, 16], F32))
        mu2w = p5.enter_context(nc.sbuf_tensor("mu2w", [128, 16], F32))
        t2w = p5.enter_context(nc.sbuf_tensor("t2w", [128, 16], F32))
        varw = p5.enter_context(nc.sbuf_tensor("varw", [128, 16], F32))
        sdw = p5.enter_context(nc.sbuf_tensor("sdw", [128, 16], F32))
        rsw = p5.enter_context(nc.sbuf_tensor("rsw", [128, 16], F32))
        psA = [p5.enter_context(nc.psum_tensor(f"psA{i}", [128, 512], F32)) for i in range(2)]
        psO = [p5.enter_context(nc.psum_tensor(f"psO{i}", [128, 512], F32)) for i in range(2)]
        psS = [p5.enter_context(nc.psum_tensor(f"psS{i}", [128, 512], F32)) for i in range(2)]
        qk_tick = [0]
        onst_base = [CNT["so0"], CNT["so1"]]
        qkst_done = [CNT["s0"], CNT["s1"]]

        for ch in range(NCH if limit >= 6 else 0):
            with nc.Block() as blk:
                pe0, dv0, ac0 = CNT["pe"], CNT["dv"], CNT["ac"]

                @blk.sync
                def _(sy, ch=ch):
                    if ch % 4 == 0:
                        wt(sy, "sy", "s0", qkst_done[0])
                        wt(sy, "sy", "s1", qkst_done[1])
                        sem = "q%d" % ((ch // 4) % 2)
                        for fc in range(8):
                            dq(sy, sem, qc4[:, fc * 512:(fc + 1) * 512],
                               qt_s[fc * 128:(fc + 1) * 128, ch * 128:(ch + 4) * 128])
                            dq(sy, sem, kc4[:, fc * 512:(fc + 1) * 512],
                               kt_s[fc * 128:(fc + 1) * 128, ch * 128:(ch + 4) * 128])
                        qk_tick[0] = CNT[sem]

                @blk.tensor
                def _(te, ch=ch, pe0=pe0, dv0=dv0, ac0=ac0):
                    wt(te, "te", "q%d" % ((ch // 4) % 2), qk_tick[0])
                    co = (ch % 4) * 128
                    for h in range(NH):
                        pb = (h % 2) * 64
                        qcol = (h // 2) * 512 + co
                        if h >= 2:
                            wt(te, "te", "dv", dv0 + 4 * (h - 2) + 1)
                        te.matmul(psA[h % 2][:, 0:128],
                                  kc4[pb:pb + 64, qcol:qcol + 128],
                                  qc4[pb:pb + 64, qcol:qcol + 128],
                                  start=True, stop=True).then_inc(SEM["pe"], 1)
                        tk("pe")
                        wt(te, "te", "dv", dv0 + 4 * h + 1)
                        if h >= 2:
                            wt(te, "te", "dv", dv0 + 4 * (h - 2) + 4)
                        if h == 0 and ch >= 1:
                            wt(te, "te", "ac", ac0)
                        te.matmul(psO[h % 2][:, 0:128], Am[h % 2][:, :],
                                  v_sb[:, ch * 2048 + h * 128: ch * 2048 + (h + 1) * 128],
                                  start=True, stop=False)
                        te.matmul(psO[h % 2][:, 0:128],
                                  qc4[pb:pb + 64, qcol:qcol + 128],
                                  S_bf[ch % 2][pb:pb + 64, (h // 2) * 128:(h // 2) * 128 + 128],
                                  start=False, stop=True).then_inc(SEM["pe"], 1)
                        tk("pe")
                        if h >= 2:
                            wt(te, "te", "dv", dv0 + 4 * (h - 2) + 3)
                        kcol = ch * 1024 + (h // 2) * 128 + pb
                        te.matmul(psS[h % 2][pb:pb + 64, 0:128],
                                  khT[:, kcol:kcol + 64],
                                  v_sb[:, ch * 2048 + h * 128: ch * 2048 + (h + 1) * 128],
                                  start=True, stop=True).then_inc(SEM["pe"], 1)
                        tk("pe")

                @blk.vector
                def _(v, ch=ch, pe0=pe0, dv0=dv0, ac0=ac0):
                    for h in range(NH):
                        pb = (h % 2) * 64
                        rs = slice(pb, pb + 64)
                        cs_ = slice((h // 2) * 128, (h // 2) * 128 + 128)
                        wt(v, "dv", "pe", pe0 + 3 * h + 1)
                        v.tensor_mul(Am[h % 2][:, :], psA[h % 2][:, 0:128], maskut[:, :]
                                     ).then_inc(SEM["dv"], 1)
                        tk("dv")
                        wt(v, "dv", "pe", pe0 + 3 * h + 3)
                        if h == 0 and ch >= 1:
                            wt(v, "dv", "ac", ac0)
                        gcol = (h // 2) * 16 + ch
                        v.tensor_scalar(S_buf[rs, cs_], S_buf[rs, cs_],
                                        gam[rs, gcol:gcol + 1], None, ALU.mult).then_inc(SEM["dv"], 1)
                        tk("dv")
                        v.tensor_add(S_buf[rs, cs_], S_buf[rs, cs_],
                                     psS[h % 2][rs, 0:128]).then_inc(SEM["dv"], 1)
                        tk("dv")
                        wt(v, "dv", "pe", pe0 + 3 * h + 2)
                        v.tensor_copy(o_raw[:, h * 128:(h + 1) * 128],
                                      psO[h % 2][:, 0:128]).then_inc(SEM["dv"], 1)
                        tk("dv")
                    v.reduce_sum(r1w[:, :], o_raw.rearrange("p (h f) -> p h f", f=128),
                                 axis=AX.X).then_inc(SEM["dv"], 1)
                    tk("dv")
                    wt(v, "dv", "ac", ac0 + 1)
                    v.reduce_sum(r2w[:, :], sq_raw.rearrange("p (h f) -> p h f", f=128),
                                 axis=AX.X).then_inc(SEM["dv"], 1)
                    tk("dv")
                    v.tensor_scalar(muw[:, :], r1w[:, :], 1.0 / 128.0, None, ALU.mult
                                    ).then_inc(SEM["dv"], 1)
                    tk("dv")
                    v.tensor_scalar(t2w[:, :], r2w[:, :], 1.0 / 128.0, None, ALU.mult
                                    ).then_inc(SEM["dv"], 1)
                    tk("dv")
                    wt(v, "dv", "ac", ac0 + 2)
                    v.tensor_tensor(varw[:, :], t2w[:, :], mu2w[:, :], ALU.subtract
                                    ).then_inc(SEM["dv"], 1)
                    tk("dv")
                    wt(v, "dv", "ac", ac0 + 3)
                    v.reciprocal(rsw[:, :], sdw[:, :]).then_inc(SEM["dv"], 1)
                    tk("dv")
                    for h in range(NH):
                        if ch >= 2 and h == 0:
                            wt(v, "dv", "so%d" % (ch % 2),
                               onst_base[ch % 2] + (ch // 2) * 16)
                        v.tensor_scalar(on_t[ch % 2][:, h * 128:(h + 1) * 128],
                                        o_raw[:, h * 128:(h + 1) * 128],
                                        muw[:, h:h + 1], rsw[:, h:h + 1],
                                        ALU.subtract, ALU.mult).then_inc(SEM["dv"], 1)
                        tk("dv")

                @blk.scalar
                def _(ac, ch=ch, pe0=pe0, dv0=dv0, ac0=ac0):
                    wt(ac, "ac", "dv", dv0 + 64)
                    ac.activation(sq_raw[:, :], o_raw[:, :], AF.Square).then_inc(SEM["ac"], 1)
                    tk("ac")
                    wt(ac, "ac", "dv", dv0 + 67)
                    ac.activation(mu2w[:, :], muw[:, :], AF.Square).then_inc(SEM["ac"], 1)
                    tk("ac")
                    wt(ac, "ac", "dv", dv0 + 69)
                    ac.activation(sdw[:, :], varw[:, :], AF.Sqrt, bias=fconst[:, 2:3]).then_inc(SEM["ac"], 1)
                    tk("ac")
                    wt(ac, "ac", "dv", dv0 + 4 * 15 + 3)
                    ac.copy(S_bf[(ch + 1) % 2][:, :], S_buf[:, :]).then_inc(SEM["ac"], 1)
                    tk("ac")
                    wt(ac, "ac", "dv", dv0 + 86)
                    dq(ac, "so%d" % (ch % 2), onorm_s[ch * 128:(ch + 1) * 128, :],
                       on_t[ch % 2][:, :])
        p5.close()
        sb2.close()
        sab.close()

        nc.leave_named_scope("zz_p5", MARK["sc_p5"], True)
        MARK["sc_p6"] = nc.enter_named_scope("zz_p6", True)[0]
        sc2 = ExitStack()
        pre_sb = sc2.enter_context(nc.sbuf_tensor("pre_sb", [128, 16 * 2048], BF16))
        p6 = ExitStack()
        wg_sb = p6.enter_context(nc.sbuf_tensor("wg_sb", [128, 16 * 2048], BF16))
        xsl6 = p6.enter_context(nc.sbuf_tensor("xsl6", [128, 16 * 512], BF16))
        bgb = p6.enter_context(nc.sbuf_tensor("bgb", [128, 2048], F32))
        bg_sb = p6.enter_context(nc.sbuf_tensor("bg_sb", [1, 2048], F32))
        oc2 = [p6.enter_context(nc.sbuf_tensor(f"oc2_{i}", [128, 2048], BF16)) for i in range(2)]
        tmp6 = [p6.enter_context(nc.sbuf_tensor(f"tmp6_{i}", [128, 512], F32)) for i in range(2)]
        sil6 = [p6.enter_context(nc.sbuf_tensor(f"sil6_{i}", [128, 512], F32)) for i in range(2)]
        ptm = [p6.enter_context(nc.sbuf_tensor(f"ptm{i}", [128, 512], BF16)) for i in range(2)]
        pg6 = [p6.enter_context(nc.psum_tensor(f"pg6_{i}", [128, 512], F32)) for i in range(2)]
        ptp = [p6.enter_context(nc.psum_tensor(f"ptp{i}", [128, 1024], BF16)) for i in range(2)]
        pbg = p6.enter_context(nc.psum_tensor("pbg", [128, 512], F32))
        wg_tick = [0]
        x6_tick = [0]
        oc_tick = [0, 0]

        if limit >= 7:
          with nc.Block() as blk:
            @blk.gpsimd
            def _(g):
                for ic in range(16):
                    dq(g, "gg", wg_sb[:, ic * 2048:(ic + 1) * 2048],
                       wgT[ic * 128:(ic + 1) * 128, :])
                wg_tick[0] = CNT["gg"]
                for e in range(4):
                    dq(g, "g2b", w2bf[e, :, :], w2r[e, :, :])
                dq(g, "g2b", sw1bf[:, :], sw1_n[:, :])

            @blk.sync
            def _(sy):
                wt(sy, "sy", "so0", CNT["so0"])
                wt(sy, "sy", "so1", CNT["so1"])
                MARK["bg"] = dq(sy, "c0", bg_sb[:, :], bg_row[:, :])
                oc_tick[0] = dq(sy, "o0", oc2[0][:, :], onorm_s[0:128, :])

            pe_b, dv_b = CNT["pe"], CNT["dv"]

            @blk.tensor
            def _(te, pe_b=pe_b, dv_b=dv_b):
                wt(te, "te", "c0", MARK["bg"])
                for j4 in range(4):
                    if j4 >= 1:
                        wt(te, "te", "dv", dv_b + j4)
                    te.matmul(pbg[:, :], onesr[0:1, :], bg_sb[0:1, j4 * 512:(j4 + 1) * 512],
                              start=True, stop=True).then_inc(SEM["pe"], 1)
                    tk("pe")

            @blk.vector
            def _(v, pe_b=pe_b, dv_b=dv_b):
                for j4 in range(4):
                    wt(v, "dv", "pe", pe_b + j4 + 1)
                    v.tensor_copy(bgb[:, j4 * 512:(j4 + 1) * 512], pbg[:, :]).then_inc(SEM["dv"], 1)
                    tk("dv")

        for tci in range(16 if limit >= 7 else 0):
            with nc.Block() as blk:
                pe0, dv0, ac0 = CNT["pe"], CNT["dv"], CNT["ac"]

                @blk.sync
                def _(sy, tci=tci):
                    if tci % 4 == 0:
                        sem = "x%d" % ((tci // 4) % 2)
                        for ic in range(16):
                            dq(sy, sem, xsl6[:, ic * 512:(ic + 1) * 512],
                               xbf[ic * 128:(ic + 1) * 128, (tci // 4) * 512:(tci // 4 + 1) * 512])
                        x6_tick[0] = CNT[sem]
                    if tci + 1 < 16:
                        oc_tick[(tci + 1) % 2] = dq(
                            sy, "o%d" % ((tci + 1) % 2),
                            oc2[(tci + 1) % 2][:, :],
                            onorm_s[(tci + 1) * 128:(tci + 2) * 128, :])

                @blk.tensor
                def _(te, tci=tci, pe0=pe0, dv0=dv0, ac0=ac0):
                    wt(te, "te", "x%d" % ((tci // 4) % 2), x6_tick[0])
                    wt(te, "te", "gg", wg_tick[0])
                    co = (tci % 4) * 128
                    for ft in range(4):
                        if ft >= 2:
                            wt(te, "te", "dv", dv0 + 6 * (ft - 2) + 1)
                        ins = None
                        for ic in range(16):
                            ins = te.matmul(pg6[ft % 2][:, :],
                                            xsl6[:, ic * 512 + co: ic * 512 + co + 128],
                                            wg_sb[:, ic * 2048 + ft * 512: ic * 2048 + (ft + 1) * 512],
                                            start=(ic == 0), stop=(ic == 15))
                        ins.then_inc(SEM["pe"], 1)
                        tk("pe")
                        wt(te, "te", "dv", dv0 + 6 * ft + 2)
                        if ft >= 2:
                            wt(te, "te", "dv", dv0 + 6 * (ft - 2) + 6)
                        for sub in range(4):
                            te.transpose(ptp[ft % 2][:, sub * 128:(sub + 1) * 128],
                                         ptm[ft % 2][:, sub * 128:(sub + 1) * 128],
                                         identb[:, :]).then_inc(SEM["pe"], 1)
                            tk("pe")

                @blk.vector
                def _(v, tci=tci, pe0=pe0, dv0=dv0, ac0=ac0):
                    for ft in range(4):
                        wt(v, "dv", "pe", pe0 + 5 * ft + 1)
                        if ft >= 2:
                            wt(v, "dv", "ac", ac0 + ft - 1)
                        v.tensor_add(tmp6[ft % 2][:, :], pg6[ft % 2][:, :],
                                     bgb[:, ft * 512:(ft + 1) * 512]).then_inc(SEM["dv"], 1)
                        tk("dv")
                        wt(v, "dv", "ac", ac0 + ft + 1)
                        wt(v, "dv", "o%d" % (tci % 2), oc_tick[tci % 2])
                        if ft >= 2:
                            wt(v, "dv", "pe", pe0 + 5 * (ft - 2) + 5)
                        v.tensor_mul(ptm[ft % 2][:, :], sil6[ft % 2][:, :],
                                     oc2[tci % 2][:, ft * 512:(ft + 1) * 512]).then_inc(SEM["dv"], 1)
                        tk("dv")
                        wt(v, "dv", "pe", pe0 + 5 * ft + 5)
                        for sub in range(4):
                            fcg = ft * 4 + sub
                            v.tensor_copy(
                                pre_sb[:, fcg * 2048 + tci * 128: fcg * 2048 + (tci + 1) * 128],
                                ptp[ft % 2][:, sub * 128:(sub + 1) * 128]).then_inc(SEM["dv"], 1)
                            tk("dv")

                @blk.scalar
                def _(ac, tci=tci, pe0=pe0, dv0=dv0, ac0=ac0):
                    for ft in range(4):
                        wt(ac, "ac", "dv", dv0 + 6 * ft + 1)
                        if ft >= 2:
                            wt(ac, "ac", "dv", dv0 + 6 * (ft - 2) + 2)
                        ac.activation(sil6[ft % 2][:, :], tmp6[ft % 2][:, :], AF.Silu
                                      ).then_inc(SEM["ac"], 1)
                        tk("ac")
        p6.close()

        nc.leave_named_scope("zz_p6", MARK["sc_p6"], True)
        MARK["sc_p7"] = nc.enter_named_scope("zz_p7", True)[0]
        p7 = ExitStack()
        wo_sb = p7.enter_context(nc.sbuf_tensor("wo_sb", [128, 16 * 2048], BF16))
        r1b = p7.enter_context(nc.sbuf_tensor("r1b", [128, 16 * 512], F32))
        xr = [p7.enter_context(nc.sbuf_tensor(f"xr{i}", [128, 512], F32)) for i in range(2)]
        sq7 = [p7.enter_context(nc.sbuf_tensor(f"sq7_{i}", [128, 512], F32)) for i in range(2)]
        x1t = [p7.enter_context(nc.sbuf_tensor(f"x1t{i}", [128, 512], F32)) for i in range(2)]
        t7 = p7.enter_context(nc.sbuf_tensor("t7", [128, 512], F32))
        mu_row = p7.enter_context(nc.sbuf_tensor("mu_row", [1, 512], F32))
        msq_row = p7.enter_context(nc.sbuf_tensor("msq_row", [1, 512], F32))
        mu2_row = p7.enter_context(nc.sbuf_tensor("mu2_row", [1, 512], F32))
        var_row = p7.enter_context(nc.sbuf_tensor("var_row", [1, 512], F32))
        sd_row = p7.enter_context(nc.sbuf_tensor("sd_row", [1, 512], F32))
        rs_row = p7.enter_context(nc.sbuf_tensor("rs_row", [1, 512], F32))
        statacc1 = p7.enter_context(nc.sbuf_tensor("statacc1", [1, 512], F32))
        statacc2 = p7.enter_context(nc.sbuf_tensor("statacc2", [1, 512], F32))
        pw7 = [p7.enter_context(nc.psum_tensor(f"pw7_{i}", [128, 512], F32)) for i in range(3)]
        pstat = [p7.enter_context(nc.psum_tensor(f"pstat{i}", [128, 512], F32)) for i in range(2)]
        pbc = [p7.enter_context(nc.psum_tensor(f"pbc{i}", [128, 512], F32)) for i in range(2)]
        wo_tick = [0]

        if limit >= 8:
          with nc.Block() as blk:
            @blk.gpsimd
            def _(g):
                for ic in range(16):
                    dq(g, "go", wo_sb[:, ic * 2048:(ic + 1) * 2048],
                       woT[ic * 128:(ic + 1) * 128, :])
                wo_tick[0] = CNT["go"]
                for e in range(4, 8):
                    dq(g, "g2b", w2bf[e, :, :], w2r[e, :, :])
                for e in range(8):
                    dq(g, "g2b", sw2bf[e, :, :], sw2r[e, :, :])
                MARK["wbf2"] = CNT["g2b"]

        for tt in range(NTT if limit >= 8 else 0):
            with nc.Block() as blk:
                pe0, dv0, ac0 = CNT["pe"], CNT["dv"], CNT["ac"]
                xr_b = [CNT["xr0"], CNT["xr1"]]
                xs_b = [CNT["xs0"], CNT["xs1"]]
                te_seq = []
                for fo in range(16):
                    te_seq.append(("W", fo))
                    if fo >= 2:
                        te_seq.append(("r1", fo - 2))
                        te_seq.append(("sq", fo - 2))
                for j in (14, 15):
                    te_seq.append(("r1", j))
                    te_seq.append(("sq", j))
                te_seq += [("bc", 0), ("bc", 1)]
                pe_t = {ev: pe0 + i + 1 for i, ev in enumerate(te_seq)}
                dv_seq = [("ms", 0), ("ms", 1)]
                for fo in range(16):
                    dv_seq += [("ev", fo), ("a1", fo), ("a2", fo)]
                dv_seq += [("mu",), ("msq",), ("var",), ("rec",)]
                for fo in range(16):
                    dv_seq += [("sub", fo), ("mul", fo)]
                dv_t = {ev: dv0 + i + 1 for i, ev in enumerate(dv_seq)}

                @blk.sync
                def _(sy, tt=tt, dv_t=dv_t):
                    for fo in range(16):
                        if fo >= 2:
                            wt(sy, "sy", "dv", dv_t[("ev", fo - 2)])
                        dq(sy, "xr%d" % (fo % 2), xr[fo % 2][:, :],
                           x_fm[fo * 128:(fo + 1) * 128, tt * 512:(tt + 1) * 512])

                @blk.tensor
                def _(te, tt=tt, pe_t=pe_t, dv_t=dv_t, ac0=ac0):
                    wt(te, "te", "go", wo_tick[0])

                    def stat_one(j):
                        wt(te, "te", "dv", dv_t[("ev", j)])
                        if j >= 1:
                            wt(te, "te", "dv", dv_t[("a1", j - 1)])
                        te.matmul(pstat[0][0:1, :], onesc[:, 0:1],
                                  r1b[:, j * 512:(j + 1) * 512],
                                  start=True, stop=True).then_inc(SEM["pe"], 1)
                        tk("pe")
                        wt(te, "te", "ac", ac0 + j + 1)
                        if j >= 1:
                            wt(te, "te", "dv", dv_t[("a2", j - 1)])
                        te.matmul(pstat[1][0:1, :], onesc[:, 0:1],
                                  sq7[j % 2][:, :],
                                  start=True, stop=True).then_inc(SEM["pe"], 1)
                        tk("pe")

                    for fo in range(16):
                        if fo >= 3:
                            wt(te, "te", "dv", dv_t[("ev", fo - 3)])
                        ins = None
                        for ic in range(16):
                            ins = te.matmul(pw7[fo % 3][:, :],
                                            wo_sb[:, ic * 2048 + fo * 128: ic * 2048 + (fo + 1) * 128],
                                            pre_sb[:, ic * 2048 + tt * 512: ic * 2048 + (tt + 1) * 512],
                                            start=(ic == 0), stop=(ic == 15))
                        ins.then_inc(SEM["pe"], 1)
                        tk("pe")
                        if fo >= 2:
                            stat_one(fo - 2)
                    for j in (14, 15):
                        stat_one(j)
                    wt(te, "te", "dv", dv_t[("mu",)])
                    te.matmul(pbc[0][:, :], onesr[0:1, :], mu_row[0:1, :],
                              start=True, stop=True).then_inc(SEM["pe"], 1)
                    tk("pe")
                    wt(te, "te", "dv", dv_t[("rec",)])
                    te.matmul(pbc[1][:, :], onesr[0:1, :], rs_row[0:1, :],
                              start=True, stop=True).then_inc(SEM["pe"], 1)
                    tk("pe")

                @blk.vector
                def _(v, tt=tt, pe_t=pe_t, dv_t=dv_t, ac0=ac0, xr_b=xr_b, xs_b=xs_b):
                    v.memset(statacc1[:, :], 0.0).then_inc(SEM["dv"], 1)
                    tk("dv")
                    v.memset(statacc2[:, :], 0.0).then_inc(SEM["dv"], 1)
                    tk("dv")
                    for fo in range(16):
                        wt(v, "dv", "pe", pe_t[("W", fo)])
                        wt(v, "dv", "xr%d" % (fo % 2), xr_b[fo % 2] + (fo // 2 + 1) * 16)
                        v.tensor_add(r1b[:, fo * 512:(fo + 1) * 512], pw7[fo % 3][:, :],
                                     xr[fo % 2][:, :]).then_inc(SEM["dv"], 1)
                        tk("dv")
                        wt(v, "dv", "pe", pe_t[("r1", fo)])
                        v.tensor_add(statacc1[:, :], statacc1[:, :], pstat[0][0:1, :]
                                     ).then_inc(SEM["dv"], 1)
                        tk("dv")
                        wt(v, "dv", "pe", pe_t[("sq", fo)])
                        v.tensor_add(statacc2[:, :], statacc2[:, :], pstat[1][0:1, :]
                                     ).then_inc(SEM["dv"], 1)
                        tk("dv")
                    v.tensor_scalar(mu_row[:, :], statacc1[:, :], 1.0 / 2048.0, None,
                                    ALU.mult).then_inc(SEM["dv"], 1)
                    tk("dv")
                    v.tensor_scalar(msq_row[:, :], statacc2[:, :], 1.0 / 2048.0, None,
                                    ALU.mult).then_inc(SEM["dv"], 1)
                    tk("dv")
                    wt(v, "dv", "ac", ac0 + 17)
                    v.tensor_sub(var_row[:, :], msq_row[:, :], mu2_row[:, :]).then_inc(SEM["dv"], 1)
                    tk("dv")
                    wt(v, "dv", "ac", ac0 + 18)
                    v.reciprocal(rs_row[:, :], sd_row[:, :]).then_inc(SEM["dv"], 1)
                    tk("dv")
                    wt(v, "dv", "pe", pe_t[("bc", 1)])
                    for fo in range(16):
                        v.tensor_sub(t7[:, :], r1b[:, fo * 512:(fo + 1) * 512], pbc[0][:, :]
                                     ).then_inc(SEM["dv"], 1)
                        tk("dv")
                        if fo >= 2:
                            wt(v, "dv", "xs%d" % (fo % 2),
                               xs_b[fo % 2] + (fo // 2) * 16)
                        v.tensor_mul(x1t[fo % 2][:, :], t7[:, :], pbc[1][:, :]).then_inc(SEM["dv"], 1)
                        tk("dv")

                @blk.scalar
                def _(ac, tt=tt, pe_t=pe_t, dv_t=dv_t, ac0=ac0):
                    for fo in range(16):
                        wt(ac, "ac", "dv", dv_t[("ev", fo)])
                        if fo >= 2:
                            wt(ac, "ac", "pe", pe_t[("sq", fo - 2)])
                        ac.activation(sq7[fo % 2][:, :], r1b[:, fo * 512:(fo + 1) * 512],
                                      AF.Square).then_inc(SEM["ac"], 1)
                        tk("ac")
                    wt(ac, "ac", "dv", dv_t[("mu",)])
                    ac.activation(mu2_row[:, :], mu_row[:, :], AF.Square).then_inc(SEM["ac"], 1)
                    tk("ac")
                    wt(ac, "ac", "dv", dv_t[("var",)])
                    ac.activation(sd_row[:, :], var_row[:, :], AF.Sqrt, bias=fconst[0:1, 2:3]).then_inc(SEM["ac"], 1)
                    tk("ac")
                    for fo in range(16):
                        wt(ac, "ac", "dv", dv_t[("mul", fo)])
                        dq(ac, "xs%d" % (fo % 2),
                           x1f_s[fo * 128:(fo + 1) * 128, tt * 512:(tt + 1) * 512],
                           x1t[fo % 2][:, :])
        p7.close()
        sc2.close()

        nc.leave_named_scope("zz_p7", MARK["sc_p7"], True)
        MARK["sc_p8"] = nc.enter_named_scope("zz_p8", True)[0]
        p8 = ExitStack()
        hbuf = p8.enter_context(nc.sbuf_tensor("hbuf", [128, 64 * 512], BF16))
        shb = p8.enter_context(nc.sbuf_tensor("shb", [128, 16 * 512], BF16))
        x1bt = p8.enter_context(nc.sbuf_tensor("x1bt", [128, 16 * 512], BF16))
        r2b = p8.enter_context(nc.sbuf_tensor("r2b", [128, 16 * 512], F32))
        wub = p8.enter_context(nc.sbuf_tensor("wub", [128, 20480], BF16))
        gate_sb = p8.enter_context(nc.sbuf_tensor("gate_sb", [128, 128], BF16))
        b1_sb = p8.enter_context(nc.sbuf_tensor("b1_sb", [128, 64], F32))
        b2_sb = p8.enter_context(nc.sbuf_tensor("b2_sb", [8, 2048], BF16))
        sb1_sb = p8.enter_context(nc.sbuf_tensor("sb1_sb", [128, 16], F32))
        sb2_sb = p8.enter_context(nc.sbuf_tensor("sb2_sb", [128, 16], F32))
        oh_sb = p8.enter_context(nc.sbuf_tensor("oh_sb", [8, 1024], F32))
        lg4 = p8.enter_context(nc.sbuf_tensor("lg4", [128, 32], F32))
        eqa = p8.enter_context(nc.sbuf_tensor("eqa", [128, 32], F32))
        cum4 = p8.enter_context(nc.sbuf_tensor("cum4", [128, 32], F32))
        c4 = p8.enter_context(nc.sbuf_tensor("c4", [128, 32], F32))
        sel14 = p8.enter_context(nc.sbuf_tensor("sel14", [128, 32], F32))
        sel24 = p8.enter_context(nc.sbuf_tensor("sel24", [128, 32], F32))
        lg24 = p8.enter_context(nc.sbuf_tensor("lg24", [128, 32], F32))
        m1c4 = p8.enter_context(nc.sbuf_tensor("m1c4", [128, 4], F32))
        m2c4 = p8.enter_context(nc.sbuf_tensor("m2c4", [128, 4], F32))
        dcc4 = p8.enter_context(nc.sbuf_tensor("dcc4", [128, 4], F32))
        g1c4 = p8.enter_context(nc.sbuf_tensor("g1c4", [128, 4], F32))
        g2c4 = p8.enter_context(nc.sbuf_tensor("g2c4", [128, 4], F32))
        gates4 = p8.enter_context(nc.sbuf_tensor("gates4", [128, 32], F32))
        ppj = p8.enter_context(nc.sbuf_tensor("ppj", [1, 8], F32))
        gfm8 = p8.enter_context(nc.sbuf_tensor("gfm8", [8, 512], F32))
        g8bf = p8.enter_context(nc.sbuf_tensor("g8bf", [8, 512], BF16))
        geb = p8.enter_context(nc.sbuf_tensor("geb", [128, 512], F32))
        glt = [p8.enter_context(nc.sbuf_tensor(f"glt{i}", [128, 512], F32)) for i in range(2)]
        x1ft = [p8.enter_context(nc.sbuf_tensor(f"x1ft{i}", [128, 512], F32)) for i in range(2)]
        sq8 = x1ft
        t8 = geb
        ot8 = glt
        mu_row8 = p8.enter_context(nc.sbuf_tensor("mu_row8", [1, 512], F32))
        statacc81 = p8.enter_context(nc.sbuf_tensor("statacc81", [1, 512], F32))
        statacc82 = p8.enter_context(nc.sbuf_tensor("statacc82", [1, 512], F32))
        msq_row8 = p8.enter_context(nc.sbuf_tensor("msq_row8", [1, 512], F32))
        mu2_row8 = p8.enter_context(nc.sbuf_tensor("mu2_row8", [1, 512], F32))
        sd_row8 = p8.enter_context(nc.sbuf_tensor("sd_row8", [1, 512], F32))
        rs_row8 = p8.enter_context(nc.sbuf_tensor("rs_row8", [1, 512], F32))
        zer8 = p8.enter_context(nc.sbuf_tensor("zer8", [128, 8], F32))
        psh = [p8.enter_context(nc.psum_tensor(f"psh{i}", [128, 512], F32)) for i in range(8)]

        if limit >= 9:
          with nc.Block() as blk:
            @blk.sync
            def _(sy):
                for dst, src in [(b1_sb, b1_t), (sb1_sb, sb1_t), (sb2_sb, sb2_t),
                                 (oh_sb, oh_d)]:
                    dq(sy, "c1", dst[:, :], src[:, :])
                MARK["p8c"] = CNT["c1"]

            @blk.gpsimd
            def _(g):
                dq(g, "gc8", gate_sb[:, :], gate_t[:, :])
                dq(g, "gc8", b2_sb[:, :], b2_n[:, :])
                MARK["p8g"] = dq(g, "gc8", zer8[:, :], zeros_d[:, 0:8])

        NDVG = 54
        for tt in range(NTT if limit >= 9 else 0):
            x1b_tick = [0]
            MARK["sc_g"] = nc.enter_named_scope("zz_g", True)[0]
            with nc.Block() as blk:
                pe0, dv0, ac0 = CNT["pe"], CNT["dv"], CNT["ac"]

                @blk.gpsimd
                def _(g, tt=tt):
                    wt(g, "gp", "xs0", CNT["xs0"])
                    wt(g, "gp", "xs1", CNT["xs1"])
                    for ic in range(16):
                        dq(g, "gx1", x1bt[:, ic * 512:(ic + 1) * 512],
                           x1f_s[ic * 128:(ic + 1) * 128, tt * 512:(tt + 1) * 512])
                    x1b_tick[0] = CNT["gx1"]

                @blk.tensor
                def _(te, tt=tt, pe0=pe0, dv0=dv0, ac0=ac0):
                    wt(te, "te", "gx1", x1b_tick[0])
                    wt(te, "te", "gc8", MARK["p8g"])
                    wt(te, "te", "c1", MARK["p8c"])
                    for tcc in range(4):
                        if tcc >= 2:
                            wt(te, "te", "dv", dv0 + (tcc - 2) + 1)
                        ins = None
                        for ic in range(16):
                            ins = te.matmul(psh[tcc % 2][0:128, 0:8],
                                            x1bt[:, ic * 512 + tcc * 128: ic * 512 + (tcc + 1) * 128],
                                            gate_sb[:, ic * 8:(ic + 1) * 8],
                                            start=(ic == 0), stop=(ic == 15))
                        ins.then_inc(SEM["pe"], 1)
                        tk("pe")
                    for tcc in range(4):
                        wt(te, "te", "dv", dv0 + 45)
                        if tcc >= 1:
                            wt(te, "te", "dv", dv0 + 45 + tcc)
                        te.transpose(psh[2][0:8, 0:128], gates4[:, tcc * 8:(tcc + 1) * 8],
                                     identf[:, :]).then_inc(SEM["pe"], 1)
                        tk("pe")

                @blk.vector
                def _(v, tt=tt, pe0=pe0, dv0=dv0, ac0=ac0):
                    nd = [0]

                    def D(ins):
                        ins.then_inc(SEM["dv"], 1)
                        tk("dv")
                        nd[0] += 1

                    def WA(k):
                        wt(v, "dv", "ac", ac0 + k)

                    for tcc in range(4):
                        wt(v, "dv", "pe", pe0 + tcc + 1)
                        D(v.tensor_copy(lg4[:, tcc * 8:(tcc + 1) * 8], psh[tcc % 2][0:128, 0:8]))
                    WA(1)
                    for tcc in range(4):
                        D(v.reduce_max(m1c4[:, tcc:tcc + 1], lg4[:, tcc * 8:(tcc + 1) * 8], axis=AX.X))
                    WA(2)
                    for tcc in range(4):
                        D(v.tensor_scalar(eqa[:, tcc * 8:(tcc + 1) * 8], lg4[:, tcc * 8:(tcc + 1) * 8],
                                          m1c4[:, tcc:tcc + 1], None, ALU.is_equal))
                    WA(3)
                    for tcc in range(4):
                        D(v.tensor_tensor_scan(cum4[:, tcc * 8:(tcc + 1) * 8],
                                               eqa[:, tcc * 8:(tcc + 1) * 8], zer8[:, :], 0.0,
                                               ALU.add, ALU.add))
                    WA(4)
                    D(v.tensor_scalar(c4[:, :], cum4[:, :], 1.0, None, ALU.is_equal))
                    WA(5)
                    D(v.tensor_mul(sel14[:, :], eqa[:, :], c4[:, :]))
                    WA(6)
                    D(v.scalar_tensor_tensor(lg24[:, :], sel14[:, :], -1e30, lg4[:, :],
                                             ALU.mult, ALU.add))
                    WA(7)
                    D(v.tensor_copy(c4[:, :], lg24[:, :]))
                    WA(8)
                    for tcc in range(4):
                        D(v.reduce_max(m2c4[:, tcc:tcc + 1], lg24[:, tcc * 8:(tcc + 1) * 8], axis=AX.X))
                    WA(9)
                    for tcc in range(4):
                        D(v.tensor_scalar(eqa[:, tcc * 8:(tcc + 1) * 8], lg24[:, tcc * 8:(tcc + 1) * 8],
                                          m2c4[:, tcc:tcc + 1], None, ALU.is_equal))
                    WA(10)
                    for tcc in range(4):
                        D(v.tensor_tensor_scan(cum4[:, tcc * 8:(tcc + 1) * 8],
                                               eqa[:, tcc * 8:(tcc + 1) * 8], zer8[:, :], 0.0,
                                               ALU.add, ALU.add))
                    WA(11)
                    D(v.tensor_scalar(c4[:, :], cum4[:, :], 1.0, None, ALU.is_equal))
                    WA(12)
                    D(v.tensor_mul(sel24[:, :], eqa[:, :], c4[:, :]))
                    WA(13)
                    D(v.tensor_sub(dcc4[:, :], m1c4[:, :], m2c4[:, :]))
                    WA(14)
                    D(v.tensor_scalar(g2c4[:, :], g1c4[:, :], -1.0, 1.0, ALU.mult, ALU.add))
                    WA(15)
                    for tcc in range(4):
                        D(v.tensor_scalar(sel14[:, tcc * 8:(tcc + 1) * 8], sel14[:, tcc * 8:(tcc + 1) * 8],
                                          g1c4[:, tcc:tcc + 1], None, ALU.mult))
                    for tcc in range(4):
                        D(v.tensor_scalar(sel24[:, tcc * 8:(tcc + 1) * 8], sel24[:, tcc * 8:(tcc + 1) * 8],
                                          g2c4[:, tcc:tcc + 1], None, ALU.mult))
                    WA(16)
                    D(v.tensor_add(gates4[:, :], sel14[:, :], sel24[:, :]))
                    for tcc in range(4):
                        wt(v, "dv", "pe", pe0 + 4 + tcc + 1)
                        D(v.tensor_copy(gfm8[0:8, tcc * 128:(tcc + 1) * 128], psh[2][0:8, 0:128]))
                    WA(17)
                    D(v.tensor_copy(g8bf[0:8, :], gfm8[0:8, :]))
                    MARK[("g8bf", tt)] = CNT["dv"]

                @blk.scalar
                def _(ac, tt=tt, pe0=pe0, dv0=dv0, ac0=ac0):
                    for k, dvtick in enumerate([4, 8, 12, 16, 17, 18, 19, 20,
                                                24, 28, 32, 33, 34], start=1):
                        wt(ac, "ac", "dv", dv0 + dvtick)
                        ac.copy(ppj[:, :], ppj[:, :]).then_inc(SEM["ac"], 1)
                        tk("ac")
                    wt(ac, "ac", "dv", dv0 + 35)
                    ac.activation(g1c4[:, :], dcc4[:, :], AF.Sigmoid).then_inc(SEM["ac"], 1)
                    tk("ac")
                    wt(ac, "ac", "dv", dv0 + 36)
                    ac.copy(ppj[:, :], ppj[:, :]).then_inc(SEM["ac"], 1)
                    tk("ac")
                    wt(ac, "ac", "dv", dv0 + 44)
                    ac.copy(ppj[:, :], ppj[:, :]).then_inc(SEM["ac"], 1)
                    tk("ac")
                    wt(ac, "ac", "dv", dv0 + 49)
                    ac.copy(ppj[:, :], ppj[:, :]).then_inc(SEM["ac"], 1)
                    tk("ac")

            if debug_taps and tt == 0:
                MARK["dbg_g8"] = nc.dram_tensor("dbg_g8", [8, 512], F32, kind="ExternalOutput")
                with nc.Block() as blk:
                    @blk.gpsimd
                    def _(g):
                        dq(g, "gi", MARK["dbg_g8"][:, :], gfm8[0:8, :])
            nc.leave_named_scope("zz_g", MARK["sc_g"], True)
            MARK["sc_w1"] = nc.enter_named_scope("zz_w1", True)[0]
            with nc.Block() as blk:
                pe0, dv0, ac0 = CNT["pe"], CNT["dv"], CNT["ac"]
                NPEU = [9] * 8 + [8, 8]
                NDVU = [9] * 8 + [0, 0]
                peb = [pe0 + sum(NPEU[:u]) for u in range(10)]
                dvb = [dv0 + sum(NDVU[:u]) for u in range(10)]
                acb = [ac0 + 8 * u for u in range(10)]
                w_b = [CNT["w0"], CNT["w1"]]

                def half_tick(u, half):
                    i = 2 * u + half
                    return w_b[i % 2] + (i // 2 + 1) * 256

                def grp_tick(u, hc):
                    return peb[u] + (1 if u < 8 else 0) + hc + 1

                @blk.sync
                def _(sy, tt=tt):
                    for u in range(10):
                        for half in range(2):
                            sem = "w%d" % ((2 * u + half) % 2)
                            if u >= 1:
                                wt(sy, "sy", "pe", grp_tick(u - 1, half * 4 + 3))
                            for ic in range(16):
                                dst = wub[:, half * 8192 + ic * 512: half * 8192 + (ic + 1) * 512]
                                if u < 8:
                                    dq(sy, sem, dst,
                                       w1bf[u, ic * 128:(ic + 1) * 128, half * 512:(half + 1) * 512])
                                else:
                                    sh2 = u - 8
                                    dq(sy, sem, dst,
                                       sw1bf[ic * 128:(ic + 1) * 128,
                                             sh2 * 1024 + half * 512: sh2 * 1024 + (half + 1) * 512])

                @blk.tensor
                def _(te, tt=tt):
                    wt(te, "te", "g1b", MARK["w1bf"])
                    wt(te, "te", "g2b", MARK["wbf2"])
                    wt(te, "te", "gx1", x1b_tick[0])
                    for u in range(10):
                        if u < 8:
                            if u >= 1:
                                wt(te, "te", "ac", acb[u - 1] + 8)
                            te.matmul(psh[7][:, :], oh_sb[0:8, u * 128:(u + 1) * 128],
                                      gfm8[0:8, :], start=True, stop=True).then_inc(SEM["pe"], 1)
                            tk("pe")
                        for hc in range(8):
                            half = hc // 4
                            wt(te, "te", "w%d" % ((2 * u + half) % 2), half_tick(u, half))
                            if u >= 1:
                                wt(te, "te", "ac", acb[u - 1] + hc + 1)
                            if hc == 7 and u < 8:
                                wt(te, "te", "dv", dvb[u] + 1)
                            ins = None
                            for ic in range(16):
                                ins = te.matmul(
                                    psh[hc][:, :],
                                    wub[:, half * 8192 + ic * 512 + (hc % 4) * 128:
                                        half * 8192 + ic * 512 + (hc % 4) * 128 + 128],
                                    x1bt[:, ic * 512:(ic + 1) * 512],
                                    start=(ic == 0), stop=(ic == 15))
                            ins.then_inc(SEM["pe"], 1)
                            tk("pe")

                @blk.vector
                def _(v, tt=tt):
                    for u in range(8):
                        wt(v, "dv", "pe", peb[u] + 1)
                        v.tensor_copy(geb[:, :], psh[7][:, :]).then_inc(SEM["dv"], 1)
                        tk("dv")
                        for hc in range(8):
                            wt(v, "dv", "ac", acb[u] + hc + 1)
                            v.tensor_mul(hbuf[:, (u * 8 + hc) * 512:(u * 8 + hc + 1) * 512],
                                         glt[hc % 2][:, :], geb[:, :]).then_inc(SEM["dv"], 1)
                            tk("dv")

                @blk.scalar
                def _(ac, tt=tt):
                    for u in range(10):
                        for hc in range(8):
                            wt(ac, "ac", "pe", grp_tick(u, hc))
                            if u < 8:
                                if u * 8 + hc >= 2:
                                    pu, ph = divmod(u * 8 + hc - 2, 8)
                                    wt(ac, "ac", "dv", dvb[pu] + 2 + ph)
                                ac.activation(glt[hc % 2][:, :], psh[hc][:, :], AF.Gelu,
                                              bias=b1_sb[:, u * 8 + hc: u * 8 + hc + 1]
                                              ).then_inc(SEM["ac"], 1)
                                tk("ac")
                            else:
                                sh2 = u - 8
                                ac.activation(shb[:, (sh2 * 8 + hc) * 512:(sh2 * 8 + hc + 1) * 512],
                                              psh[hc][:, :], AF.Gelu,
                                              bias=sb1_sb[:, sh2 * 8 + hc: sh2 * 8 + hc + 1]
                                              ).then_inc(SEM["ac"], 1)
                                tk("ac")

            nc.leave_named_scope("zz_w1", MARK["sc_w1"], True)
            MARK["sc_w2"] = nc.enter_named_scope("zz_w2", True)[0]
            if debug_taps and tt == 0:
                MARK["dbg_h"] = nc.dram_tensor("dbg_h", [128, 64 * 512], BF16, kind="ExternalOutput")
                MARK["dbg_sh"] = nc.dram_tensor("dbg_sh", [128, 16 * 512], BF16, kind="ExternalOutput")
                with nc.Block() as blk:
                    @blk.gpsimd
                    def _(g):
                        dq(g, "gi", MARK["dbg_h"][:, :], hbuf[:, :])
                        dq(g, "gi", MARK["dbg_sh"][:, :], shb[:, :])
            with nc.Block() as blk:
                pe0, dv0, ac0 = CNT["pe"], CNT["dv"], CNT["ac"]
                w_b2 = [CNT["w0"], CNT["w1"]]
                xf_b = [CNT["xf0"], CNT["xf1"]]

                @blk.sync
                def _(sy, tt=tt, pe0=pe0):
                    for fo in range(0, 16, 2):
                        if fo >= 2:
                            wt(sy, "sy", "pe", pe0 + (fo - 2) + 1)
                        for j in range(10):
                            dst = wub[:, j * 1024:(j + 1) * 1024]
                            if j < 8:
                                dq(sy, "w0", dst, w2bf[fo // 2, j * 128:(j + 1) * 128, :])
                            else:
                                dq(sy, "w0", dst,
                                   sw2bf[fo // 2, (j - 8) * 128:(j - 7) * 128, :])

                @blk.scalar
                def _(ac, tt=tt, pe0=pe0, dv0=dv0):
                    wt(ac, "ac", "xs0", CNT["xs0"])
                    wt(ac, "ac", "xs1", CNT["xs1"])
                    for fo in range(16):
                        if fo % 2 == 1:
                            if fo >= 2:
                                wt(ac, "ac", "pe", pe0 + (fo - 2) + 1)
                            for j in range(10):
                                dst = wub[:, 10240 + j * 1024: 10240 + (j + 1) * 1024]
                                if j < 8:
                                    dq(ac, "w1", dst,
                                       w2bf[fo // 2, 1024 + j * 128: 1024 + (j + 1) * 128, :])
                                else:
                                    dq(ac, "w1", dst,
                                       sw2bf[fo // 2, 256 + (j - 8) * 128: 256 + (j - 7) * 128, :])
                        if fo >= 2:
                            wt(ac, "ac", "dv", dv0 + (fo - 2) + 1)
                        dq(ac, "xf%d" % (fo % 2), x1ft[fo % 2][:, :],
                           x1f_s[fo * 128:(fo + 1) * 128, tt * 512:(tt + 1) * 512])

                @blk.tensor
                def _(te, tt=tt, pe0=pe0, dv0=dv0, w_b2=w_b2):
                    for fo in range(16):
                        wt(te, "te", "w%d" % (fo % 2),
                           w_b2[fo % 2] + (fo // 2 + 1) * 160)
                        if fo >= 2:
                            wt(te, "te", "dv", dv0 + (fo - 2) + 1)
                        te.matmul(psh[fo % 2][:, :],
                                  b2_sb[0:8, fo * 128: fo * 128 + 128],
                                  g8bf[0:8, :], start=True, stop=False)
                        ins = None
                        wbase = (fo % 2) * 10240
                        for ci in range(80):
                            rhs = hbuf[:, ci * 512:(ci + 1) * 512] if ci < 64 else \
                                shb[:, (ci - 64) * 512:(ci - 63) * 512]
                            ins = te.matmul(
                                psh[fo % 2][:, :],
                                wub[:, wbase + ci * 128: wbase + ci * 128 + 128],
                                rhs, start=False, stop=(ci == 79))
                        ins.then_inc(SEM["pe"], 1)
                        tk("pe")

                @blk.vector
                def _(v, tt=tt, pe0=pe0, dv0=dv0, xf_b=xf_b):
                    for fo in range(16):
                        wt(v, "dv", "pe", pe0 + fo + 1)
                        wt(v, "dv", "xf%d" % (fo % 2), xf_b[fo % 2] + (fo // 2 + 1) * 16)
                        v.scalar_tensor_tensor(r2b[:, fo * 512:(fo + 1) * 512],
                                               psh[fo % 2][:, :], sb2_sb[:, fo:fo + 1],
                                               x1ft[fo % 2][:, :],
                                               ALU.add, ALU.add).then_inc(SEM["dv"], 1)
                        tk("dv")

            nc.leave_named_scope("zz_w2", MARK["sc_w2"], True)
            MARK["sc_ln"] = nc.enter_named_scope("zz_ln", True)[0]
            if debug_taps and tt == 0:
                MARK["dbg_r2"] = nc.dram_tensor("dbg_r2", [128, 16 * 512], F32, kind="ExternalOutput")
                with nc.Block() as blk:
                    @blk.gpsimd
                    def _(g):
                        dq(g, "gi", MARK["dbg_r2"][:, :], r2b[:, :])
            with nc.Block() as blk:
                pe0, dv0, ac0 = CNT["pe"], CNT["dv"], CNT["ac"]
                os_b = [CNT["os0"], CNT["os1"]]

                @blk.tensor
                def _(te, tt=tt, pe0=pe0, dv0=dv0, ac0=ac0):
                    for fo in range(16):
                        if fo >= 1:
                            wt(te, "te", "dv", dv0 + 2 + 2 * (fo - 1) + 1)
                        te.matmul(psh[2][0:1, :], onesc[:, 0:1],
                                  r2b[:, fo * 512:(fo + 1) * 512],
                                  start=True, stop=True).then_inc(SEM["pe"], 1)
                        tk("pe")
                        wt(te, "te", "ac", ac0 + fo + 1)
                        if fo >= 1:
                            wt(te, "te", "dv", dv0 + 2 + 2 * (fo - 1) + 2)
                        te.matmul(psh[3][0:1, :], onesc[:, 0:1], sq8[fo % 2][:, :],
                                  start=True, stop=True).then_inc(SEM["pe"], 1)
                        tk("pe")
                    wt(te, "te", "dv", dv0 + 35)
                    te.matmul(psh[4][:, :], onesr[0:1, :], mu_row8[0:1, :],
                              start=True, stop=True).then_inc(SEM["pe"], 1)
                    tk("pe")
                    wt(te, "te", "dv", dv0 + 38)
                    te.matmul(psh[5][:, :], onesr[0:1, :], rs_row8[0:1, :],
                              start=True, stop=True).then_inc(SEM["pe"], 1)
                    tk("pe")

                @blk.scalar
                def _(ac, tt=tt, pe0=pe0, dv0=dv0, ac0=ac0):
                    for fo in range(16):
                        if fo >= 2:
                            wt(ac, "ac", "pe", pe0 + 2 * (fo - 2) + 2)
                        ac.activation(sq8[fo % 2][:, :], r2b[:, fo * 512:(fo + 1) * 512],
                                      AF.Square).then_inc(SEM["ac"], 1)
                        tk("ac")
                    wt(ac, "ac", "dv", dv0 + 35)
                    ac.activation(mu2_row8[:, :], mu_row8[:, :], AF.Square).then_inc(SEM["ac"], 1)
                    tk("ac")
                    wt(ac, "ac", "dv", dv0 + 37)
                    ac.activation(sd_row8[:, :], msq_row8[:, :], AF.Sqrt, bias=fconst[0:1, 2:3]
                                  ).then_inc(SEM["ac"], 1)
                    tk("ac")
                    for fo in range(16):
                        wt(ac, "ac", "dv", dv0 + 38 + 2 * fo + 2)
                        dq(ac, "os%d" % (fo % 2),
                           out_fm[fo * 128:(fo + 1) * 128, tt * 512:(tt + 1) * 512],
                           ot8[fo % 2][:, :])

                @blk.vector
                def _(v, tt=tt, pe0=pe0, dv0=dv0, ac0=ac0, os_b=os_b):
                    v.memset(statacc81[:, :], 0.0).then_inc(SEM["dv"], 1)
                    tk("dv")
                    v.memset(statacc82[:, :], 0.0).then_inc(SEM["dv"], 1)
                    tk("dv")
                    for fo in range(16):
                        wt(v, "dv", "pe", pe0 + 2 * fo + 1)
                        v.tensor_add(statacc81[:, :], statacc81[:, :], psh[2][0:1, :]
                                     ).then_inc(SEM["dv"], 1)
                        tk("dv")
                        wt(v, "dv", "pe", pe0 + 2 * fo + 2)
                        v.tensor_add(statacc82[:, :], statacc82[:, :], psh[3][0:1, :]
                                     ).then_inc(SEM["dv"], 1)
                        tk("dv")
                    v.tensor_scalar(mu_row8[:, :], statacc81[:, :], 1.0 / 2048.0, None,
                                    ALU.mult).then_inc(SEM["dv"], 1)
                    tk("dv")
                    v.tensor_scalar(msq_row8[:, :], statacc82[:, :], 1.0 / 2048.0, None,
                                    ALU.mult).then_inc(SEM["dv"], 1)
                    tk("dv")
                    wt(v, "dv", "ac", ac0 + 17)
                    v.tensor_sub(msq_row8[:, :], msq_row8[:, :], mu2_row8[:, :]).then_inc(SEM["dv"], 1)
                    tk("dv")
                    wt(v, "dv", "ac", ac0 + 18)
                    v.reciprocal(rs_row8[:, :], sd_row8[:, :]).then_inc(SEM["dv"], 1)
                    tk("dv")
                    wt(v, "dv", "pe", pe0 + 34)
                    for fo in range(16):
                        v.tensor_sub(t8[:, :], r2b[:, fo * 512:(fo + 1) * 512], psh[4][:, :]
                                     ).then_inc(SEM["dv"], 1)
                        tk("dv")
                        if fo >= 2:
                            wt(v, "dv", "os%d" % (fo % 2),
                               os_b[fo % 2] + (fo // 2) * 16)
                        v.tensor_mul(ot8[fo % 2][:, :], t8[:, :], psh[5][:, :]).then_inc(SEM["dv"], 1)
                        tk("dv")
            nc.leave_named_scope("zz_ln", MARK["sc_ln"], True)

        if debug_taps:
            dbg_qt = nc.dram_tensor("dbg_qt", [1024, T], BF16, kind="ExternalOutput")
            dbg_kt = nc.dram_tensor("dbg_kt", [1024, T], BF16, kind="ExternalOutput")
            dbg_on = nc.dram_tensor("dbg_on", [T, D], BF16, kind="ExternalOutput")
            dbg_x1 = nc.dram_tensor("dbg_x1", [D, T], F32, kind="ExternalOutput")
            with nc.Block() as blk:
                @blk.gpsimd
                def _(g):
                    for k in DMASEMS:
                        if CNT[k] > 0:
                            wt(g, "gp", k, CNT[k])
                    dq(g, "gi", dbg_qt[:, :], qt_s[:, :])
                    dq(g, "gi", dbg_kt[:, :], kt_s[:, :])
                    dq(g, "gi", dbg_on[:, :], onorm_s[:, :])
                    dq(g, "gi", dbg_x1[:, :], x1f_s[:, :])
                    wt(g, "gp", "gi", CNT["gi"])

        nc.leave_named_scope("zz_p8", MARK["sc_p8"], True)
        with nc.Block() as blk:
            @blk.sync
            def _(sy):
                for k in DMASEMS:
                    if CNT[k] > 0:
                        sy.wait_ge(SEM[k], CNT[k])
        p8.close()

    return nc


_CACHE = {}


def _prep_shared(inp):
    asc = np.ascontiguousarray
    f32 = lambda a: np.asarray(a, dtype=np.float32)
    Wq, Wk = f32(inp["L0_attn_Wq"]), f32(inp["L0_attn_Wk"])
    Wkg1, Wkg2 = f32(inp["L0_attn_Wkg1"]), f32(inp["L0_attn_Wkg2"])
    bkg2 = f32(inp["L0_attn_bkg2"])
    Wv, Wg, Wo = f32(inp["L0_attn_Wv"]), f32(inp["L0_attn_Wg"]), f32(inp["L0_attn_Wo"])
    bg = f32(inp["L0_attn_bg"])
    gate_w = f32(inp["L0_ffn_gate_w"])
    w1, b1 = f32(inp["L0_ffn_w1"]), f32(inp["L0_ffn_b1"])
    w2, b2 = f32(inp["L0_ffn_w2"]), f32(inp["L0_ffn_b2"])
    sw1, sb1 = f32(inp["L0_ffn_shared_w1"]), f32(inp["L0_ffn_shared_b1"])
    sw2, sb2 = f32(inp["L0_ffn_shared_w2"]), f32(inp["L0_ffn_shared_b2"])

    d = {}
    d["wqT"] = asc(Wq.T)
    d["wkT"] = asc(Wk.T)
    d["wkg1"] = asc(Wkg1.T.reshape(16, 128, 16).transpose(1, 0, 2).reshape(128, 256))
    d["wkg2"] = asc(Wkg2.T)
    d["bkg2_t"] = asc(bkg2.reshape(8, 128).T)
    d["wvT"] = asc(Wv.T)
    d["wgT"] = asc(Wg.T)
    d["woT"] = asc(Wo.T)
    d["bg_row"] = asc(bg.reshape(1, D))
    d["w1_n"] = asc(w1)
    d["w2r"] = asc(w2.reshape(8, 8, 128, 8, 2, 128).transpose(3, 4, 0, 2, 1, 5)
                 .reshape(8, 2048, 1024))
    d["sw1_n"] = asc(sw1)
    d["sw2r"] = asc(sw2.reshape(2, 8, 128, 8, 2, 128).transpose(3, 4, 0, 2, 1, 5)
                  .reshape(8, 512, 1024))
    d["gate_t"] = asc(gate_w.reshape(16, 128, 8).transpose(1, 0, 2).reshape(128, 128))
    d["b1_t"] = asc(b1.reshape(8, 8, 128).transpose(2, 0, 1).reshape(128, 64))
    d["b2_n"] = asc(b2)
    d["sb1_t"] = asc(sb1.reshape(16, 128).T)
    d["sb2_t"] = asc(sb2.reshape(16, 128).T)
    d["identb_d"] = np.eye(128, dtype=BFNP)
    d["identf_d"] = np.eye(128, dtype=np.float32)
    s = np.arange(128)[:, None]
    t = np.arange(128)[None, :]
    d["maskut_d"] = (s <= t).astype(np.float32)
    d["zeros_d"] = np.zeros((128, 128), np.float32)
    d["ones_r"] = np.ones((1, 128), np.float32)
    d["ones_c"] = np.ones((128, 1), np.float32)
    oh = np.zeros((8, 1024), np.float32)
    for e in range(8):
        oh[e, e * 128:(e + 1) * 128] = 1.0
    d["oh_d"] = oh
    em = np.zeros((128, 64), np.float32)
    for j in range(8):
        for e in range(8):
            if e > j:
                em[:, j * 8 + e] = 1.0
    d["emask_d"] = em
    fc = np.zeros((128, 4), np.float32)
    fc[:, 0] = LNB
    fc[:, 1] = 1.0
    fc[:, 2] = EPS
    d["fconst_d"] = fc
    return d


def kernel(**inputs):
    x = np.asarray(inputs["L0_attn_x"], dtype=np.float32)
    B, L, _ = x.shape
    shared = _prep_shared(inputs)

    if "nc" not in _CACHE:
        _CACHE["nc"] = build_program()
    nc = _CACHE["nc"]

    in_maps = []
    for c in range(8):
        b, hf = c // 2, c % 2
        m = dict(shared)
        m["x_fm"] = np.ascontiguousarray(x[b, hf * T:(hf + 1) * T, :].T)
        if c % 2 == 1:
            ix = (c - 1) * 128 + np.arange(128, dtype=np.int32)
            ms = np.ones((128, 1), np.float32)
        else:
            ix = c * 128 + np.arange(128, dtype=np.int32)
            ms = np.zeros((128, 1), np.float32)
        m["idx_d"] = ix[:, None]
        m["maskS_d"] = ms
        in_maps.append(m)

    res = run_bass_kernel_spmd(nc, in_maps, core_ids=list(range(8)))
    out = np.empty((B, L, D), np.float32)
    for c in range(8):
        b, hf = c // 2, c % 2
        out[b, hf * T:(hf + 1) * T, :] = res.results[c]["out_fm"].T
    return out


if __name__ == "__main__":
    nc = build_program()
    n = sum(1 for _ in nc.all_instructions())
    print("instructions:", n)
